# version 07
# baseline (speedup 1.0000x reference)
from contextlib import ExitStack

import numpy as np
import ml_dtypes
import concourse.bass as bass
import concourse.mybir as mybir
from concourse.bass_utils import run_bass_kernel_spmd

F32 = mybir.dt.float32
BF16 = mybir.dt.bfloat16
AF = mybir.ActivationFunctionType
ALU = mybir.AluOpType
BF = ml_dtypes.bfloat16

B, T, C, H, N = 2, 8192, 1024, 16, 64
TC = 2048
TQ = TC + 1
NCH = 8
QW = 128
NCHUNK = 16
NT = 4
EPS_LN = 1e-5
EPS_RMS = 1e-6
E_CLAMP = 1e30

STAT_TILES = [(0, 512), (512, 1024), (1024, 1536), (1536, 2048), (2048, 2049)]


def _consts():
    c = {}
    c["ones_col"] = np.ones((128, 1), np.float32)
    c["ones_row"] = np.ones((1, 128), np.float32)
    c["iota_q"] = np.broadcast_to(np.arange(128, dtype=np.float32)[None, :], (128, 128)).copy()
    c["drev_row"] = (np.arange(128, dtype=np.float32) - 127.0)[None, :].copy()
    dd = np.arange(128)[None, :] - np.arange(128)[:, None] - 1
    c["dpq_relu"] = np.maximum(dd, 0).astype(np.float32)
    c["lowmask"] = (dd >= 0).astype(np.float32)
    c["diagmask"] = np.eye(128, dtype=np.float32)
    c["eye16"] = np.eye(16, dtype=np.float32)
    c["sel_lo"] = np.array([[1.0 if hh == 2 * cr else 0.0 for cr in range(8)] for hh in range(16)], np.float32)
    c["sel_hi"] = np.array([[1.0 if hh == 2 * cr + 1 else 0.0 for cr in range(8)] for hh in range(16)], np.float32)
    c["half_lo"] = np.array([[1.0 if p < 64 else 0.0 for p in range(128)] for _ in range(16)], np.float32)
    c["half_hi"] = np.array([[1.0 if p >= 64 else 0.0 for p in range(128)] for _ in range(16)], np.float32)
    return c


CONSTS = _consts()


class Ctr:
    def __init__(self, h):
        self.h = h
        self.v = 0

    def inc(self, inst, n=1):
        inst.then_inc(self.h, n)
        self.v += n
        return self.v


class Rec:

    def __init__(self, eng):
        self._e = eng
        self.last = None

    def wait_ge(self, h, val):
        return self._e.wait_ge(h, val)

    def __getattr__(self, nm):
        fn = getattr(self._e, nm)

        def wrap(*a, **k):
            inst = fn(*a, **k)
            self.last = inst
            return inst

        return wrap


class Chain:

    def __init__(self, eng, ctr):
        self._e = eng
        self._c = ctr

    def wait_ge(self, h, val):
        return self._e.wait_ge(h, val)

    def __getattr__(self, nm):
        fn = getattr(self._e, nm)

        def wrap(*a, **k):
            if self._c.v > 0:
                self._e.wait_ge(self._c.h, self._c.v)
            inst = fn(*a, **k)
            inst.then_inc(self._c.h, 1)
            self._c.v += 1
            return inst

        return wrap


def ap(tensor, offset, dims):
    return bass.AP(tensor, offset, dims)


def build_launch1(upto='full'):
    nc = bass.Bass("TRN2", target_bir_lowering=False)
    es = ExitStack()

    def sb(name, shape, dt=F32, stack=None, side=None):
        return (stack or es).enter_context(nc.sbuf_tensor(name, shape, dt, side=side))

    xin = nc.dram_tensor("xT", [C, TQ], F32, kind="ExternalInput").ap()
    wnames = ["fWrT", "fWkT", "fWvT", "WrT", "WkT", "WvT"]
    wts = {nm: nc.dram_tensor(nm, [C, C], F32, kind="ExternalInput").ap() for nm in wnames}
    vnames = ["br", "bk", "tmk", "tmv", "tmr", "mur", "muk"]
    VID = {nm: i for i, nm in enumerate(vnames)}
    VID["omur"] = 7
    VID["omuk"] = 8
    vecs = {nm: nc.dram_tensor(nm, [128, 8], F32, kind="ExternalInput").ap() for nm in vnames}
    bv_row = nc.dram_tensor("bv_row", [1, C], F32, kind="ExternalInput").ap()
    td16 = nc.dram_tensor("td16", [16, 1], F32, kind="ExternalInput").ap()
    tf16 = nc.dram_tensor("tf16", [16, 1], F32, kind="ExternalInput").ap()
    cns = {nm: nc.dram_tensor("c_" + nm, list(a.shape), F32, kind="ExternalInput").ap()
           for nm, a in CONSTS.items()}

    xpf_o = nc.dram_tensor("xpf", [C, TC], F32, kind="ExternalOutput").ap()
    y1_o = nc.dram_tensor("y1T", [NCHUNK, 128, NCH, 128], BF16, kind="ExternalOutput").ap()
    r_o = nc.dram_tensor("rT", [C, TC], BF16, kind="ExternalOutput").ap()
    agg_o = nc.dram_tensor("agg", [128, 512], F32, kind="ExternalOutput").ap()
    kfd = nc.dram_tensor("kf_dram", [C, TC], BF16, kind="Internal").ap()

    onec = sb("onec", [128, 1], BF16)
    onerow = sb("onerow", [1, 128], BF16)
    onerow32 = sb("onerow32", [1, 128], F32)
    vbuf = sb("vbuf", [128, 9, 8])
    bvr = sb("bvr", [1, C], BF16)
    e16 = sb("e16", [16, 1])
    u16 = sb("u16", [16, 1])
    e16x8 = sb("e16x8", [16, 8])
    u16x8 = sb("u16x8", [16, 8])
    erow16 = sb("erow16", [1, 16])
    urow16 = sb("urow16", [1, 16])
    erow = sb("erow", [1, 1024])
    ecol = sb("ecol", [128, 8])
    necol = sb("necol", [128, 8])
    necol128 = sb("necol128", [128, 16])
    ucol128 = sb("ucol128", [128, 16])
    wsp = sb("wsp", [128, 8])
    kq_iota = sb("kq_iota", [128, 128])
    kdpq = sb("kdpq", [128, 128])
    klow = sb("klow", [128, 128], BF16)
    kdiag = sb("kdiag", [128, 128])
    drev = sb("drev", [1, 128])
    sel_lo = sb("sel_lo", [16, 8])
    sel_hi = sb("sel_hi", [16, 8])
    eye16 = sb("eye16", [16, 16])
    half_lo = sb("half_lo", [16, 128])
    half_hi = sb("half_hi", [16, 128])
    wtmask = sb("wtmask", [128, 16, 128], BF16)
    wtf = sb("wtf", [128, 128])
    wbt = sb("wbt", [128, 8, 128], BF16)
    kwk = sb("kwk", [128, 1024], BF16)
    srow = sb("srow", [1, 4, 512])

    psA = es.enter_context(nc.psum_tensor("psA", [128, 4, 512], F32))
    psY = es.enter_context(nc.psum_tensor("psY", [128, 2, 512], F32))
    psS = es.enter_context(nc.psum_tensor("psS", [128, 512], F32))
    psG = es.enter_context(nc.psum_tensor("psG", [128, 512], F32))

    s_in = Ctr(es.enter_context(nc.semaphore("s_in")))
    s_x = Ctr(es.enter_context(nc.semaphore("s_x")))
    s_w = [Ctr(es.enter_context(nc.semaphore("s_w0"))), Ctr(es.enter_context(nc.semaphore("s_w1")))]
    s_pe = Ctr(es.enter_context(nc.semaphore("s_pe")))
    s_ve = Ctr(es.enter_context(nc.semaphore("s_ve")))
    s_ac = Ctr(es.enter_context(nc.semaphore("s_ac")))
    s_st = Ctr(es.enter_context(nc.semaphore("s_st")))
    s_tr = Ctr(es.enter_context(nc.semaphore("s_tr")))
    s_kf = Ctr(es.enter_context(nc.semaphore("s_kf")))
    s_xr = [Ctr(es.enter_context(nc.semaphore("s_xr0"))), Ctr(es.enter_context(nc.semaphore("s_xr1")))]
    s_sp = [Ctr(es.enter_context(nc.semaphore("s_sp0"))), Ctr(es.enter_context(nc.semaphore("s_sp1")))]
    s_y = [Ctr(es.enter_context(nc.semaphore("s_y0"))), Ctr(es.enter_context(nc.semaphore("s_y1")))]

    g = nc.gpsimd
    t = Rec(nc.tensor)
    v = Chain(nc.vector, s_ve)
    sc = Chain(nc.scalar, s_ac)
    sy = nc.sync

    for dst, src in [(onec, cns["ones_col"]), (onerow, cns["ones_row"]), (onerow32, cns["ones_row"]),
                     (bvr, bv_row), (e16, td16), (u16, tf16),
                     (kq_iota, cns["iota_q"]), (kdpq, cns["dpq_relu"]), (klow, cns["lowmask"]),
                     (kdiag, cns["diagmask"]), (drev, cns["drev_row"]), (sel_lo, cns["sel_lo"]),
                     (sel_hi, cns["sel_hi"]), (eye16, cns["eye16"]),
                     (half_lo, cns["half_lo"]), (half_hi, cns["half_hi"])]:
        s_in.inc(g.dma_start(out=dst[:, :] if len(dst.shape) == 2 else dst, in_=src), 16)
    for nm in vnames:
        s_in.inc(g.dma_start(out=vbuf[:, VID[nm], :], in_=vecs[nm]), 16)
    consts_loaded = s_in.v

    esXN = ExitStack()
    xnb = sb("xnb", [128, NCH, TQ], BF16, esXN, side="right")
    esA = ExitStack()
    xf32 = sb("xf32", [128, NCH, TQ], F32, esA, side="right")
    xsq = sb("xsq", [128, NCH, TQ], BF16, esA, side="right")

    s_x.inc(sy.dma_start(out=xf32[:, :, :], in_=xin.rearrange("(a p) t -> p a t", p=128)), 16)

    sc.wait_ge(s_in.h, consts_loaded)
    sc.activation(out=e16[:, :], in_=e16[:, :], func=AF.Exp)
    ac_e = s_ac.v
    v.wait_ge(s_ac.h, ac_e)
    v.tensor_scalar_min(e16[:, :], e16[:, :], E_CLAMP)
    ve_e = s_ve.v
    v.wait_ge(s_in.h, consts_loaded)
    v.tensor_scalar(vbuf[:, VID["omur"], :], vbuf[:, VID["mur"], :], -1.0, 1.0, ALU.mult, ALU.add)
    v.tensor_scalar(vbuf[:, VID["omuk"], :], vbuf[:, VID["muk"], :], -1.0, 1.0, ALU.mult, ALU.add)
    ve_omu = s_ve.v
    t.wait_ge(s_ve.h, ve_e)
    t.matmul(psG[0:1, 0:16], e16[:, :], eye16[:, :], start=True, stop=True)
    s_pe.inc(t.matmul(psG[0:1, 16:32], u16[:, :], eye16[:, :], start=True, stop=True))
    pe_rows = s_pe.v
    v.wait_ge(s_pe.h, pe_rows)
    v.tensor_copy(erow16[:, :], psG[0:1, 0:16])
    v.tensor_copy(urow16[:, :], psG[0:1, 16:32])
    v.tensor_copy(ap(erow, 0, [[1024, 1], [64, 16], [1, 64]]),
                           ap(erow16, 0, [[16, 1], [1, 16], [0, 64]]))
    ve_rows = s_ve.v
    v.tensor_copy(ap(e16x8, 0, [[8, 16], [1, 8]]), ap(e16, 0, [[1, 16], [0, 8]]))
    v.tensor_mul(e16x8[:, :], e16x8[:, :], sel_lo[:, :])
    v.tensor_copy(ap(u16x8, 0, [[8, 16], [1, 8]]), ap(e16, 0, [[1, 16], [0, 8]]))
    v.tensor_mul(u16x8[:, :], u16x8[:, :], sel_hi[:, :])
    ve_sel = s_ve.v
    t.wait_ge(s_ve.h, ve_sel)
    t.matmul(psG[:, 32:40], half_lo[:, :], e16x8[:, :], start=True, stop=False)
    s_pe.inc(t.matmul(psG[:, 32:40], half_hi[:, :], u16x8[:, :], start=False, stop=True))
    pe_ecol = s_pe.v
    v.wait_ge(s_pe.h, pe_ecol)
    v.tensor_copy(ecol[:, :], psG[:, 32:40])
    v.tensor_scalar_mul(necol[:, :], ecol[:, :], -1.0)
    ve_ecol = s_ve.v
    t.wait_ge(s_ve.h, ve_rows)
    t.matmul(psG[:, 48:64], onerow32[:, :], erow16[:, :], start=True, stop=True)
    s_pe.inc(t.matmul(psG[:, 64:80], onerow32[:, :], urow16[:, :], start=True, stop=True))
    pe_c128 = s_pe.v
    v.wait_ge(s_pe.h, pe_c128)
    v.tensor_scalar_mul(necol128[:, :], psG[:, 48:64], -1.0)
    v.tensor_copy(ucol128[:, :], psG[:, 64:80])
    ve_c128 = s_ve.v
    v.wait_ge(s_ve.h, ve_ecol)
    v.tensor_scalar_mul(wsp[:, :], necol[:, :], 128.0)
    sc.wait_ge(s_ve.h, s_ve.v)
    sc.activation(out=wsp[:, :], in_=wsp[:, :], func=AF.Exp)
    for cr in range(8):
        sc.activation(out=wbt[:, cr, :], in_=kq_iota[:, :], func=AF.Exp,
                               scale=necol[:, cr:cr + 1])
    sc.wait_ge(s_ve.h, ve_c128)
    def wslot(hh):
        return (hh % 2) * 8 + hh // 2

    for hh in range(16):
        sc.activation(out=wtf[:, :], in_=kdpq[:, :], func=AF.Exp,
                               scale=necol128[:, hh:hh + 1])
        v.wait_ge(s_ac.h, s_ac.v)
        v.tensor_mul(wtmask[:, wslot(hh), :], wtf[:, :], klow[:, :])
        v.scalar_tensor_tensor(wtmask[:, wslot(hh), :], kdiag[:, :], ucol128[:, hh:hh + 1],
                                        wtmask[:, wslot(hh), :], ALU.mult, ALU.add)
        sc.wait_ge(s_ve.h, s_ve.v - 1)
    t.wait_ge(s_ve.h, ve_rows)
    for i in range(2):
        s_pe.inc(t.matmul(psS[:, 0:512], drev[:, :], erow[0:1, i * 512:(i + 1) * 512],
                          start=True, stop=True))
        sc.wait_ge(s_pe.h, s_pe.v)
        sc.activation(out=kwk[:, i * 512:(i + 1) * 512], in_=psS[:, 0:512],
                               func=AF.Exp)
        t.wait_ge(s_ac.h, s_ac.v)
    ac_tables = s_ac.v
    ve_tables = s_ve.v

    v.wait_ge(s_x.h, 16)
    sc.wait_ge(s_x.h, 16)
    for ch in range(NCH):
        v.tensor_copy(xnb[:, ch, :], xf32[:, ch, :])
        sc.activation(out=xsq[:, ch, :], in_=xf32[:, ch, :], func=AF.Square)
    ve_xbf = s_ve.v
    ac_xsq = s_ac.v
    t.wait_ge(s_ve.h, ve_xbf)
    t.wait_ge(s_ac.h, ac_xsq)
    t.wait_ge(s_ac.h, ac_tables)
    t.wait_ge(s_ve.h, ve_tables)
    for ti, (a, b2) in enumerate(STAT_TILES):
        w = b2 - a
        for ch in range(NCH):
            t.matmul(psA[0:1, 0, 0:w], onec[:, :], xnb[:, ch, a:b2], start=(ch == 0), stop=(ch == NCH - 1))
        for ch in range(NCH):
            t.matmul(psA[0:1, 1, 0:w], onec[:, :], xsq[:, ch, a:b2], start=(ch == 0), stop=(ch == NCH - 1))
        s_pe.inc(t.last)
        pe_stat = s_pe.v
        v.wait_ge(s_pe.h, pe_stat)
        v.tensor_scalar_mul(srow[0:1, 0, 0:w], psA[0:1, 0, 0:w], 1.0 / C)
        v.tensor_scalar_mul(srow[0:1, 1, 0:w], psA[0:1, 1, 0:w], 1.0 / C)
        sc.wait_ge(s_ve.h, s_ve.v)
        sc.activation(out=srow[0:1, 3, 0:w], in_=srow[0:1, 0, 0:w], func=AF.Square)
        v.wait_ge(s_ac.h, s_ac.v)
        v.tensor_sub(srow[0:1, 1, 0:w], srow[0:1, 1, 0:w], srow[0:1, 3, 0:w])
        v.tensor_scalar_add(srow[0:1, 1, 0:w], srow[0:1, 1, 0:w], EPS_LN)
        sc.wait_ge(s_ve.h, s_ve.v)
        sc.activation(out=srow[0:1, 1, 0:w], in_=srow[0:1, 1, 0:w], func=AF.Ln)
        v.wait_ge(s_ac.h, s_ac.v)
        v.tensor_scalar_mul(srow[0:1, 1, 0:w], srow[0:1, 1, 0:w], -0.5)
        sc.wait_ge(s_ve.h, s_ve.v)
        sc.activation(out=srow[0:1, 1, 0:w], in_=srow[0:1, 1, 0:w], func=AF.Exp)
        v.wait_ge(s_ac.h, s_ac.v)
        v.tensor_mul(srow[0:1, 2, 0:w], srow[0:1, 0, 0:w], srow[0:1, 1, 0:w])
        ve_row = s_ve.v
        t.wait_ge(s_ve.h, ve_row)
        t.matmul(psA[:, 2, 0:w], onerow32[:, :], srow[0:1, 1, 0:w], start=True, stop=True)
        s_pe.inc(t.matmul(psA[:, 3, 0:w], onerow32[:, :], srow[0:1, 2, 0:w], start=True, stop=True))
        pe_bc = s_pe.v
        v.wait_ge(s_pe.h, pe_bc)
        for ch in range(NCH):
            v.tensor_mul(xnb[:, ch, a:b2], xnb[:, ch, a:b2], psA[:, 2, 0:w])
            v.tensor_sub(xnb[:, ch, a:b2], xnb[:, ch, a:b2], psA[:, 3, 0:w])
        v.nop()
        t.wait_ge(s_ve.h, s_ve.v)
    ve_xn = s_ve.v
    esA.close()
    if upto == 'stageA':
        sy.wait_ge(s_ve.h, s_ve.v)
        es.close()
        return nc

    wbuf = sb("wbuf", [128, 2, NCH, C], BF16, esXN, side="right")
    mixt = sb("mixt", [128, 2, NCH, 512], BF16, esXN, side="right")
    esFFN = ExitStack()
    ffs = sb("ffs", [128, NCH, TC], BF16, esFFN, side="right")
    ffk = sb("ffk", [128, NCH, TC], BF16, esFFN, side="right")
    xrel = sb("xrel", [128, 2, 512], F32, esFFN, side="right")
    xpfs = sb("xpfs", [128, 2, 512], F32, esFFN, side="right")

    proj_done_pe = [None] * 6
    w_loaded = []

    def load_weight(i):
        sl = i % 2
        if i < 2:
            g.wait_ge(s_ve.h, ve_xn)
        else:
            g.wait_ge(s_pe.h, proj_done_pe[i - 2])
            g.wait_ge(s_w[sl].h, s_w[sl].v)
        s_w[sl].inc(g.dma_start(out=wbuf[:, sl, :, :],
                                in_=wts[wnames[i]].rearrange("(a p) o -> p a o", p=128)), 16)
        w_loaded.append((s_w[sl].h, s_w[sl].v))

    load_weight(0)
    load_weight(1)

    def emit_mix_tile2(slot, ti, vec_id):
        a = ti * 512
        for ch in range(NCH):
            sh = xnb[:, ch, a:a + 512]
            cu = xnb[:, ch, a + 1:a + 513]
            d = mixt[:, slot, ch, :]
            v.tensor_sub(d, sh, cu)
            last = v.scalar_tensor_tensor(d, d, vbuf[:, vec_id, ch:ch + 1], cu, ALU.mult, ALU.add)
        return last

    def emit_projA(wslot, vec_id, evac, marks, prev_war, mix_war_pe=0):
        t.wait_ge(*w_loaded[wslot])
        tile_pe_marks = []
        for ti in range(NT):
            v.wait_ge(s_pe.h, tile_pe_marks[ti - 2] if ti >= 2 else mix_war_pe)
            v.wait_ge(s_ve.h, ve_xn)
            emit_mix_tile2(ti % 2, ti, vec_id)
            t.wait_ge(s_ve.h, s_ve.v)
            for mi in range(8):
                idx = ti * 8 + mi
                bank = idx % 2
                pa = psA[:, bank, :]
                wv = marks[idx - 2] if idx >= 2 else prev_war[idx]
                t.wait_ge(wv[0], wv[1])
                for k in range(NCH):
                    t.matmul(pa, wbuf[:, wslot % 2, k, mi * 128:(mi + 1) * 128],
                             mixt[:, ti % 2, k, :], start=(k == 0), stop=(k == NCH - 1))
                s_pe.inc(t.last)
                evac(mi, ti, pa, s_pe.v)
            tile_pe_marks.append(s_pe.v)
        proj_done_pe[wslot] = s_pe.v
        return marks

    evac_marks = []

    def evac_sig(mi, ti, pa, pe_val):
        sc.wait_ge(s_pe.h, pe_val)
        sc.activation(out=ffs[:, mi, ti * 512:(ti + 1) * 512], in_=pa, func=AF.Sigmoid)
        evac_marks.append((s_ac.h, s_ac.v))

    v.wait_ge(s_ve.h, ve_omu)
    marks0 = emit_projA(0, VID["omur"], evac_sig, evac_marks, [(s_ve.h, ve_xn)] * 2)
    load_weight(2)

    p0_done = proj_done_pe[0]
    evac_marks = []

    def evac_relu(mi, ti, pa, pe_val):
        sc.wait_ge(s_pe.h, pe_val)
        sc.activation(out=ffk[:, mi, ti * 512:(ti + 1) * 512], in_=pa, func=AF.Relu)
        evac_marks.append((s_ac.h, s_ac.v))

    marks1 = emit_projA(1, VID["omuk"], evac_relu, evac_marks, marks0[-2:],
                        mix_war_pe=p0_done)
    load_weight(3)
    sc.wait_ge(s_ac.h, evac_marks[-1][1])
    for ch in range(NCH):
        sc.activation(out=ffk[:, ch, :], in_=ffk[:, ch, :], func=AF.Square)
    ac_ffk = s_ac.v

    evac_marks = []

    t.wait_ge(*w_loaded[2])
    t.wait_ge(s_ac.h, ac_ffk)
    idx = 0
    for ti in range(NT):
        for mi in range(8):
            bank = idx % 2
            pa = psA[:, bank, :]
            wv = evac_marks[idx - 2] if idx >= 2 else marks1[-2:][idx]
            t.wait_ge(wv[0], wv[1])
            for k in range(NCH):
                t.matmul(pa, wbuf[:, 2 % 2, k, mi * 128:(mi + 1) * 128],
                         ffk[:, k, ti * 512:(ti + 1) * 512], start=(k == 0), stop=(k == NCH - 1))
            s_pe.inc(t.last)
            pe_val = s_pe.v
            par = idx % 2
            if idx >= 2:
                sy.wait_ge(evac_marks[idx - 2][0], evac_marks[idx - 2][1] + 1)
            s_xr[par].inc(sy.dma_start(out=xrel[:, par, :],
                                       in_=ap(xin.tensor, mi * 128 * TQ + ti * 512 + 1, [[TQ, 128], [1, 512]])), 16)
            v.wait_ge(s_pe.h, pe_val)
            v.wait_ge(s_xr[par].h, s_xr[par].v)
            if idx >= 2:
                v.wait_ge(s_sp[par].h, 16 * (idx // 2))
            v.tensor_mul(xpfs[:, par, :], pa, ffs[:, mi, ti * 512:(ti + 1) * 512])
            v.tensor_add(xpfs[:, par, :], xpfs[:, par, :], xrel[:, par, :])
            evac_marks.append((s_ve.h, s_ve.v - 1))
            sy.wait_ge(s_ve.h, s_ve.v)
            s_sp[par].inc(sy.dma_start(out=ap(xpf_o.tensor, mi * 128 * TC + ti * 512, [[TC, 128], [1, 512]]),
                                       in_=xpfs[:, par, :]), 16)
            idx += 1
    proj_done_pe[2] = s_pe.v
    marks2 = evac_marks
    if upto == 'ffn':
        sy.wait_ge(s_sp[0].h, s_sp[0].v)
        sy.wait_ge(s_sp[1].h, s_sp[1].v)
        es.close()
        return nc
    load_weight(4)
    esFFN.close()

    rfe = sb("rfe", [128, NCH, TC], BF16)
    kfe = sb("kfe", [128, NCH, TC], BF16)
    vtk = sb("vtk", [128, NCHUNK, C], BF16)
    sc.wait_ge(s_sp[0].h, s_sp[0].v)
    sc.wait_ge(s_sp[1].h, s_sp[1].v)
    v.wait_ge(s_sp[0].h, s_sp[0].v)
    v.wait_ge(s_sp[1].h, s_sp[1].v)

    evac_marks = []

    def evac_r(mi, ti, pa, pe_val):
        sc.wait_ge(s_pe.h, pe_val)
        sc.activation(out=rfe[:, mi, ti * 512:(ti + 1) * 512], in_=pa, func=AF.Identity,
                               bias=vbuf[:, VID["br"], mi:mi + 1])
        evac_marks.append((s_ac.h, s_ac.v))
        sy.wait_ge(s_ac.h, s_ac.v)
        s_st.inc(sy.dma_start(out=ap(r_o.tensor, mi * 128 * TC + ti * 512, [[TC, 128], [1, 512]]),
                              in_=rfe[:, mi, ti * 512:(ti + 1) * 512]), 16)

    marks3 = emit_projA(3, VID["tmr"], evac_r, evac_marks, marks2[-2:],
                        mix_war_pe=proj_done_pe[1])
    load_weight(5)

    evac_marks = []

    def evac_k(mi, ti, pa, pe_val):
        sc.wait_ge(s_pe.h, pe_val)
        sc.activation(out=kfe[:, mi, ti * 512:(ti + 1) * 512], in_=pa, func=AF.Identity,
                               bias=vbuf[:, VID["bk"], mi:mi + 1])
        evac_marks.append((s_ac.h, s_ac.v))
        sy.wait_ge(s_ac.h, s_ac.v)
        s_kf.inc(sy.dma_start(out=ap(kfd.tensor, mi * 128 * TC + ti * 512, [[TC, 128], [1, 512]]),
                              in_=kfe[:, mi, ti * 512:(ti + 1) * 512]), 16)

    marks4 = emit_projA(4, VID["tmk"], evac_k, evac_marks, marks3[-2:],
                        mix_war_pe=proj_done_pe[3])
    kf_dram_done = s_kf.v

    vb_marks = []
    p5_tile_pe = []
    t.wait_ge(*w_loaded[5])
    for ti in range(NT):
        v.wait_ge(s_pe.h, p5_tile_pe[ti - 2] if ti >= 2 else proj_done_pe[4])
        emit_mix_tile2(ti % 2, ti, VID["tmv"])
        t.wait_ge(s_ve.h, s_ve.v)
        for sub in range(4):
            tj = ti * 4 + sub
            for ni in range(2):
                j2 = tj * 2 + ni
                pa = psA[:, j2 % 2, :]
                wv = vb_marks[j2 - 2] if j2 >= 2 else marks4[-2:][j2]
                t.wait_ge(wv[0], wv[1])
                for k in range(NCH):
                    t.matmul(pa, mixt[:, ti % 2, k, sub * 128:(sub + 1) * 128],
                             wbuf[:, 5 % 2, k, ni * 512:(ni + 1) * 512],
                             start=(k == 0), stop=False)
                t.matmul(pa, onerow[:, :], bvr[0:1, ni * 512:(ni + 1) * 512], start=False, stop=True)
                s_pe.inc(t.last)
                v.wait_ge(s_pe.h, s_pe.v)
                v.tensor_copy(vtk[:, tj, ni * 512:(ni + 1) * 512], pa)
                vb_marks.append((s_ve.h, s_ve.v))
        p5_tile_pe.append(s_pe.v)
    proj_done_pe[5] = s_pe.v
    vtk_done = s_ve.v

    esXN.close()

    ktk = sb("ktk", [128, NCHUNK, C], BF16)
    rwb = sb("rwb", [128, NCH, TC], BF16)
    asb = sb("asb", [128, 16, 128], BF16)
    st2 = sb("st2", [128, 2, 512], F32)
    st2b = sb("st2b", [128, 2, 512], BF16)
    y1s = sb("y1s", [128, 2, 8, 128], BF16)

    sy.wait_ge(s_kf.h, kf_dram_done)
    sy.wait_ge(s_pe.h, proj_done_pe[5])
    sy.wait_ge(s_ve.h, vtk_done)
    for tj in range(NCHUNK):
        s_tr.inc(sy.dma_start_transpose(ktk[:, tj, :],
                                        ap(kfd.tensor, tj * 128, [[TC, 1024], [1, 128]])), 16)
    ktk_done = s_tr.v
    v.wait_ge(s_tr.h, ktk_done)
    v.wait_ge(s_ac.h, ac_tables)
    for tj in range(NCHUNK):
        v.tensor_mul(ktk[:, tj, :], ktk[:, tj, :], kwk[:, :])
    kscaled_done = s_ve.v

    v.wait_ge(s_ac.h, s_ac.v)
    for cr in range(8):
        v.tensor_mul(ap(rwb, cr * TC, [[NCH * TC, 128], [128, 16], [1, 128]]),
                     ap(rfe, cr * TC, [[NCH * TC, 128], [128, 16], [1, 128]]),
                     ap(wbt, cr * 128, [[8 * 128, 128], [0, 16], [1, 128]]))
    v.memset(st2[:, 0, :], 0.0)
    v.memset(st2b[:, 0, :], 0.0)
    wkv_ready_ve = s_ve.v

    if upto == 'attnproj':
        sy.wait_ge(s_ve.h, s_ve.v)
        sy.wait_ge(s_sp[0].h, s_sp[0].v)
        sy.wait_ge(s_sp[1].h, s_sp[1].v)
        sy.wait_ge(s_st.h, s_st.v)
        es.close()
        return nc

    peb = s_pe.v
    veb = s_ve.v
    acb = s_ac.v

    def peA(j):
        return peb + 4 * j + 1

    def peS(j):
        return peb + 4 * j + 2

    def peY(j):
        return peb + 4 * j + 3

    def peYo(j):
        return peb + 4 * j + 4

    def veM(j):
        return veb + 15 * j + 4

    def veS(j):
        return veb + 15 * j + 13

    def veYo(j):
        return veb + 15 * j + 15

    def acY(j):
        return acb + 8 * (j + 1)

    def slot_of(hh):
        return (hh % 2) * 8 + hh // 2

    t.wait_ge(s_ve.h, wkv_ready_ve)
    t.wait_ge(s_ac.h, ac_tables)
    for j in range(NCHUNK):
        cur = j % 2
        if j >= 1:
            t.wait_ge(s_ve.h, veYo(j - 1))
        for s_ in range(16):
            hh = (s_ // 8) + 2 * (s_ % 8)
            cr, half = hh // 2, (hh % 2) * 64
            t.matmul(psA[:, s_ // 4, (s_ % 4) * 128:(s_ % 4) * 128 + 128],
                     kfe[half:half + 64, cr, j * 128:(j + 1) * 128],
                     rfe[half:half + 64, cr, j * 128:(j + 1) * 128],
                     start=True, stop=True, skip_group_check=True)
        s_pe.inc(t.last)
        if j >= 1:
            t.wait_ge(s_ve.h, veS(j - 1))
        for hh in range(16):
            pr, half = hh // 2, (hh % 2) * 64
            t.matmul(psS[half:half + 64, pr * 64:pr * 64 + 64],
                     ktk[:, j, hh * 64:hh * 64 + 64],
                     vtk[:, j, hh * 64:hh * 64 + 64],
                     start=True, stop=True, skip_group_check=True)
        s_pe.inc(t.last)
        t.wait_ge(s_ve.h, veM(j))
        if j >= 1:
            t.wait_ge(s_ac.h, acY(j - 1))
        for pr in range(8):
            bank, off = pr // 4, (pr % 4) * 128
            t.matmul(psY[0:64, bank, off:off + 128],
                     vtk[:, j, (2 * pr) * 64:(2 * pr) * 64 + 64],
                     asb[:, slot_of(2 * pr), :],
                     start=True, stop=False, skip_group_check=True)
            t.matmul(psY[0:64, bank, off:off + 128],
                     st2b[0:64, cur, pr * 64:pr * 64 + 64],
                     rwb[0:64, pr, j * 128:(j + 1) * 128],
                     start=False, stop=True, skip_group_check=True)
            t.matmul(psY[64:128, bank, off:off + 128],
                     vtk[:, j, (2 * pr + 1) * 64:(2 * pr + 1) * 64 + 64],
                     asb[:, slot_of(2 * pr + 1), :],
                     start=True, stop=True, skip_group_check=True)
        s_pe.inc(t.last)
        for pr in range(8):
            bank, off = 2 + pr // 4, (pr % 4) * 128
            t.matmul(psA[64:128, bank, off:off + 128],
                     st2b[64:128, cur, pr * 64:pr * 64 + 64],
                     rwb[64:128, pr, j * 128:(j + 1) * 128],
                     start=True, stop=True, skip_group_check=True)
        s_pe.inc(t.last)

    for j in range(NCHUNK):
        cur = j % 2
        nxt = 1 - cur
        v.wait_ge(s_pe.h, peA(j))
        for bank in range(4):
            v.tensor_mul(ap(asb, bank * 4 * 128, [[16 * 128, 128], [1, 512]]),
                         psA[:, bank, :],
                         ap(wtmask, bank * 4 * 128, [[16 * 128, 128], [1, 512]]))
        assert s_ve.v == veM(j)
        v.wait_ge(s_pe.h, peS(j))
        if j >= 1:
            v.wait_ge(s_pe.h, peY(j - 1))
        for pr in range(8):
            v.scalar_tensor_tensor(st2[:, nxt, pr * 64:pr * 64 + 64],
                                   st2[:, cur, pr * 64:pr * 64 + 64],
                                   wsp[:, pr:pr + 1],
                                   psS[:, pr * 64:pr * 64 + 64],
                                   ALU.mult, ALU.add)
        if j >= 1:
            v.wait_ge(s_pe.h, peYo(j - 1))
        v.tensor_copy(st2b[:, nxt, :], st2[:, nxt, :])
        assert s_ve.v == veS(j)
        v.wait_ge(s_ac.h, acY(j))
        v.wait_ge(s_pe.h, peYo(j))
        for bank in range(2):
            v.tensor_add(ap(y1s, (j % 2) * 8 * 128 + bank * 4 * 128 + 64 * (2 * 8 * 128),
                            [[2 * 8 * 128, 64], [1, 512]]),
                         ap(y1s, (j % 2) * 8 * 128 + bank * 4 * 128 + 64 * (2 * 8 * 128),
                            [[2 * 8 * 128, 64], [1, 512]]),
                         ap(psA, 64 * (4 * 512) + (2 + bank) * 512, [[4 * 512, 64], [1, 512]]))
        assert s_ve.v == veYo(j)

    for j in range(NCHUNK):
        sc.wait_ge(s_pe.h, peY(j))
        if j >= 2:
            sc.wait_ge(s_y[j % 2].h, 16 * (j // 2))
        for pr in range(8):
            bank, off = pr // 4, (pr % 4) * 128
            sc.activation(out=y1s[:, j % 2, pr, :], in_=psY[:, bank, off:off + 128], func=AF.Copy)
        assert s_ac.v == acY(j)
        sy.wait_ge(s_ve.h, veYo(j))
        s_y[j % 2].inc(sy.dma_start(out=y1_o[j], in_=y1s[:, j % 2, :, :]), 16)

    sy.wait_ge(s_ve.h, veS(NCHUNK - 1))
    s_st.inc(sy.dma_start(out=agg_o, in_=st2[:, NCHUNK % 2, :]), 16)
    sy.wait_ge(s_st.h, s_st.v)
    sy.wait_ge(s_sp[0].h, s_sp[0].v)
    sy.wait_ge(s_sp[1].h, s_sp[1].v)
    sy.wait_ge(s_y[0].h, s_y[0].v)
    sy.wait_ge(s_y[1].h, s_y[1].v)

    es.close()
    return nc




def build_launch2():
    nc = bass.Bass("TRN2", target_bir_lowering=False)
    es = ExitStack()

    def sb(name, shape, dt=F32, stack=None, side=None):
        return (stack or es).enter_context(nc.sbuf_tensor(name, shape, dt, side=side))

    xpf_i = nc.dram_tensor("xpfT", [C, TC], F32, kind="ExternalInput").ap()
    y1_i = nc.dram_tensor("y1d", [NCHUNK, 128, NCH, 128], BF16, kind="ExternalInput").ap()
    r_i = nc.dram_tensor("rT", [C, TC], BF16, kind="ExternalInput").ap()
    agg_i = nc.dram_tensor("aggp", [4, 128, 512], F32, kind="ExternalInput").ap()
    selm_i = nc.dram_tensor("selmat", [1, 16], F32, kind="ExternalInput").ap()
    wo_i = nc.dram_tensor("WoT", [C, C], F32, kind="ExternalInput").ap()
    wq_i = nc.dram_tensor("Wq", [C, C], F32, kind="ExternalInput").ap()
    wk1_i = nc.dram_tensor("Wk1", [C, C], F32, kind="ExternalInput").ap()
    wv1_i = nc.dram_tensor("Wv1", [C, C], F32, kind="ExternalInput").ap()
    gw_i = nc.dram_tensor("gwT", [C, 64], F32, kind="ExternalInput").ap()
    bo_i = nc.dram_tensor("bo", [128, 8], F32, kind="ExternalInput").ap()
    rmsw_i = nc.dram_tensor("rmsw", [128, 8], F32, kind="ExternalInput").ap()
    td16 = nc.dram_tensor("td16", [16, 1], F32, kind="ExternalInput").ap()
    cns = {}
    for nm in ["ones_col", "ones_row", "iota_q", "eye16", "sel_lo", "sel_hi",
               "half_lo", "half_hi"]:
        cns[nm] = nc.dram_tensor("c_" + nm, list(CONSTS[nm].shape), F32, kind="ExternalInput").ap()
    j128_i = nc.dram_tensor("c_j128", [128, 16], F32, kind="ExternalInput").ap()
    k2048_i = nc.dram_tensor("c_k2048", [128, 4], F32, kind="ExternalInput").ap()
    tri_i = nc.dram_tensor("c_tri", [128, 128], F32, kind="ExternalInput").ap()
    iota64_i = nc.dram_tensor("c_iota64", [128, 64], F32, kind="ExternalInput").ap()
    hsel_i = nc.dram_tensor("c_hsel", [128, 8, 16], F32, kind="ExternalInput").ap()
    hselT_i = nc.dram_tensor("c_hselT", [16, 8, 128], F32, kind="ExternalInput").ap()

    x1_o = nc.dram_tensor("x1T", [C, TC], F32, kind="ExternalOutput").ap()
    qn_o = nc.dram_tensor("qnT", [C, TC], BF16, kind="ExternalOutput").ap()
    xn2_o = nc.dram_tensor("xn2f", [C, TC], BF16, kind="ExternalOutput").ap()
    xr2f_o = nc.dram_tensor("xr2f", [C, TC], BF16, kind="ExternalOutput").ap()
    xr2t_o = nc.dram_tensor("xr2t", [TC, C], BF16, kind="ExternalOutput").ap()
    s_o = nc.dram_tensor("sloc", [1, C], F32, kind="ExternalOutput").ap()
    c_o = nc.dram_tensor("cloc", [64, C], F32, kind="ExternalOutput").ap()
    meta_o = nc.dram_tensor("meta", [128, 16, 8], F32, kind="ExternalOutput").ap()
    cnt_o = nc.dram_tensor("cnt", [1, 64], F32, kind="ExternalOutput").ap()
    xrfd = nc.dram_tensor("xrf_dram", [C, TC], BF16, kind="Internal").ap()

    onec = sb("onec", [128, 1], BF16)
    onerow = sb("onerow", [1, 128], BF16)
    onerow32 = sb("onerow32", [1, 128], F32)
    e16 = sb("e16", [16, 1])
    eye16 = sb("eye16", [16, 16])
    erow16 = sb("erow16", [1, 16])
    e16x8 = sb("e16x8", [16, 8])
    u16x8 = sb("u16x8", [16, 8])
    sel_lo = sb("sel_lo", [16, 8])
    sel_hi = sb("sel_hi", [16, 8])
    half_lo = sb("half_lo", [16, 128])
    half_hi = sb("half_hi", [16, 128])
    ecol = sb("ecol", [128, 8])
    necol = sb("necol", [128, 8])
    kq_iota = sb("kq_iota", [128, 128])
    j128m = sb("j128m", [128, 16])
    k2048m = sb("k2048m", [128, 4])
    wbt = sb("wbt", [128, 8, 128], BF16)
    wsjt = sb("wsjt", [128, 8, 16])
    ws2k = sb("ws2k", [128, 8, 4])
    selb = sb("selb", [128, 16])
    coefp = sb("coefp", [128, 4, 8])
    ist = sb("ist", [128, 512])
    istb = sb("istb", [128, 512], BF16)
    isttmp = sb("isttmp", [128, 512])
    bo_v = sb("bo_v", [128, 8])
    rmsw_v = sb("rmsw_v", [128, 8])
    tri = sb("tri", [128, 128], BF16)
    iota64 = sb("iota64", [128, 64])
    hsel = sb("hsel", [128, 8, 16], BF16)
    hselT = sb("hselT", [16, 8, 128])
    srow = sb("srow", [1, 4, 512])
    gwb = sb("gwb", [128, 8, 64], BF16)
    csb = sb("csb", [64, 16, 64])
    qrow = sb("qrow", [16, 512])
    xrt = sb("xrt", [128, 2, C], BF16)
    lgt = sb("lgt", [128, NCHUNK, 64])
    m8 = sb("m8", [128, 8])
    i8 = sb("i8", [128, 8], mybir.dt.uint32)
    i8f = sb("i8f", [128, 2])
    wrow = sb("wrow", [128, 4])
    m1 = sb("m1", [128, 64], BF16)
    m2 = sb("m2", [128, 64], BF16)
    mb = sb("mb", [128, 64], BF16)
    pfx = sb("pfx", [128, 64])
    carry = sb("carry", [1, 64])
    carryb = sb("carryb", [1, 64], BF16)
    meta = sb("meta_sb", [128, NCHUNK, 8])
    rk = sb("rk", [128, 2])
    ostage = sb("ostage", [128, 2, 512], BF16)
    sbrow = sb("sbrow", [1, 16])

    psA = es.enter_context(nc.psum_tensor("psA", [128, 4, 512], F32))
    psY = es.enter_context(nc.psum_tensor("psY", [128, 2, 512], F32))
    psS = es.enter_context(nc.psum_tensor("psS", [128, 512], F32))
    psG = es.enter_context(nc.psum_tensor("psG", [128, 512], F32))

    s_in = Ctr(es.enter_context(nc.semaphore("s_in")))
    s_x = Ctr(es.enter_context(nc.semaphore("s_x")))
    s_y1 = Ctr(es.enter_context(nc.semaphore("s_y1")))
    s_r = Ctr(es.enter_context(nc.semaphore("s_r")))
    s_w = [Ctr(es.enter_context(nc.semaphore("s_w0"))), Ctr(es.enter_context(nc.semaphore("s_w1")))]
    s_pe = Ctr(es.enter_context(nc.semaphore("s_pe")))
    s_ve = Ctr(es.enter_context(nc.semaphore("s_ve")))
    s_ac = Ctr(es.enter_context(nc.semaphore("s_ac")))
    s_st = Ctr(es.enter_context(nc.semaphore("s_st")))
    s_tr = Ctr(es.enter_context(nc.semaphore("s_tr")))
    s_xf = Ctr(es.enter_context(nc.semaphore("s_xf")))
    s_sc2 = Ctr(es.enter_context(nc.semaphore("s_sc2")))

    g = nc.gpsimd
    t = Rec(nc.tensor)
    v = Chain(nc.vector, s_ve)
    sc = Chain(nc.scalar, s_ac)
    sy = nc.sync

    for dst, srcp in [(onec, cns["ones_col"]), (onerow, cns["ones_row"]),
                      (onerow32, cns["ones_row"]), (e16, td16), (eye16, cns["eye16"]),
                      (sel_lo, cns["sel_lo"]), (sel_hi, cns["sel_hi"]),
                      (half_lo, cns["half_lo"]), (half_hi, cns["half_hi"]),
                      (kq_iota, cns["iota_q"]), (j128m, j128_i), (k2048m, k2048_i),
                      (bo_v, bo_i), (rmsw_v, rmsw_i), (tri, tri_i), (iota64, iota64_i),
                      (hsel, hsel_i), (hselT, hselT_i), (selb, None), (gwb, gw_i)]:
        if dst is selb:
            s_in.inc(g.dma_start(out=selb[0:1, :], in_=selm_i), 16)
        elif dst is gwb:
            s_in.inc(g.dma_start(out=gwb[:, :, :], in_=gw_i.rearrange("(a p) o -> p a o", p=128)), 16)
        elif dst is hsel:
            s_in.inc(g.dma_start(out=hsel[:, :, :], in_=hsel_i), 16)
        elif dst is hselT:
            s_in.inc(g.dma_start(out=hselT[:, :, :], in_=hselT_i), 16)
        else:
            s_in.inc(g.dma_start(out=dst[:, :], in_=srcp), 16)
    consts_loaded = s_in.v

    aggs = sb("aggs", [128, 4, 512])
    s_in.inc(g.dma_start(out=aggs[:, :, :], in_=agg_i.rearrange("c p n -> p c n")), 16)
    consts_loaded = s_in.v

    esX = ExitStack()
    xpf = sb("xpf", [128, NCH, TC], F32, esX, side="right")
    esY = ExitStack()
    ysb = sb("ysb", [128, NCHUNK, NCH, 128], BF16, esY, side="right")
    esR = ExitStack()
    rsb = sb("rsb", [128, NCH, TC], BF16, esR, side="right")
    rwb2 = rsb

    for j in range(NCHUNK):
        s_y1.inc(sy.dma_start(out=ysb[:, j, :, :], in_=y1_i[j]), 16)
    y1_loaded = s_y1.v
    s_r.inc(sy.dma_start(out=rsb[:, :, :], in_=r_i.rearrange("(a p) t -> p a t", p=128)), 16)
    s_x.inc(sy.dma_start(out=xpf[:, :, :], in_=xpf_i.rearrange("(a p) t -> p a t", p=128)), 16)

    wbuf = sb("wbuf", [128, 1, NCH, C], BF16)
    wnames2 = ["WoT", "Wv1", "Wk1", "Wq"]
    wsrcs = {"WoT": wo_i, "Wv1": wv1_i, "Wk1": wk1_i, "Wq": wq_i}
    w_loaded = []
    proj_done_pe2 = [0] * 4

    def load_weight2(i, war_pe=0):
        if i >= 1:
            g.wait_ge(s_pe.h, war_pe)
            g.wait_ge(s_w[0].h, s_w[0].v)
        s_w[0].inc(g.dma_start(out=wbuf[:, 0, :, :],
                               in_=wsrcs[wnames2[i]].rearrange("(a p) o -> p a o", p=128)), 16)
        w_loaded.append((s_w[0].h, s_w[0].v))

    load_weight2(0)

    sc.wait_ge(s_in.h, consts_loaded)
    sc.activation(out=e16[:, :], in_=e16[:, :], func=AF.Exp)
    v.wait_ge(s_ac.h, s_ac.v)
    v.tensor_scalar_min(e16[:, :], e16[:, :], E_CLAMP)
    ve_e = s_ve.v
    t.wait_ge(s_ve.h, ve_e)
    t.matmul(psG[0:1, 0:16], e16[:, :], eye16[:, :], start=True, stop=True)
    s_pe.inc(t.last)
    v.wait_ge(s_pe.h, s_pe.v)
    v.tensor_copy(erow16[:, :], psG[0:1, 0:16])
    v.tensor_copy(ap(e16x8, 0, [[8, 16], [1, 8]]), ap(e16, 0, [[1, 16], [0, 8]]))
    v.tensor_mul(e16x8[:, :], e16x8[:, :], sel_lo[:, :])
    v.tensor_copy(ap(u16x8, 0, [[8, 16], [1, 8]]), ap(e16, 0, [[1, 16], [0, 8]]))
    v.tensor_mul(u16x8[:, :], u16x8[:, :], sel_hi[:, :])
    ve_sel = s_ve.v
    t.wait_ge(s_ve.h, ve_sel)
    t.matmul(psG[:, 32:40], half_lo[:, :], e16x8[:, :], start=True, stop=False)
    t.matmul(psG[:, 32:40], half_hi[:, :], u16x8[:, :], start=False, stop=True)
    s_pe.inc(t.last)
    v.wait_ge(s_pe.h, s_pe.v)
    v.tensor_copy(ecol[:, :], psG[:, 32:40])
    v.tensor_scalar_mul(necol[:, :], ecol[:, :], -1.0)
    ve_ecol = s_ve.v
    wtf = sb("wtf", [128, 128])
    for cr in range(8):
        sc.wait_ge(s_ve.h, ve_ecol)
        sc.activation(out=wbt[:, cr, :], in_=kq_iota[:, :], func=AF.Exp,
                      scale=necol[:, cr:cr + 1])
        v.tensor_scalar(wsjt[:, cr, :], j128m[:, :16], necol[:, cr:cr + 1], None, ALU.mult)
        v.tensor_scalar(ws2k[:, cr, :], k2048m[:, :], necol[:, cr:cr + 1], None, ALU.mult)
    v_exps = s_ve.v
    sc.wait_ge(s_ve.h, v_exps)
    for cr in range(8):
        sc.activation(out=wsjt[:, cr, :], in_=wsjt[:, cr, :], func=AF.Exp)
        sc.activation(out=ws2k[:, cr, :], in_=ws2k[:, cr, :], func=AF.Exp)
    ac_tables = s_ac.v

    v.tensor_copy(sbrow[:, :], selb[0:1, :])
    v.wait_ge(s_in.h, consts_loaded)
    t.wait_ge(s_ve.h, s_ve.v)
    t.matmul(psG[:, 64:80], onerow32[:, :], sbrow[:, :], start=True, stop=True)
    s_pe.inc(t.last)
    v.wait_ge(s_pe.h, s_pe.v)
    v.tensor_copy(selb[:, :], psG[:, 64:80])
    v.wait_ge(s_ac.h, ac_tables)
    for cp in range(4):
        v.tensor_scalar(coefp[:, cp, :], ap(ws2k, 0, [[32, 128], [4, 8]]),
                        selb[:, cp * 4:cp * 4 + 1], None, ALU.mult)
        for k in range(1, 4):
            v.scalar_tensor_tensor(coefp[:, cp, :], ap(ws2k, k, [[32, 128], [4, 8]]),
                                   selb[:, cp * 4 + k:cp * 4 + k + 1],
                                   coefp[:, cp, :], ALU.mult, ALU.add)
    v.memset(ist[:, :], 0.0)
    for cp in range(4):
        v.tensor_mul(isttmp[:, :], aggs[:, cp, :],
                     ap(coefp, cp * 8, [[32, 128], [1, 8], [0, 64]]))
        v.tensor_add(ist[:, :], ist[:, :], isttmp[:, :])
    v.tensor_copy(istb[:, :], ist[:, :])
    ve_ist = s_ve.v

    v.wait_ge(s_r.h, 16)
    for cr in range(8):
        v.tensor_mul(ap(rwb2, cr * TC, [[NCH * TC, 128], [128, 16], [1, 128]]),
                     ap(rsb, cr * TC, [[NCH * TC, 128], [128, 16], [1, 128]]),
                     ap(wbt, cr * 128, [[8 * 128, 128], [0, 16], [1, 128]]))
        v.tensor_mul(ap(rwb2, cr * TC, [[NCH * TC, 128], [128, 16], [1, 128]]),
                     ap(rwb2, cr * TC, [[NCH * TC, 128], [128, 16], [1, 128]]),
                     ap(wsjt, cr * 16, [[8 * 16, 128], [1, 16], [0, 128]]))
    ve_rwb2 = s_ve.v

    t.wait_ge(s_ve.h, ve_rwb2)
    t.wait_ge(s_y1.h, y1_loaded)
    y2_marks = {}
    for hh in range(16):
        half = (hh % 2) * 64
        pr = hh // 2
        bank = 2 + (hh % 2)
        for ti in range(NT):
            key = (hh % 2, (hh // 2) * NT + ti)
            prev = y2_marks.get((hh % 2, (hh // 2) * NT + ti - 1))
            if prev is not None:
                t.wait_ge(s_ve.h, prev)
            t.matmul(psA[half:half + 64, bank, :],
                     istb[half:half + 64, pr * 64:pr * 64 + 64],
                     rwb2[half:half + 64, pr, ti * 512:(ti + 1) * 512],
                     start=True, stop=True, skip_group_check=True)
            s_pe.inc(t.last)
            v.wait_ge(s_pe.h, s_pe.v)
            off = half * (NCHUNK * NCH * 128) + (ti * 4) * (NCH * 128) + pr * 128
            vw = ap(ysb, off, [[NCHUNK * NCH * 128, 64], [NCH * 128, 4], [1, 128]])
            v.tensor_add(vw, vw, psA[half:half + 64, bank, :])
            y2_marks[key] = s_ve.v
    ve_y2 = s_ve.v
    esR.close()

    t.wait_ge(*w_loaded[0])
    t.wait_ge(s_ve.h, ve_y2)
    t.wait_ge(s_x.h, 16)
    op_marks = []
    add_marks = []
    for mi in range(8):
        for ti in range(NT):
            idx = mi * NT + ti
            pa = psA[:, idx % 2, :]
            if idx >= 2:
                t.wait_ge(*op_marks[idx - 2])
            for k in range(NCH):
                t.matmul(pa, wbuf[:, 0, k, mi * 128:(mi + 1) * 128],
                         ap(ysb, (ti * 4) * (NCH * 128) + k * 128,
                            [[NCHUNK * NCH * 128, 128], [NCH * 128, 4], [1, 128]]),
                         start=(k == 0), stop=(k == NCH - 1))
            s_pe.inc(t.last)
            sc.wait_ge(s_pe.h, s_pe.v)
            if idx >= 2:
                sc.wait_ge(s_ve.h, add_marks[idx - 2])
            sc.activation(out=ostage[:, idx % 2, :], in_=pa, func=AF.Identity,
                          bias=bo_v[:, mi:mi + 1])
            op_marks.append((s_ac.h, s_ac.v))
            v.wait_ge(s_ac.h, s_ac.v)
            v.tensor_add(xpf[:, mi, ti * 512:(ti + 1) * 512],
                         xpf[:, mi, ti * 512:(ti + 1) * 512], ostage[:, idx % 2, :])
            add_marks.append(s_ve.v)
    ve_x1 = s_ve.v
    proj_done_pe2[0] = s_pe.v
    load_weight2(1, war_pe=s_pe.v)
    esY.close()

    xn2 = sb("xn2", [128, NCH, TC], BF16)
    esQ = ExitStack()
    xsq2 = sb("xsq2", [128, NCH, TC], BF16, esQ, side="right")
    v.wait_ge(s_ve.h, ve_x1)
    for ch in range(NCH):
        v.tensor_copy(xn2[:, ch, :], xpf[:, ch, :])
        sc.wait_ge(s_ve.h, s_ve.v)
        sc.activation(out=xsq2[:, ch, :], in_=xpf[:, ch, :], func=AF.Square)
    ve_xbf2 = s_ve.v
    ac_xsq2 = s_ac.v
    t.wait_ge(s_ve.h, ve_xbf2)
    t.wait_ge(s_ac.h, ac_xsq2)
    for ti in range(NT):
        a, b2 = ti * 512, (ti + 1) * 512
        w = 512
        for ch in range(NCH):
            t.matmul(psA[0:1, 0, 0:w], onec[:, :], xn2[:, ch, a:b2], start=(ch == 0), stop=(ch == NCH - 1))
        for ch in range(NCH):
            t.matmul(psA[0:1, 1, 0:w], onec[:, :], xsq2[:, ch, a:b2], start=(ch == 0), stop=(ch == NCH - 1))
        s_pe.inc(t.last)
        v.wait_ge(s_pe.h, s_pe.v)
        v.tensor_scalar_mul(srow[0:1, 0, 0:w], psA[0:1, 0, 0:w], 1.0 / C)
        v.tensor_scalar_mul(srow[0:1, 1, 0:w], psA[0:1, 1, 0:w], 1.0 / C)
        sc.wait_ge(s_ve.h, s_ve.v)
        sc.activation(out=srow[0:1, 3, 0:w], in_=srow[0:1, 0, 0:w], func=AF.Square)
        v.wait_ge(s_ac.h, s_ac.v)
        v.tensor_sub(srow[0:1, 1, 0:w], srow[0:1, 1, 0:w], srow[0:1, 3, 0:w])
        v.tensor_scalar_add(srow[0:1, 1, 0:w], srow[0:1, 1, 0:w], EPS_LN)
        sc.wait_ge(s_ve.h, s_ve.v)
        sc.activation(out=srow[0:1, 1, 0:w], in_=srow[0:1, 1, 0:w], func=AF.Ln)
        v.wait_ge(s_ac.h, s_ac.v)
        v.tensor_scalar_mul(srow[0:1, 1, 0:w], srow[0:1, 1, 0:w], -0.5)
        sc.wait_ge(s_ve.h, s_ve.v)
        sc.activation(out=srow[0:1, 1, 0:w], in_=srow[0:1, 1, 0:w], func=AF.Exp)
        v.wait_ge(s_ac.h, s_ac.v)
        v.tensor_mul(srow[0:1, 2, 0:w], srow[0:1, 0, 0:w], srow[0:1, 1, 0:w])
        t.wait_ge(s_ve.h, s_ve.v)
        t.matmul(psA[:, 2, 0:w], onerow32[:, :], srow[0:1, 1, 0:w], start=True, stop=True)
        t.matmul(psA[:, 3, 0:w], onerow32[:, :], srow[0:1, 2, 0:w], start=True, stop=True)
        s_pe.inc(t.last)
        v.wait_ge(s_pe.h, s_pe.v)
        for ch in range(NCH):
            v.tensor_mul(xn2[:, ch, a:b2], xn2[:, ch, a:b2], psA[:, 2, 0:w])
            v.tensor_sub(xn2[:, ch, a:b2], xn2[:, ch, a:b2], psA[:, 3, 0:w])
        t.wait_ge(s_ve.h, s_ve.v)
    ve_xn2 = s_ve.v
    esQ.close()

    sy.wait_ge(s_ve.h, ve_xn2)
    s_st.inc(sy.dma_start(out=x1_o.rearrange("(a p) t -> p a t", p=128), in_=xpf[:, :, :]), 16)
    s_st.inc(sy.dma_start(out=xn2_o.rearrange("(a p) t -> p a t", p=128), in_=xn2[:, :, :]), 16)
    x1_stored = s_st.v
    esX.close()

    esKV = ExitStack()
    vtk2 = sb("vtk2", [128, NCHUNK, C], BF16, esKV, side="right")
    ek = sb("ek", [128, NCHUNK, C], BF16, esKV, side="right")
    v.wait_ge(s_st.h, x1_stored)
    sc.wait_ge(s_st.h, x1_stored)
    g.wait_ge(s_st.h, x1_stored)

    def emit_projB(wslot, dst, act_exp, war):
        t.wait_ge(*w_loaded[wslot])
        t.wait_ge(s_ve.h, ve_xn2)
        marks = []
        for tj in range(NCHUNK):
            for ni in range(2):
                j2 = tj * 2 + ni
                pa = psA[:, j2 % 2, :]
                wv = marks[j2 - 2] if j2 >= 2 else war[j2]
                if wv is not None:
                    t.wait_ge(wv[0], wv[1])
                for k in range(NCH):
                    t.matmul(pa, xn2[:, k, tj * 128:(tj + 1) * 128],
                             wbuf[:, 0, k, ni * 512:(ni + 1) * 512],
                             start=(k == 0), stop=(k == NCH - 1))
                s_pe.inc(t.last)
                if act_exp:
                    sc.wait_ge(s_pe.h, s_pe.v)
                    sc.activation(out=dst[:, tj, ni * 512:(ni + 1) * 512], in_=pa, func=AF.Exp)
                    marks.append((s_ac.h, s_ac.v))
                else:
                    v.wait_ge(s_pe.h, s_pe.v)
                    v.tensor_copy(dst[:, tj, ni * 512:(ni + 1) * 512], pa)
                    marks.append((s_ve.h, s_ve.v))
        return marks, s_pe.v

    mv, pe_v1 = emit_projB(1, vtk2, False, [op_marks[-2], op_marks[-1]])
    proj_done_pe2[1] = pe_v1
    load_weight2(2, war_pe=pe_v1)
    mk, pe_k1 = emit_projB(2, ek, True, [mv[-2], mv[-1]])
    proj_done_pe2[2] = pe_k1

    t.wait_ge(mk[-1][0], mk[-1][1])
    t.wait_ge(mk[-2][0], mk[-2][1])
    for half in range(2):
        for tj in range(NCHUNK):
            t.matmul(psY[0:1, 0, 0:512], onec[:, :], ek[:, tj, half * 512:(half + 1) * 512],
                     start=(tj == 0), stop=(tj == NCHUNK - 1))
        s_pe.inc(t.last)
        v.wait_ge(s_pe.h, s_pe.v)
        v.tensor_copy(srow[0:1, half, 0:512], psY[0:1, 0, 0:512])
        t.wait_ge(s_ve.h, s_ve.v)
    sy.wait_ge(s_ve.h, s_ve.v)
    s_sc2.inc(sy.dma_start(out=s_o, in_=srow[0:1, 0:2, :].rearrange("p a t -> p (a t)")), 16)
    for bank in range(2):
        for hh in range(bank * 8, bank * 8 + 8):
            off = (hh % 8) * 64
            for tj in range(NCHUNK):
                t.matmul(psY[0:64, bank, off:off + 64],
                         ek[:, tj, hh * 64:hh * 64 + 64],
                         vtk2[:, tj, hh * 64:hh * 64 + 64],
                         start=(tj == 0), stop=(tj == NCHUNK - 1), skip_group_check=True)
        s_pe.inc(t.last)
        v.wait_ge(s_pe.h, s_pe.v)
        v.tensor_copy(ap(csb, bank * 8 * 64, [[16 * 64, 64], [1, 512]]), psY[0:64, bank, :])
    ve_csb = s_ve.v
    sy.wait_ge(s_ve.h, s_ve.v)
    s_sc2.inc(sy.dma_start(out=c_o, in_=csb[:, :, :].rearrange("p a t -> p (a t)")), 16)
    pe_sc = s_pe.v
    load_weight2(3, war_pe=pe_sc)
    esKV.close()

    qe = sb("qe", [128, NCH, TC], BF16, side="right")
    v.wait_ge(s_pe.h, pe_sc)
    sc.wait_ge(s_pe.h, pe_sc)
    t.wait_ge(*w_loaded[3])
    q_marks = []
    for mi in range(8):
        for ti in range(NT):
            idx = mi * NT + ti
            pa = psA[:, idx % 2, :]
            wv = q_marks[idx - 2] if idx >= 2 else (mk[-2:] + [None, None])[idx]
            if wv is not None:
                t.wait_ge(wv[0], wv[1])
            for k in range(NCH):
                t.matmul(pa, wbuf[:, 0, k, mi * 128:(mi + 1) * 128],
                         xn2[:, k, ti * 512:(ti + 1) * 512],
                         start=(k == 0), stop=(k == NCH - 1))
            s_pe.inc(t.last)
            sc.wait_ge(s_pe.h, s_pe.v)
            sc.activation(out=qe[:, mi, ti * 512:(ti + 1) * 512], in_=pa, func=AF.Exp)
            q_marks.append((s_ac.h, s_ac.v))
    pe_q = s_pe.v
    sc.wait_ge(s_ac.h, q_marks[-1][1])
    t.wait_ge(s_ac.h, q_marks[-1][1])
    t.wait_ge(s_ve.h, ve_csb)
    for ti in range(NT):
        for cr in range(8):
            t.matmul(psY[0:16, 0, 0:512], hsel[:, cr, :], qe[:, cr, ti * 512:(ti + 1) * 512],
                     start=(cr == 0), stop=(cr == 7))
        s_pe.inc(t.last)
        v.wait_ge(s_pe.h, s_pe.v)
        v.reciprocal(qrow[:, :], psY[0:16, 0, 0:512])
        t.wait_ge(s_ve.h, s_ve.v)
        for cr in range(8):
            t.matmul(psY[:, 1, 0:512], hselT[:, cr, :], qrow[:, :], start=True, stop=True)
            s_pe.inc(t.last)
            v.wait_ge(s_pe.h, s_pe.v)
            v.tensor_mul(qe[:, cr, ti * 512:(ti + 1) * 512],
                         qe[:, cr, ti * 512:(ti + 1) * 512], psY[:, 1, 0:512])
            t.wait_ge(s_ve.h, s_ve.v)
    ve_qn = s_ve.v
    sy.wait_ge(s_ve.h, ve_qn)
    s_st.inc(sy.dma_start(out=qn_o.rearrange("(a p) t -> p a t", p=128), in_=qe[:, :, :]), 16)
    qn_stored = s_st.v

    xr2f = sb("xr2f_sb", [128, NCH, TC], BF16, side="right")
    v.wait_ge(s_sc2.h, s_sc2.v)
    sc.wait_ge(s_sc2.h, s_sc2.v)
    for ch in range(NCH):
        sc.activation(out=xr2f[:, ch, :], in_=xn2[:, ch, :], func=AF.Square)
    ac_sq3 = s_ac.v
    t.wait_ge(s_ac.h, ac_sq3)
    for ti in range(NT):
        a, b2 = ti * 512, (ti + 1) * 512
        for ch in range(NCH):
            t.matmul(psA[0:1, 2, 0:512], onec[:, :], xr2f[:, ch, a:b2], start=(ch == 0), stop=(ch == NCH - 1))
        s_pe.inc(t.last)
        v.wait_ge(s_pe.h, s_pe.v)
        v.tensor_scalar_mul(srow[0:1, 0, 0:512], psA[0:1, 2, 0:512], 1.0 / C)
        v.tensor_scalar_add(srow[0:1, 0, 0:512], srow[0:1, 0, 0:512], EPS_RMS)
        sc.wait_ge(s_ve.h, s_ve.v)
        sc.activation(out=srow[0:1, 0, 0:512], in_=srow[0:1, 0, 0:512], func=AF.Ln)
        v.wait_ge(s_ac.h, s_ac.v)
        v.tensor_scalar_mul(srow[0:1, 0, 0:512], srow[0:1, 0, 0:512], -0.5)
        sc.wait_ge(s_ve.h, s_ve.v)
        sc.activation(out=srow[0:1, 0, 0:512], in_=srow[0:1, 0, 0:512], func=AF.Exp)
        t.wait_ge(s_ac.h, s_ac.v)
        t.matmul(psA[:, 3, 0:512], onerow32[:, :], srow[0:1, 0, 0:512], start=True, stop=True)
        s_pe.inc(t.last)
        v.wait_ge(s_pe.h, s_pe.v)
        for ch in range(NCH):
            v.tensor_mul(xr2f[:, ch, a:b2], xn2[:, ch, a:b2], psA[:, 3, 0:512])
            v.tensor_scalar(xr2f[:, ch, a:b2], xr2f[:, ch, a:b2], rmsw_v[:, ch:ch + 1], None, ALU.mult)
        t.wait_ge(s_ve.h, s_ve.v)
    ve_xr2 = s_ve.v
    sy.wait_ge(s_ve.h, ve_xr2)
    s_st.inc(sy.dma_start(out=xr2f_o.rearrange("(a p) t -> p a t", p=128), in_=xr2f[:, :, :]), 16)
    s_xf.inc(sy.dma_start(out=xrfd.rearrange("(a p) t -> p a t", p=128), in_=xr2f[:, :, :]), 16)

    s_t2 = [Ctr(es.enter_context(nc.semaphore("s_t20"))), Ctr(es.enter_context(nc.semaphore("s_t21")))]
    sy.wait_ge(s_xf.h, s_xf.v)
    for tj in range(NCHUNK):
        par = tj % 2
        if tj >= 2:
            sy.wait_ge(s_t2[par].h, 16 * (tj // 2))
        s_tr.inc(sy.dma_start_transpose(xrt[:, par, :],
                                        ap(xrfd.tensor, tj * 128, [[TC, 1024], [1, 128]])), 16)
        sy.wait_ge(s_tr.h, s_tr.v)
        s_t2[par].inc(sy.dma_start(out=xr2t_o[tj * 128:(tj + 1) * 128, :], in_=xrt[:, par, :]), 16)

    t.wait_ge(s_ve.h, ve_xr2)
    lg_marks = []
    for tj in range(NCHUNK):
        pa2 = psG[:, 0:64]
        if tj >= 1:
            t.wait_ge(s_ve.h, lg_marks[tj - 1])
        for k in range(NCH):
            t.matmul(pa2, xr2f[:, k, tj * 128:(tj + 1) * 128], gwb[:, k, :],
                     start=(k == 0), stop=(k == NCH - 1))
        s_pe.inc(t.last)
        v.wait_ge(s_pe.h, s_pe.v)
        v.tensor_copy(lgt[:, tj, :], pa2)
        lg_marks.append(s_ve.v)
    ve_lgt = s_ve.v
    v.memset(carry[:, :], 0.0)
    v.memset(meta[:, :, :], 0.0)
    for tj in range(NCHUNK):
        v.max(m8[:, :], lgt[:, tj, :])
        v.max_index(i8[:, :], m8[:, :], lgt[:, tj, :])
        v.tensor_copy(i8f[:, 0:2], i8[:, 0:2])
        v.tensor_sub(wrow[:, 0:1], m8[:, 0:1], m8[:, 1:2])
        sc.wait_ge(s_ve.h, s_ve.v)
        sc.activation(out=wrow[:, 1:2], in_=wrow[:, 0:1], func=AF.Sigmoid)
        v.wait_ge(s_ac.h, s_ac.v)
        v.tensor_scalar(wrow[:, 2:3], wrow[:, 1:2], -1.0, 1.0, ALU.mult, ALU.add)
        v.tensor_scalar(m1[:, :], iota64[:, :], i8f[:, 0:1], None, ALU.is_equal)
        v.tensor_scalar(m2[:, :], iota64[:, :], i8f[:, 1:2], None, ALU.is_equal)
        v.tensor_add(mb[:, :], m1[:, :], m2[:, :])
        v.tensor_copy(carryb[:, :], carry[:, :])
        ve_oh = s_ve.v
        t.wait_ge(s_ve.h, ve_oh)
        t.matmul(psG[:, 64:128], tri[:, :], mb[:, :], start=True, stop=False)
        t.matmul(psG[:, 64:128], onerow[:, :], carryb[0:1, :], start=False, stop=True)
        s_pe.inc(t.last)
        pe_pfx = s_pe.v
        t.matmul(psS[0:1, 0:64], onec[:, :], mb[:, :], start=True, stop=True)
        s_pe.inc(t.last)
        v.wait_ge(s_pe.h, s_pe.v)
        v.tensor_copy(pfx[:, :], psG[:, 64:128])
        v.tensor_add(carry[:, :], carry[:, :], psS[0:1, 0:64])
        v.tensor_mul(m1[:, :], m1[:, :], pfx[:, :])
        v.reduce_sum(rk[:, 0:1], m1[:, :], axis=mybir.AxisListType.X)
        v.tensor_add(pfx[:, :], pfx[:, :], m1[:, :])
        v.tensor_mul(m2[:, :], m2[:, :], pfx[:, :])
        v.reduce_sum(rk[:, 1:2], m2[:, :], axis=mybir.AxisListType.X)
        v.tensor_copy(meta[:, tj, 0:2], i8f[:, 0:2])
        v.tensor_copy(meta[:, tj, 2:3], wrow[:, 1:2])
        v.tensor_copy(meta[:, tj, 3:4], wrow[:, 2:3])
        v.tensor_copy(meta[:, tj, 4:6], rk[:, 0:2])
        t.wait_ge(s_ve.h, s_ve.v)
    ve_meta = s_ve.v
    sy.wait_ge(s_ve.h, ve_meta)
    s_st.inc(sy.dma_start(out=meta_o, in_=meta[:, :, :]), 16)
    s_st.inc(sy.dma_start(out=cnt_o, in_=carry[:, :]), 16)

    sy.wait_ge(s_st.h, s_st.v)
    sy.wait_ge(s_sc2.h, s_sc2.v)
    sy.wait_ge(s_t2[0].h, s_t2[0].v)
    sy.wait_ge(s_t2[1].h, s_t2[1].v)
    es.close()
    return nc


def _vec_pc(vec):
    return np.ascontiguousarray(np.asarray(vec, np.float32).reshape(8, 128).T)


def launch1_inputs(inp):
    x = np.asarray(inp["L0_attn_x"], np.float32)
    w_l1 = {
        "fWrT": np.ascontiguousarray(np.asarray(inp["L0_ffn_W_r"], np.float32).T),
        "fWkT": np.ascontiguousarray(np.asarray(inp["L0_ffn_W_k"], np.float32).T),
        "fWvT": np.ascontiguousarray(np.asarray(inp["L0_ffn_W_v"], np.float32).T),
        "WrT": np.ascontiguousarray(np.asarray(inp["L0_attn_Wr"], np.float32).T),
        "WkT": np.ascontiguousarray(np.asarray(inp["L0_attn_Wk"], np.float32).T),
        "WvT": np.ascontiguousarray(np.asarray(inp["L0_attn_Wv"], np.float32).T),
    }
    vecs = {
        "br": _vec_pc(inp["L0_attn_br"]),
        "bk": _vec_pc(inp["L0_attn_bk"]),
        "tmk": _vec_pc(inp["L0_attn_time_maa_k"]),
        "tmv": _vec_pc(inp["L0_attn_time_maa_v"]),
        "tmr": _vec_pc(inp["L0_attn_time_maa_r"]),
        "mur": _vec_pc(inp["L0_ffn_mu_r"]),
        "muk": _vec_pc(inp["L0_ffn_mu_k"]),
    }
    consts = {("c_" + k): v for k, v in CONSTS.items()}
    in_maps = []
    for c in range(8):
        b, s = divmod(c, 4)
        t0 = s * TC
        xs = np.zeros((C, TQ), np.float32)
        xs[:, 1:] = x[b, t0:t0 + TC, :].T
        if t0 > 0:
            xs[:, 0] = x[b, t0 - 1, :]
        m = {"xT": np.ascontiguousarray(xs),
             "bv_row": np.asarray(inp["L0_attn_bv"], np.float32)[None, :],
             "td16": np.asarray(inp["L0_attn_time_decay"], np.float32)[:, None],
             "tf16": np.asarray(inp["L0_attn_time_faaaa"], np.float32)[:, None]}
        m.update(w_l1)
        m.update(vecs)
        m.update(consts)
        in_maps.append(m)
    return in_maps


def _l2_consts():
    c = {}
    c["c_j128"] = np.broadcast_to((128.0 * np.arange(16, dtype=np.float32))[None, :], (128, 16)).copy()
    c["c_k2048"] = np.broadcast_to((2048.0 * np.arange(4, dtype=np.float32))[None, :], (128, 4)).copy()
    tri = (np.arange(128)[:, None] < np.arange(128)[None, :]).astype(np.float32)
    c["c_tri"] = tri
    c["c_iota64"] = np.broadcast_to(np.arange(64, dtype=np.float32)[None, :], (128, 64)).copy()
    hsel = np.zeros((128, 8, 16), np.float32)
    for p in range(128):
        for cr in range(8):
            hsel[p, cr, 2 * cr + (p >= 64)] = 1.0
    c["c_hsel"] = hsel
    c["c_hselT"] = np.ascontiguousarray(hsel.transpose(2, 1, 0))
    return c


L2CONSTS = _l2_consts()


def launch2_inputs(inp, l1res):
    consts = {("c_" + k): CONSTS[k] for k in ["ones_col", "ones_row", "iota_q", "eye16",
                                              "sel_lo", "sel_hi", "half_lo", "half_hi"]}
    consts.update(L2CONSTS)
    w2 = {
        "WoT": np.ascontiguousarray(np.asarray(inp["L0_attn_Wo"], np.float32).T),
        "Wq": np.ascontiguousarray(np.asarray(inp["L1_attn_Wq"], np.float32)),
        "Wk1": np.ascontiguousarray(np.asarray(inp["L1_attn_Wk"], np.float32)),
        "Wv1": np.ascontiguousarray(np.asarray(inp["L1_attn_Wv"], np.float32)),
        "gwT": np.ascontiguousarray(np.asarray(inp["L1_ffn_gate_w"], np.float32).T),
    }
    bo = _vec_pc(inp["L0_attn_bo"])
    rmsw = _vec_pc(inp["L1_ffn_rms_w"])
    in_maps = []
    for c in range(8):
        b, s = divmod(c, 4)
        aggp = np.stack([l1res[b * 4 + cc]["agg"] for cc in range(4)])
        selmat = np.zeros((4, 4), np.float32)
        for cp in range(s):
            selmat[cp, s - 1 - cp] = 1.0
        m = {"xpfT": l1res[c]["xpf"], "y1d": l1res[c]["y1T"], "rT": l1res[c]["rT"],
             "aggp": np.ascontiguousarray(aggp), "selmat": selmat.reshape(1, 16),
             "bo": bo, "rmsw": rmsw,
             "td16": np.asarray(inp["L0_attn_time_decay"], np.float32)[:, None]}
        m.update(w2)
        m.update(consts)
        in_maps.append(m)
    return in_maps


def launch2b_inputs(l2res):
    cnts = np.stack([np.asarray(l2res[cc]["cnt"], np.float32)[0] for cc in range(8)])
    base_tok = np.add.outer(np.arange(128, dtype=np.float32),
                            128.0 * np.arange(16, dtype=np.float32))
    in_maps = []
    for c in range(8):
        m = {"meta": np.asarray(l2res[c]["meta"], np.float32),
             "cnts": cnts,
             "pmask": (np.arange(8) < c).astype(np.float32)[:, None],
             "tokid": np.ascontiguousarray(base_tok + c * TC),
             "c_iota64": L2CONSTS["c_iota64"],
             "c_ones_row": CONSTS["ones_row"]}
        in_maps.append(m)
    return in_maps


def launch3_inputs(inp, l2res, l2bres):
    xall = np.concatenate([np.asarray(l2res[c]["xr2t"], BF) for c in range(8)], axis=0)
    w13 = np.asarray(inp["L1_ffn_w13"], np.float32)
    w2 = np.asarray(inp["L1_ffn_w2"], np.float32)
    in_maps = []
    for c in range(8):
        arr = np.stack([np.asarray(l2bres[cc]["invt"], np.float32)[c * 8192:(c + 1) * 8192]
                        for cc in range(8)])
        invp = np.ascontiguousarray(arr.reshape(8, 64, 128, 2).transpose(2, 0, 3, 1))
        m = {"invp": invp, "xall": xall, "c_eye": CONSTS["diagmask"],
             "w13s": np.ascontiguousarray(w13[c * 8:(c + 1) * 8]),
             "w2s": np.ascontiguousarray(w2[c * 8:(c + 1) * 8])}
        in_maps.append(m)
    return in_maps


def launch4_inputs(inp, l2res, l2bres, l3res):
    ytab = np.concatenate([np.asarray(l3res[c]["ytab"], BF) for c in range(8)]
                          + [np.zeros((1, C), BF)], axis=0)
    shared_w = {
        "Wo1": np.asarray(inp["L1_attn_Wo"], np.float32),
        "bo1pc": _vec_pc(inp["L1_attn_bo"]),
        "swg": np.asarray(inp["L1_ffn_sw_gate"], np.float32),
        "swu": np.asarray(inp["L1_ffn_sw_up"], np.float32),
        "swd": np.asarray(inp["L1_ffn_sw_down"], np.float32),
    }
    in_maps = []
    for c in range(8):
        b = c // 4
        sp4 = np.stack([np.asarray(l2res[b * 4 + cc]["sloc"], np.float32)[0]
                        for cc in range(4)])[None]
        cp4 = np.stack([np.asarray(l2res[b * 4 + cc]["cloc"], np.float32)
                        for cc in range(4)], axis=1)
        m = {"x1T": np.asarray(l2res[c]["x1T"], np.float32),
             "qnT": np.asarray(l2res[c]["qnT"], BF),
             "xn2f": np.asarray(l2res[c]["xn2f"], BF),
             "xr2f": np.asarray(l2res[c]["xr2f"], BF),
             "sp4": np.ascontiguousarray(sp4), "cp4": np.ascontiguousarray(cp4),
             "meta": np.asarray(l2res[c]["meta"], np.float32),
             "dtok": np.asarray(l2bres[c]["dtok"], np.float32),
             "ytab": ytab}
        m.update(shared_w)
        in_maps.append(m)
    return in_maps


def kernel(**inp):
    res1 = run_bass_kernel_spmd(build_launch1(), launch1_inputs(inp),
                                core_ids=list(range(8))).results
    res2 = run_bass_kernel_spmd(build_launch2(), launch2_inputs(inp, res1),
                                core_ids=list(range(8))).results
    res2b = run_bass_kernel_spmd(build_launch2b(), launch2b_inputs(res2),
                                 core_ids=list(range(8))).results
    res3 = run_bass_kernel_spmd(build_launch3(), launch3_inputs(inp, res2, res2b),
                                core_ids=list(range(8))).results
    res4 = run_bass_kernel_spmd(build_launch4(), launch4_inputs(inp, res2, res2b, res3),
                                core_ids=list(range(8))).results
    out = np.empty((B, T, C), np.float32)
    for c in range(8):
        b, s = divmod(c, 4)
        out[b, s * TC:(s + 1) * TC, :] = np.asarray(res4[c]["outT"], np.float32).T
    return out

I32 = mybir.dt.int32


def build_launch2b():
    nc = bass.Bass("TRN2", target_bir_lowering=False)
    es = ExitStack()

    def sb(name, shape, dt=F32):
        return es.enter_context(nc.sbuf_tensor(name, shape, dt))

    meta_i = nc.dram_tensor("meta", [128, 16, 8], F32, kind="ExternalInput").ap()
    cnts_i = nc.dram_tensor("cnts", [8, 64], F32, kind="ExternalInput").ap()
    pmask_i = nc.dram_tensor("pmask", [8, 1], F32, kind="ExternalInput").ap()
    tokid_i = nc.dram_tensor("tokid", [128, 16], F32, kind="ExternalInput").ap()
    iota64_i = nc.dram_tensor("c_iota64", [128, 64], F32, kind="ExternalInput").ap()
    onesr_i = nc.dram_tensor("c_ones_row", [1, 128], F32, kind="ExternalInput").ap()

    invt_o = nc.dram_tensor("invt", [65536, 2], F32, kind="ExternalOutput").ap()
    dtok_o = nc.dram_tensor("dtok", [128, 16, 2], F32, kind="ExternalOutput").ap()

    meta = sb("meta_sb", [128, 16, 8])
    cnts = sb("cnts_sb", [8, 64])
    pmask = sb("pmask_sb", [8, 1])
    tokid = sb("tokid_sb", [128, 16])
    iota64 = sb("iota64", [128, 64])
    onerow32 = sb("onerow32", [1, 128])
    prow = sb("prow", [1, 64])
    pb = sb("pb", [128, 64])
    m1 = sb("m1", [128, 64])
    selv = sb("selv", [128, 16, 2])
    gr = sb("gr", [128, 16, 2])
    flg = sb("flg", [128, 16, 2])
    nfl = sb("nfl", [128, 16, 2])
    dcol = sb("dcol", [128, 16, 2])
    di = sb("di", [128, 16, 2], I32)
    pay = sb("pay", [128, 16, 2, 2])
    zt = sb("zt", [128, 1024])

    psG = es.enter_context(nc.psum_tensor("psG", [128, 512], F32))
    s_in = Ctr(es.enter_context(nc.semaphore("s_in")))
    s_pe = Ctr(es.enter_context(nc.semaphore("s_pe")))
    s_ve = Ctr(es.enter_context(nc.semaphore("s_ve")))
    s_z = Ctr(es.enter_context(nc.semaphore("s_z")))
    s_st = Ctr(es.enter_context(nc.semaphore("s_st")))
    s_scp = [Ctr(es.enter_context(nc.semaphore("s_sc0"))),
             Ctr(es.enter_context(nc.semaphore("s_sc1")))]

    g = nc.gpsimd
    t = Rec(nc.tensor)
    v = Chain(nc.vector, s_ve)
    sy = nc.sync

    s_in.inc(g.dma_start(out=meta[:, :, :], in_=meta_i), 16)
    s_in.inc(g.dma_start(out=cnts[:, :], in_=cnts_i), 16)
    s_in.inc(g.dma_start(out=pmask[:, :], in_=pmask_i), 16)
    s_in.inc(g.dma_start(out=tokid[:, :], in_=tokid_i), 16)
    s_in.inc(g.dma_start(out=iota64[:, :], in_=iota64_i), 16)
    s_in.inc(g.dma_start(out=onerow32[:, :], in_=onesr_i), 16)
    loaded = s_in.v

    v.memset(zt[:, :], 0.0)
    sy.wait_ge(s_ve.h, s_ve.v)
    s_z.inc(sy.dma_start(out=ap(invt_o.tensor, 0, [[1024, 128], [1, 1024]]), in_=zt[:, :]), 16)

    t.wait_ge(s_in.h, loaded)
    t.matmul(psG[0:1, 0:64], pmask[:, :], cnts[:, :], start=True, stop=True)
    s_pe.inc(t.last)
    v.wait_ge(s_pe.h, s_pe.v)
    v.tensor_copy(prow[:, :], psG[0:1, 0:64])
    t.wait_ge(s_ve.h, s_ve.v)
    t.matmul(psG[:, 64:128], onerow32[:, :], prow[:, :], start=True, stop=True)
    s_pe.inc(t.last)
    v.wait_ge(s_pe.h, s_pe.v)
    v.tensor_copy(pb[:, :], psG[:, 64:128])
    g.wait_ge(s_z.h, 16)

    for tj in range(NCHUNK):
        for sl in range(2):
            v.scalar_tensor_tensor(m1[:, :], iota64[:, :], meta[:, tj, sl:sl + 1],
                                   pb[:, :], ALU.is_equal, ALU.mult)
            v.reduce_sum(selv[:, tj, sl:sl + 1], m1[:, :], axis=mybir.AxisListType.X)
    v.tensor_add(gr[:, :, :], selv[:, :, :], meta[:, :, 4:6])
    v.tensor_scalar(dcol[:, :, :], meta[:, :, 0:2], 1024.0, None, ALU.mult)
    v.tensor_add(dcol[:, :, :], dcol[:, :, :], gr[:, :, :])
    v.tensor_scalar(flg[:, :, :], gr[:, :, :], 1024.0, None, ALU.is_ge)
    v.tensor_scalar(nfl[:, :, :], flg[:, :, :], -1.0, 1.0, ALU.mult, ALU.add)
    v.tensor_mul(dcol[:, :, :], dcol[:, :, :], nfl[:, :, :])
    v.scalar_tensor_tensor(dcol[:, :, :], flg[:, :, :], 65536.0, dcol[:, :, :],
                           ALU.mult, ALU.add)
    v.tensor_copy(di[:, :, :], dcol[:, :, :])
    for sl in range(2):
        v.tensor_copy(pay[:, :, sl, 0], tokid[:, :])
        v.tensor_copy(pay[:, :, sl, 1], meta[:, :, 2 + sl])
    ve_all = s_ve.v
    sy.wait_ge(s_ve.h, ve_all)
    s_st.inc(sy.dma_start(out=dtok_o, in_=dcol[:, :, :]), 16)
    g.wait_ge(s_ve.h, ve_all)
    for tj in range(NCHUNK):
        for sl in range(2):
            s_scp[tj % 2].inc(g.indirect_dma_start(
                invt_o[:, :], bass.IndirectOffsetOnAxis(di[:, tj, sl:sl + 1], 0),
                pay[:, tj, sl, :], None,
                bounds_check=65535, oob_is_err=False), 16)
    sy.wait_ge(s_scp[0].h, s_scp[0].v)
    sy.wait_ge(s_scp[1].h, s_scp[1].v)
    sy.wait_ge(s_z.h, 16)
    sy.wait_ge(s_st.h, s_st.v)
    es.close()
    return nc


def build_launch3():
    nc = bass.Bass("TRN2", target_bir_lowering=False)
    es = ExitStack()

    def sb(name, shape, dt=F32, side=None):
        return es.enter_context(nc.sbuf_tensor(name, shape, dt, side=side))

    invp_i = nc.dram_tensor("invp", [128, 8, 2, 64], F32, kind="ExternalInput").ap()
    xall_i = nc.dram_tensor("xall", [B * T, C], BF16, kind="ExternalInput").ap()
    w13_i = nc.dram_tensor("w13s", [8, C, C], F32, kind="ExternalInput").ap()
    w2_i = nc.dram_tensor("w2s", [8, 512, C], F32, kind="ExternalInput").ap()
    eye_i = nc.dram_tensor("c_eye", [128, 128], F32, kind="ExternalInput").ap()

    ytab_o = nc.dram_tensor("ytab", [8192, C], BF16, kind="ExternalOutput").ap()

    invsb = sb("invsb", [128, 8, 2, 64])
    invs = sb("invs", [128, 2, 64])
    toki = sb("toki", [128, 64], I32)
    wcol = sb("wcol", [128, 64])
    eyeb = sb("eyeb", [128, 128], BF16)
    w13b = sb("w13b", [128, 2, NCH, C], BF16)
    w2b = sb("w2b", [128, 2, 4, C], BF16)
    xbt = sb("xbt", [128, 2, NCH, 1024], BF16)
    gact = sb("gact", [128, 4, 1024], BF16)
    usb = sb("usb", [128, 4, 1024], BF16)
    gsg = sb("gsg", [128, NCH, C], BF16)
    ybse = sb("ybse", [128, 16, 512], BF16)
    ytk2 = sb("ytk2", [128, NCH, C], BF16)

    psA = es.enter_context(nc.psum_tensor("psA", [128, 4, 512], F32))
    psT = es.enter_context(nc.psum_tensor("psT", [128, 4, 4, 128], F32))

    s_in = Ctr(es.enter_context(nc.semaphore("s_in")))
    s_pe = Ctr(es.enter_context(nc.semaphore("s_pe")))
    s_ve = Ctr(es.enter_context(nc.semaphore("s_ve")))
    s_ac = Ctr(es.enter_context(nc.semaphore("s_ac")))
    s_gq = Ctr(es.enter_context(nc.semaphore("s_gq")))
    s_w13 = [Ctr(es.enter_context(nc.semaphore("s_w13a"))), Ctr(es.enter_context(nc.semaphore("s_w13b")))]
    s_w2 = [Ctr(es.enter_context(nc.semaphore("s_w2a"))), Ctr(es.enter_context(nc.semaphore("s_w2b")))]
    s_st = Ctr(es.enter_context(nc.semaphore("s_st")))

    g = nc.gpsimd
    t = Rec(nc.tensor)
    v = Chain(nc.vector, s_ve)
    sc = Chain(nc.scalar, s_ac)
    sy = nc.sync

    s_in.inc(g.dma_start(out=invsb[:, :, :, :], in_=invp_i), 16)
    s_in.inc(g.dma_start(out=eyeb[:, :], in_=eye_i), 16)
    v.wait_ge(s_in.h, s_in.v)
    v.tensor_copy(invs[:, :, :], invsb[:, 0, :, :])
    for c2 in range(1, 8):
        v.tensor_add(invs[:, :, :], invs[:, :, :], invsb[:, c2, :, :])
    v.tensor_copy(toki[:, :], invs[:, 0, :])
    v.tensor_copy(wcol[:, :], invs[:, 1, :])
    ve_idx = s_ve.v
    g.wait_ge(s_ve.h, ve_idx)

    def load_w13(le, pe_mark):
        sl = le % 2
        if pe_mark:
            g.wait_ge(s_pe.h, pe_mark)
        s_w13[sl].inc(g.dma_start(out=w13b[:, sl, :, :],
                                  in_=w13_i[le].rearrange("(a p) o -> p a o", p=128)), 16)
        return (s_w13[sl].h, s_w13[sl].v)

    def load_w2(le, pe_mark):
        sl = le % 2
        if pe_mark:
            g.wait_ge(s_pe.h, pe_mark)
        s_w2[sl].inc(g.dma_start(out=w2b[:, sl, :, :],
                                 in_=w2_i[le].rearrange("(a p) o -> p a o", p=128)), 16)
        return (s_w2[sl].h, s_w2[sl].v)

    w13m = {0: load_w13(0, 0)}
    w2m = {0: load_w2(0, 0)}
    gu_mark = [0] * 8
    yb_mark = [0] * 8
    t1_mark = [0] * 8
    t1ve = [0] * 8
    hact_ve = [0] * 8
    guac = []
    ybve = []
    t1ev = {}
    t2ev = {}
    gq_mark = [0] * 8
    st_mark = 0
    ac_store_gate = 0

    for le in range(8):
        sl = le % 2
        if le >= 1:
            g.wait_ge(s_pe.h, t1_mark[le - 1])
        for i in range(8):
            s_gq.inc(g.indirect_dma_start(
                gsg[:, i, :], None,
                xall_i[:, :], bass.IndirectOffsetOnAxis(toki[:, le * 8 + i:le * 8 + i + 1], 0)), 16)
        gq_mark[le] = s_gq.v
        t.wait_ge(s_gq.h, gq_mark[le])
        for i in range(8):
            bp = (i % 2) * 2
            if (le, i - 2) in t1ev:
                t.wait_ge(s_ve.h, t1ev[(le, i - 2)])
            elif le >= 1 and i < 2:
                t.wait_ge(*t2ev[(le - 1, 14 + i)])
            for k in range(NCH):
                t.matmul(psT[:, bp + k // 4, k % 4, :], gsg[:, i, k * 128:(k + 1) * 128],
                         eyeb[:, :], start=True, stop=True)
            s_pe.inc(t.last)
            v.wait_ge(s_pe.h, s_pe.v)
            if i == 0 and le >= 2:
                v.wait_ge(s_pe.h, gu_mark[le - 2])
            v.tensor_copy(xbt[:, sl, :, i * 128:(i + 1) * 128], psT[:, bp:bp + 2, :, :])
            t1ev[(le, i)] = s_ve.v
        t1_mark[le] = s_pe.v
        t1ve[le] = s_ve.v
        t.wait_ge(s_ve.h, t1ve[le])
        t.wait_ge(*w13m[le])
        for n in range(16):
            mi, ti = n // 2, n % 2
            if len(guac) >= 2:
                t.wait_ge(s_ac.h, guac[-2])
            pa = psA[:, n % 2, :]
            for k in range(NCH):
                t.matmul(pa, w13b[:, sl, k, mi * 128:(mi + 1) * 128],
                         xbt[:, sl, k, ti * 512:(ti + 1) * 512],
                         start=(k == 0), stop=(k == NCH - 1))
            s_pe.inc(t.last)
            sc.wait_ge(s_pe.h, s_pe.v)
            if mi < 4:
                sc.activation(out=gact[:, mi, ti * 512:(ti + 1) * 512], in_=pa, func=AF.Silu)
            else:
                sc.activation(out=usb[:, mi - 4, ti * 512:(ti + 1) * 512], in_=pa, func=AF.Copy)
            guac.append(s_ac.v)
        gu_mark[le] = s_pe.v
        if le < 7:
            w13m[le + 1] = load_w13(le + 1, gu_mark[le - 1] if le >= 1 else 0)
        v.wait_ge(s_ac.h, guac[-1])
        for i4 in range(4):
            v.tensor_mul(gact[:, i4, :], gact[:, i4, :], usb[:, i4, :])
        hact_ve[le] = s_ve.v
        t.wait_ge(s_ve.h, hact_ve[le])
        t.wait_ge(*w2m[le])
        for n in range(16):
            mi, ti = n // 2, n % 2
            par = n % 2
            if len(ybve) >= 2:
                t.wait_ge(s_ve.h, ybve[-2])
            pa = psA[:, 2 + par, :]
            for k in range(4):
                t.matmul(pa, w2b[:, sl, k, mi * 128:(mi + 1) * 128],
                         gact[:, k, ti * 512:(ti + 1) * 512],
                         start=(k == 0), stop=(k == 3))
            s_pe.inc(t.last)
            v.wait_ge(s_pe.h, s_pe.v)
            v.tensor_copy(ybse[:, n, :], pa)
            ybve.append(s_ve.v)
        t.wait_ge(s_ve.h, ybve[-1])
        for n in range(16):
            mi, ti = n // 2, n % 2
            bp = (n % 2) * 2
            if n >= 2:
                t.wait_ge(*t2ev[(le, n - 2)])
            else:
                t.wait_ge(s_ve.h, t1ev[(le, 6 + n)])
            for j in range(4):
                t.matmul(psT[:, bp + j // 2, j % 2, :], ybse[:, n, j * 128:(j + 1) * 128],
                         eyeb[:, :], start=True, stop=True)
            s_pe.inc(t.last)
            eng = sc if n % 2 == 0 else v
            eng.wait_ge(s_pe.h, s_pe.v)
            if le >= 1 and n < 2:
                eng.wait_ge(s_st.h, ac_store_gate)
            if n % 2 == 0:
                sc.activation(out=ytk2[:, ti * 4:(ti + 1) * 4, mi * 128:(mi + 1) * 128],
                              in_=psT[:, bp:bp + 2, 0:2, :], func=AF.Copy)
                t2ev[(le, n)] = (s_ac.h, s_ac.v)
            else:
                v.tensor_copy(ytk2[:, ti * 4:(ti + 1) * 4, mi * 128:(mi + 1) * 128],
                              psT[:, bp:bp + 2, 0:2, :])
                t2ev[(le, n)] = (s_ve.h, s_ve.v)
        yb_mark[le] = s_pe.v
        if le < 7:
            w2m[le + 1] = load_w2(le + 1, yb_mark[le - 1] if le >= 1 else 0)
        v.wait_ge(s_ac.h, s_ac.v)
        if le >= 1:
            v.wait_ge(s_st.h, st_mark)
        for st in range(NCH):
            v.tensor_scalar(ytk2[:, st, :], ytk2[:, st, :],
                            wcol[:, le * 8 + st:le * 8 + st + 1], None, ALU.mult)
        ve_scale = s_ve.v
        sy.wait_ge(s_ve.h, ve_scale)
        s_st.inc(sy.dma_start(
            out=ap(ytab_o.tensor, le * 1024 * C, [[C, 128], [128 * C, NCH], [1, C]]),
            in_=ytk2[:, :, :]), 16)
        st_mark = s_st.v
        ac_store_gate = s_st.v
    sy.wait_ge(s_st.h, s_st.v)
    es.close()
    return nc


def build_launch4():
    nc = bass.Bass("TRN2", target_bir_lowering=False)
    es = ExitStack()

    def sb(name, shape, dt=F32, stack=None, side=None):
        return (stack or es).enter_context(nc.sbuf_tensor(name, shape, dt, side=side))

    x1_i = nc.dram_tensor("x1T", [C, TC], F32, kind="ExternalInput").ap()
    qn_i = nc.dram_tensor("qnT", [C, TC], BF16, kind="ExternalInput").ap()
    xn2_i = nc.dram_tensor("xn2f", [C, TC], BF16, kind="ExternalInput").ap()
    xr2_i = nc.dram_tensor("xr2f", [C, TC], BF16, kind="ExternalInput").ap()
    sp4_i = nc.dram_tensor("sp4", [1, 4, C], F32, kind="ExternalInput").ap()
    cp4_i = nc.dram_tensor("cp4", [64, 4, C], F32, kind="ExternalInput").ap()
    meta_i = nc.dram_tensor("meta", [128, 16, 8], F32, kind="ExternalInput").ap()
    dtok_i = nc.dram_tensor("dtok", [128, 16, 2], F32, kind="ExternalInput").ap()
    ytab_i = nc.dram_tensor("ytab", [65537, C], BF16, kind="ExternalInput").ap()
    wo1_i = nc.dram_tensor("Wo1", [C, C], F32, kind="ExternalInput").ap()
    bo1_i = nc.dram_tensor("bo1pc", [128, 8], F32, kind="ExternalInput").ap()
    swg_i = nc.dram_tensor("swg", [C, 512], F32, kind="ExternalInput").ap()
    swu_i = nc.dram_tensor("swu", [C, 512], F32, kind="ExternalInput").ap()
    swd_i = nc.dram_tensor("swd", [512, C], F32, kind="ExternalInput").ap()

    out_o = nc.dram_tensor("outT", [C, TC], F32, kind="ExternalOutput").ap()
    ctd = nc.dram_tensor("ctd", [64, 1024], F32, kind="Internal").ap()
    srsd = nc.dram_tensor("srsd", [1, 1024], F32, kind="Internal").ap()
    maccd = nc.dram_tensor("maccd", [TC, C], BF16, kind="Internal").ap()

    xacc = sb("xacc", [128, NCH, TC], F32)
    cbb = sb("cbb", [128, 8, 64], BF16)
    srs = sb("srs", [128, 8])
    bo1v = sb("bo1v", [128, 8])
    metw = sb("metw", [128, 16, 8])
    dtk = sb("dtk", [128, 16, 2])
    dti = sb("dti", [128, 16, 2], I32)

    psB = es.enter_context(nc.psum_tensor("psB", [128, 4, 512], F32))

    s_in = Ctr(es.enter_context(nc.semaphore("s_in")))
    s_x = Ctr(es.enter_context(nc.semaphore("s_x")))
    s_pe = Ctr(es.enter_context(nc.semaphore("s_pe")))
    s_ve = Ctr(es.enter_context(nc.semaphore("s_ve")))
    s_ac = Ctr(es.enter_context(nc.semaphore("s_ac")))
    s_rt = Ctr(es.enter_context(nc.semaphore("s_rt")))
    s_sr = Ctr(es.enter_context(nc.semaphore("s_sr")))
    s_cb = Ctr(es.enter_context(nc.semaphore("s_cb")))
    s_w = Ctr(es.enter_context(nc.semaphore("s_w")))
    s_ga = Ctr(es.enter_context(nc.semaphore("s_ga")))
    s_tr = Ctr(es.enter_context(nc.semaphore("s_tr")))
    s_mp = [Ctr(es.enter_context(nc.semaphore(f"s_mp{i}"))) for i in range(4)]
    s_gap = [Ctr(es.enter_context(nc.semaphore(f"s_gap{i}"))) for i in range(4)]
    s_trp = [Ctr(es.enter_context(nc.semaphore("s_trp0"))), Ctr(es.enter_context(nc.semaphore("s_trp1")))]
    s_st = Ctr(es.enter_context(nc.semaphore("s_st")))

    g = nc.gpsimd
    t = Rec(nc.tensor)
    v = Chain(nc.vector, s_ve)
    sc = Chain(nc.scalar, s_ac)
    sy = nc.sync

    b01 = []
    b23 = []

    s_in.inc(sy.dma_start(out=metw[:, :, :], in_=meta_i), 16)
    s_in.inc(sy.dma_start(out=dtk[:, :, :], in_=dtok_i), 16)
    s_in.inc(sy.dma_start(out=bo1v[:, :], in_=bo1_i), 16)
    s_x.inc(sy.dma_start(out=xacc[:, :, :], in_=x1_i.rearrange("(a p) t -> p a t", p=128)), 16)

    gth = sb("gth", [128, 4, 2, C], BF16, side="right")
    mst = sb("mst", [128, 4, C], BF16, side="right")
    msb = sb("msb", [128, 2, TC], BF16, side="right")

    st1 = ExitStack()
    xn2sb = sb("xn2sb", [128, NCH, TC], BF16, stack=st1, side="right")
    s4r = sb("s4r", [1, 4, C], F32, stack=st1, side="right")
    c4 = sb("c4", [64, 4, C], F32, stack=st1, side="right")
    csum = sb("csum", [64, C], F32, stack=st1, side="right")
    srow = sb("srow", [1, 2, C], F32, stack=st1, side="right")
    s_x.inc(sy.dma_start(out=xn2sb[:, :, :], in_=xn2_i.rearrange("(a p) t -> p a t", p=128)), 16)
    s_in.inc(sy.dma_start(out=s4r[:, :, :], in_=sp4_i), 16)
    s_in.inc(sy.dma_start(out=c4[:, :, :], in_=cp4_i), 16)
    in_done = s_in.v
    x_done = s_x.v

    v.wait_ge(s_in.h, in_done)
    v.tensor_copy(dti[:, :, :], dtk[:, :, :])
    ve_dti = s_ve.v
    v.tensor_add(csum[:, :], c4[:, 0, :], c4[:, 1, :])
    v.tensor_add(csum[:, :], csum[:, :], c4[:, 2, :])
    v.tensor_add(csum[:, :], csum[:, :], c4[:, 3, :])
    v.tensor_add(srow[0:1, 0, :], s4r[0:1, 0, :], s4r[0:1, 1, :])
    v.tensor_add(srow[0:1, 0, :], srow[0:1, 0, :], s4r[0:1, 2, :])
    v.tensor_add(srow[0:1, 0, :], srow[0:1, 0, :], s4r[0:1, 3, :])
    v.reciprocal(srow[0:1, 1, :], srow[0:1, 0, :])
    ve_sc = s_ve.v
    sy.wait_ge(s_ve.h, ve_sc)
    s_rt.inc(sy.dma_start(out=ctd, in_=csum[:, :]), 16)
    s_rt.inc(sy.dma_start(out=srsd, in_=srow[0:1, 1, :]), 16)
    rt_done = s_rt.v
    sy.wait_ge(s_rt.h, rt_done)
    s_sr.inc(sy.dma_start(out=srs[:, :], in_=srsd.rearrange("o (a p) -> p (o a)", p=128),
                          allow_slow_non_contiguous=True), 16)
    g.wait_ge(s_rt.h, rt_done)
    s_cb.inc(g.dma_start(out=cbb[0:64, :, :], in_=ap(ctd.tensor, 0, [[1024, 64], [128, 8], [1, 64]])), 16)
    s_cb.inc(g.dma_start(out=cbb[64:128, :, :], in_=ap(ctd.tensor, 64, [[1024, 64], [128, 8], [1, 64]])), 16)
    cb_done = s_cb.v

    gth_ve = [0, 0, 0, 0]

    def moe_gather(tj):
        slot = tj % 4
        if gth_ve[slot]:
            g.wait_ge(s_ve.h, gth_ve[slot])
        for sl in range(2):
            s_gap[slot].inc(g.indirect_dma_start(
                gth[:, slot, sl, :], None,
                ytab_i[:, :], bass.IndirectOffsetOnAxis(dti[:, tj, sl:sl + 1], 0)), 16)

    g.wait_ge(s_ve.h, ve_dti)
    for tj in range(4):
        moe_gather(tj)

    v.wait_ge(s_x.h, x_done)
    for a in range(NCH):
        v.tensor_add(xacc[:, a, :], xacc[:, a, :], xn2sb[:, a, :])
    ve_ph1 = s_ve.v
    st1.close()

    st2 = ExitStack()
    qnsb = sb("qnsb", [128, NCH, TC], BF16, stack=st2, side="right")
    asb = sb("asb", [128, NCH, TC], BF16, stack=st2, side="right")
    wo1b = sb("wo1b", [128, NCH, C], BF16, stack=st2, side="right")
    sy.wait_ge(s_ve.h, ve_ph1)
    s_x.inc(sy.dma_start(out=qnsb[:, :, :], in_=qn_i.rearrange("(a p) t -> p a t", p=128)), 16)
    qn_done = s_x.v
    s_w.inc(g.dma_start(out=wo1b[:, :, :], in_=wo1_i.rearrange("(a p) o -> p a o", p=128)), 16)
    wo_done = s_w.v

    v.wait_ge(s_sr.h, s_sr.v)
    v.tensor_scalar(srs[:, :], srs[:, :], 0.125, None, ALU.mult)
    v.wait_ge(s_x.h, qn_done)
    for a in range(NCH):
        v.tensor_scalar(qnsb[:, a, :], qnsb[:, a, :], srs[:, a:a + 1], None, ALU.mult)
    ve_qn = s_ve.v

    t.wait_ge(s_cb.h, cb_done)
    t.wait_ge(s_ve.h, ve_qn)
    n = 0
    for a in range(NCH):
        for ti in range(NT):
            bk = n % 2
            if len(b01) >= 2:
                t.wait_ge(*b01[-2])
            pa = psB[:, bk, :]
            t.matmul(pa[0:64, :], cbb[0:64, a, :], qnsb[0:64, a, ti * 512:(ti + 1) * 512],
                     start=True, stop=True)
            t.matmul(pa[64:128, :], cbb[64:128, a, :], qnsb[64:128, a, ti * 512:(ti + 1) * 512],
                     start=True, stop=True)
            s_pe.inc(t.last)
            v.wait_ge(s_pe.h, s_pe.v)
            v.tensor_copy(asb[:, a, ti * 512:(ti + 1) * 512], pa)
            b01.append((s_ve.h, s_ve.v))
            n += 1
    t.wait_ge(*b01[-1])
    t.wait_ge(s_w.h, wo_done)
    n = 0
    for mi in range(NCH):
        for ti in range(NT):
            bk = 2 + n % 2
            if len(b23) >= 2:
                t.wait_ge(*b23[-2])
            pa = psB[:, bk, :]
            for k in range(NCH):
                t.matmul(pa, wo1b[:, k, mi * 128:(mi + 1) * 128],
                         asb[:, k, ti * 512:(ti + 1) * 512],
                         start=(k == 0), stop=(k == NCH - 1))
            s_pe.inc(t.last)
            v.wait_ge(s_pe.h, s_pe.v)
            v.scalar_tensor_tensor(xacc[:, mi, ti * 512:(ti + 1) * 512], pa, bo1v[:, mi:mi + 1],
                                   xacc[:, mi, ti * 512:(ti + 1) * 512], ALU.add, ALU.add)
            b23.append((s_ve.h, s_ve.v))
            n += 1
    ve_attn = s_ve.v
    st2.close()

    st3 = ExitStack()
    xr2sb = sb("xr2sb", [128, NCH, TC], BF16, stack=st3, side="right")
    gs = sb("gs", [128, 4, TC], BF16, stack=st3, side="right")
    us = sb("us", [128, 4, TC], BF16, stack=st3, side="right")
    swgb = sb("swgb", [128, NCH, 512], BF16, stack=st3, side="right")
    swub = sb("swub", [128, NCH, 512], BF16, stack=st3, side="right")
    swdb = sb("swdb", [128, 4, C], BF16, stack=st3, side="right")
    sy.wait_ge(s_ve.h, ve_attn)
    s_x.inc(sy.dma_start(out=xr2sb[:, :, :], in_=xr2_i.rearrange("(a p) t -> p a t", p=128)), 16)
    xr_done = s_x.v
    g.wait_ge(s_ve.h, ve_attn)
    s_w.inc(g.dma_start(out=swgb[:, :, :], in_=swg_i.rearrange("(a p) o -> p a o", p=128)), 16)
    s_w.inc(g.dma_start(out=swub[:, :, :], in_=swu_i.rearrange("(a p) o -> p a o", p=128)), 16)
    s_w.inc(g.dma_start(out=swdb[:, :, :], in_=swd_i.rearrange("(k p) o -> p k o", p=128)), 16)
    sw_done = s_w.v

    for tj in range(NCHUNK):
        slot = tj % 4
        v.wait_ge(s_gap[slot].h, s_gap[slot].v)
        if s_mp[slot].v:
            v.wait_ge(s_mp[slot].h, s_mp[slot].v)
        v.tensor_add(mst[:, slot, :], gth[:, slot, 0, :], gth[:, slot, 1, :])
        gth_ve[slot] = s_ve.v
        if tj + 4 < NCHUNK:
            moe_gather(tj + 4)
        sy.wait_ge(s_ve.h, s_ve.v)
        s_mp[slot].inc(sy.dma_start(out=maccd[tj * 128:(tj + 1) * 128, :], in_=mst[:, slot, :]), 16)


    t.wait_ge(s_x.h, xr_done)
    t.wait_ge(s_w.h, sw_done)
    for phase, dst in ((0, gs), (1, us)):
        wsb = swgb if phase == 0 else swub
        n = 0
        for mi in range(4):
            for ti in range(NT):
                bk = n % 2
                if len(b01) >= 2:
                    t.wait_ge(*b01[-2])
                pa = psB[:, bk, :]
                for k in range(NCH):
                    t.matmul(pa, wsb[:, k, mi * 128:(mi + 1) * 128],
                             xr2sb[:, k, ti * 512:(ti + 1) * 512],
                             start=(k == 0), stop=(k == NCH - 1))
                s_pe.inc(t.last)
                sc.wait_ge(s_pe.h, s_pe.v)
                sc.activation(out=dst[:, mi, ti * 512:(ti + 1) * 512], in_=pa,
                              func=AF.Silu if phase == 0 else AF.Copy)
                b01.append((s_ac.h, s_ac.v))
                n += 1
    v.wait_ge(s_ac.h, s_ac.v)
    for i4 in range(4):
        v.tensor_mul(gs[:, i4, :], gs[:, i4, :], us[:, i4, :])
    ve_hs = s_ve.v
    t.wait_ge(s_ve.h, ve_hs)
    n = 0
    for mi in range(NCH):
        for ti in range(NT):
            bk = 2 + n % 2
            if len(b23) >= 2:
                t.wait_ge(*b23[-2])
            pa = psB[:, bk, :]
            for k in range(4):
                t.matmul(pa, swdb[:, k, mi * 128:(mi + 1) * 128],
                         gs[:, k, ti * 512:(ti + 1) * 512],
                         start=(k == 0), stop=(k == 3))
            s_pe.inc(t.last)
            v.wait_ge(s_pe.h, s_pe.v)
            v.tensor_add(xacc[:, mi, ti * 512:(ti + 1) * 512], pa,
                         xacc[:, mi, ti * 512:(ti + 1) * 512])
            b23.append((s_ve.h, s_ve.v))
            n += 1
    ve_sh = s_ve.v
    st3.close()

    for i in range(4):
        sy.wait_ge(s_mp[i].h, s_mp[i].v)
    msb_ve = [0, 0]
    for k in range(NCH):
        par = k % 2
        if msb_ve[par]:
            sy.wait_ge(s_ve.h, msb_ve[par])
        s_trp[par].inc(sy.dma_start_transpose(msb[:, par, :],
                                              ap(maccd.tensor, k * 128, [[C, TC], [1, 128]])), 16)
        v.wait_ge(s_trp[par].h, s_trp[par].v)
        v.tensor_add(xacc[:, k, :], msb[:, par, :], xacc[:, k, :])
        msb_ve[par] = s_ve.v

    sy.wait_ge(s_ve.h, s_ve.v)
    s_st.inc(sy.dma_start(out=out_o.rearrange("(a p) t -> p a t", p=128), in_=xacc[:, :, :]), 16)
    sy.wait_ge(s_st.h, s_st.v)
    es.close()
    return nc


# version 08 (final)
# speedup vs baseline: 1.0119x; 1.0119x over previous
from contextlib import ExitStack

import numpy as np
import ml_dtypes
import concourse.bass as bass
import concourse.mybir as mybir
from concourse.bass_utils import run_bass_kernel_spmd

F32 = mybir.dt.float32
BF16 = mybir.dt.bfloat16
AF = mybir.ActivationFunctionType
ALU = mybir.AluOpType
BF = ml_dtypes.bfloat16

B, T, C, H, N = 2, 8192, 1024, 16, 64
TC = 2048
TQ = TC + 1
NCH = 8
QW = 128
NCHUNK = 16
NT = 4
EPS_LN = 1e-5
EPS_RMS = 1e-6
E_CLAMP = 1e30

STAT_TILES = [(0, 512), (512, 1024), (1024, 1536), (1536, 2048), (2048, 2049)]


def _consts():
    c = {}
    c["ones_col"] = np.ones((128, 1), np.float32)
    c["ones_row"] = np.ones((1, 128), np.float32)
    c["iota_q"] = np.broadcast_to(np.arange(128, dtype=np.float32)[None, :], (128, 128)).copy()
    c["drev_row"] = (np.arange(128, dtype=np.float32) - 127.0)[None, :].copy()
    dd = np.arange(128)[None, :] - np.arange(128)[:, None] - 1
    c["dpq_relu"] = np.maximum(dd, 0).astype(np.float32)
    c["lowmask"] = (dd >= 0).astype(np.float32)
    c["diagmask"] = np.eye(128, dtype=np.float32)
    c["eye16"] = np.eye(16, dtype=np.float32)
    c["sel_lo"] = np.array([[1.0 if hh == 2 * cr else 0.0 for cr in range(8)] for hh in range(16)], np.float32)
    c["sel_hi"] = np.array([[1.0 if hh == 2 * cr + 1 else 0.0 for cr in range(8)] for hh in range(16)], np.float32)
    c["half_lo"] = np.array([[1.0 if p < 64 else 0.0 for p in range(128)] for _ in range(16)], np.float32)
    c["half_hi"] = np.array([[1.0 if p >= 64 else 0.0 for p in range(128)] for _ in range(16)], np.float32)
    return c


CONSTS = _consts()


class Ctr:
    def __init__(self, h):
        self.h = h
        self.v = 0

    def inc(self, inst, n=1):
        inst.then_inc(self.h, n)
        self.v += n
        return self.v


class Rec:

    def __init__(self, eng):
        self._e = eng
        self.last = None

    def wait_ge(self, h, val):
        return self._e.wait_ge(h, val)

    def __getattr__(self, nm):
        fn = getattr(self._e, nm)

        def wrap(*a, **k):
            inst = fn(*a, **k)
            self.last = inst
            return inst

        return wrap


class Chain:

    def __init__(self, eng, ctr):
        self._e = eng
        self._c = ctr

    def wait_ge(self, h, val):
        return self._e.wait_ge(h, val)

    def __getattr__(self, nm):
        fn = getattr(self._e, nm)

        def wrap(*a, **k):
            if self._c.v > 0:
                self._e.wait_ge(self._c.h, self._c.v)
            inst = fn(*a, **k)
            inst.then_inc(self._c.h, 1)
            self._c.v += 1
            return inst

        return wrap


def ap(tensor, offset, dims):
    return bass.AP(tensor, offset, dims)


def build_launch1(upto='full'):
    nc = bass.Bass("TRN2", target_bir_lowering=False)
    es = ExitStack()

    def sb(name, shape, dt=F32, stack=None, side=None):
        return (stack or es).enter_context(nc.sbuf_tensor(name, shape, dt, side=side))

    xin = nc.dram_tensor("xT", [C, TQ], F32, kind="ExternalInput").ap()
    wnames = ["fWrT", "fWkT", "fWvT", "WrT", "WkT", "WvT"]
    wts = {nm: nc.dram_tensor(nm, [C, C], F32, kind="ExternalInput").ap() for nm in wnames}
    vnames = ["br", "bk", "tmk", "tmv", "tmr", "mur", "muk"]
    VID = {nm: i for i, nm in enumerate(vnames)}
    VID["omur"] = 7
    VID["omuk"] = 8
    vecs = {nm: nc.dram_tensor(nm, [128, 8], F32, kind="ExternalInput").ap() for nm in vnames}
    bv_row = nc.dram_tensor("bv_row", [1, C], F32, kind="ExternalInput").ap()
    td16 = nc.dram_tensor("td16", [16, 1], F32, kind="ExternalInput").ap()
    tf16 = nc.dram_tensor("tf16", [16, 1], F32, kind="ExternalInput").ap()
    cns = {nm: nc.dram_tensor("c_" + nm, list(a.shape), F32, kind="ExternalInput").ap()
           for nm, a in CONSTS.items()}

    xpf_o = nc.dram_tensor("xpf", [C, TC], F32, kind="ExternalOutput").ap()
    y1_o = nc.dram_tensor("y1T", [NCHUNK, 128, NCH, 128], BF16, kind="ExternalOutput").ap()
    r_o = nc.dram_tensor("rT", [C, TC], BF16, kind="ExternalOutput").ap()
    agg_o = nc.dram_tensor("agg", [128, 512], F32, kind="ExternalOutput").ap()
    kfd = nc.dram_tensor("kf_dram", [C, TC], BF16, kind="Internal").ap()

    onec = sb("onec", [128, 1], BF16)
    onerow = sb("onerow", [1, 128], BF16)
    onerow32 = sb("onerow32", [1, 128], F32)
    vbuf = sb("vbuf", [128, 9, 8])
    bvr = sb("bvr", [1, C], BF16)
    e16 = sb("e16", [16, 1])
    u16 = sb("u16", [16, 1])
    e16x8 = sb("e16x8", [16, 8])
    u16x8 = sb("u16x8", [16, 8])
    erow16 = sb("erow16", [1, 16])
    urow16 = sb("urow16", [1, 16])
    erow = sb("erow", [1, 1024])
    ecol = sb("ecol", [128, 8])
    necol = sb("necol", [128, 8])
    necol128 = sb("necol128", [128, 16])
    ucol128 = sb("ucol128", [128, 16])
    wsp = sb("wsp", [128, 8])
    kq_iota = sb("kq_iota", [128, 128])
    kdpq = sb("kdpq", [128, 128])
    klow = sb("klow", [128, 128], BF16)
    kdiag = sb("kdiag", [128, 128])
    drev = sb("drev", [1, 128])
    sel_lo = sb("sel_lo", [16, 8])
    sel_hi = sb("sel_hi", [16, 8])
    eye16 = sb("eye16", [16, 16])
    half_lo = sb("half_lo", [16, 128])
    half_hi = sb("half_hi", [16, 128])
    wtmask = sb("wtmask", [128, 16, 128], BF16)
    wtf = sb("wtf", [128, 128])
    wbt = sb("wbt", [128, 8, 128], BF16)
    kwk = sb("kwk", [128, 1024], BF16)
    srow = sb("srow", [1, 4, 512])

    psA = es.enter_context(nc.psum_tensor("psA", [128, 4, 512], F32))
    psY = es.enter_context(nc.psum_tensor("psY", [128, 2, 512], F32))
    psS = es.enter_context(nc.psum_tensor("psS", [128, 512], F32))
    psG = es.enter_context(nc.psum_tensor("psG", [128, 512], F32))

    s_in = Ctr(es.enter_context(nc.semaphore("s_in")))
    s_x = Ctr(es.enter_context(nc.semaphore("s_x")))
    s_w = [Ctr(es.enter_context(nc.semaphore("s_w0"))), Ctr(es.enter_context(nc.semaphore("s_w1")))]
    s_pe = Ctr(es.enter_context(nc.semaphore("s_pe")))
    s_ve = Ctr(es.enter_context(nc.semaphore("s_ve")))
    s_ac = Ctr(es.enter_context(nc.semaphore("s_ac")))
    s_st = Ctr(es.enter_context(nc.semaphore("s_st")))
    s_tr = Ctr(es.enter_context(nc.semaphore("s_tr")))
    s_kf = Ctr(es.enter_context(nc.semaphore("s_kf")))
    s_xr = [Ctr(es.enter_context(nc.semaphore("s_xr0"))), Ctr(es.enter_context(nc.semaphore("s_xr1")))]
    s_sp = [Ctr(es.enter_context(nc.semaphore("s_sp0"))), Ctr(es.enter_context(nc.semaphore("s_sp1")))]
    s_y = [Ctr(es.enter_context(nc.semaphore("s_y0"))), Ctr(es.enter_context(nc.semaphore("s_y1")))]

    g = nc.gpsimd
    t = Rec(nc.tensor)
    v = Chain(nc.vector, s_ve)
    sc = Chain(nc.scalar, s_ac)
    sy = nc.sync

    for dst, src in [(onec, cns["ones_col"]), (onerow, cns["ones_row"]), (onerow32, cns["ones_row"]),
                     (bvr, bv_row), (e16, td16), (u16, tf16),
                     (kq_iota, cns["iota_q"]), (kdpq, cns["dpq_relu"]), (klow, cns["lowmask"]),
                     (kdiag, cns["diagmask"]), (drev, cns["drev_row"]), (sel_lo, cns["sel_lo"]),
                     (sel_hi, cns["sel_hi"]), (eye16, cns["eye16"]),
                     (half_lo, cns["half_lo"]), (half_hi, cns["half_hi"])]:
        s_in.inc(g.dma_start(out=dst[:, :] if len(dst.shape) == 2 else dst, in_=src), 16)
    for nm in vnames:
        s_in.inc(g.dma_start(out=vbuf[:, VID[nm], :], in_=vecs[nm]), 16)
    consts_loaded = s_in.v

    esXN = ExitStack()
    xnb = sb("xnb", [128, NCH, TQ], BF16, esXN, side="right")
    esA = ExitStack()
    xf32 = sb("xf32", [128, NCH, TQ], F32, esA, side="right")
    xsq = sb("xsq", [128, NCH, TQ], BF16, esA, side="right")

    s_x.inc(sy.dma_start(out=xf32[:, :, :], in_=xin.rearrange("(a p) t -> p a t", p=128)), 16)

    sc.wait_ge(s_in.h, consts_loaded)
    sc.activation(out=e16[:, :], in_=e16[:, :], func=AF.Exp)
    ac_e = s_ac.v
    v.wait_ge(s_ac.h, ac_e)
    v.tensor_scalar_min(e16[:, :], e16[:, :], E_CLAMP)
    ve_e = s_ve.v
    v.wait_ge(s_in.h, consts_loaded)
    v.tensor_scalar(vbuf[:, VID["omur"], :], vbuf[:, VID["mur"], :], -1.0, 1.0, ALU.mult, ALU.add)
    v.tensor_scalar(vbuf[:, VID["omuk"], :], vbuf[:, VID["muk"], :], -1.0, 1.0, ALU.mult, ALU.add)
    ve_omu = s_ve.v
    t.wait_ge(s_ve.h, ve_e)
    t.matmul(psG[0:1, 0:16], e16[:, :], eye16[:, :], start=True, stop=True)
    s_pe.inc(t.matmul(psG[0:1, 16:32], u16[:, :], eye16[:, :], start=True, stop=True))
    pe_rows = s_pe.v
    v.wait_ge(s_pe.h, pe_rows)
    v.tensor_copy(erow16[:, :], psG[0:1, 0:16])
    v.tensor_copy(urow16[:, :], psG[0:1, 16:32])
    v.tensor_copy(ap(erow, 0, [[1024, 1], [64, 16], [1, 64]]),
                           ap(erow16, 0, [[16, 1], [1, 16], [0, 64]]))
    ve_rows = s_ve.v
    v.tensor_copy(ap(e16x8, 0, [[8, 16], [1, 8]]), ap(e16, 0, [[1, 16], [0, 8]]))
    v.tensor_mul(e16x8[:, :], e16x8[:, :], sel_lo[:, :])
    v.tensor_copy(ap(u16x8, 0, [[8, 16], [1, 8]]), ap(e16, 0, [[1, 16], [0, 8]]))
    v.tensor_mul(u16x8[:, :], u16x8[:, :], sel_hi[:, :])
    ve_sel = s_ve.v
    t.wait_ge(s_ve.h, ve_sel)
    t.matmul(psG[:, 32:40], half_lo[:, :], e16x8[:, :], start=True, stop=False)
    s_pe.inc(t.matmul(psG[:, 32:40], half_hi[:, :], u16x8[:, :], start=False, stop=True))
    pe_ecol = s_pe.v
    v.wait_ge(s_pe.h, pe_ecol)
    v.tensor_copy(ecol[:, :], psG[:, 32:40])
    v.tensor_scalar_mul(necol[:, :], ecol[:, :], -1.0)
    ve_ecol = s_ve.v
    t.wait_ge(s_ve.h, ve_rows)
    t.matmul(psG[:, 48:64], onerow32[:, :], erow16[:, :], start=True, stop=True)
    s_pe.inc(t.matmul(psG[:, 64:80], onerow32[:, :], urow16[:, :], start=True, stop=True))
    pe_c128 = s_pe.v
    v.wait_ge(s_pe.h, pe_c128)
    v.tensor_scalar_mul(necol128[:, :], psG[:, 48:64], -1.0)
    v.tensor_copy(ucol128[:, :], psG[:, 64:80])
    ve_c128 = s_ve.v
    v.wait_ge(s_ve.h, ve_ecol)
    v.tensor_scalar_mul(wsp[:, :], necol[:, :], 128.0)
    sc.wait_ge(s_ve.h, s_ve.v)
    sc.activation(out=wsp[:, :], in_=wsp[:, :], func=AF.Exp)
    for cr in range(8):
        sc.activation(out=wbt[:, cr, :], in_=kq_iota[:, :], func=AF.Exp,
                               scale=necol[:, cr:cr + 1])
    sc.wait_ge(s_ve.h, ve_c128)
    def wslot(hh):
        return (hh % 2) * 8 + hh // 2

    for hh in range(16):
        sc.activation(out=wtf[:, :], in_=kdpq[:, :], func=AF.Exp,
                               scale=necol128[:, hh:hh + 1])
        v.wait_ge(s_ac.h, s_ac.v)
        v.tensor_mul(wtmask[:, wslot(hh), :], wtf[:, :], klow[:, :])
        v.scalar_tensor_tensor(wtmask[:, wslot(hh), :], kdiag[:, :], ucol128[:, hh:hh + 1],
                                        wtmask[:, wslot(hh), :], ALU.mult, ALU.add)
        sc.wait_ge(s_ve.h, s_ve.v - 1)
    t.wait_ge(s_ve.h, ve_rows)
    for i in range(2):
        s_pe.inc(t.matmul(psS[:, 0:512], drev[:, :], erow[0:1, i * 512:(i + 1) * 512],
                          start=True, stop=True))
        sc.wait_ge(s_pe.h, s_pe.v)
        sc.activation(out=kwk[:, i * 512:(i + 1) * 512], in_=psS[:, 0:512],
                               func=AF.Exp)
        t.wait_ge(s_ac.h, s_ac.v)
    ac_tables = s_ac.v
    ve_tables = s_ve.v

    v.wait_ge(s_x.h, 16)
    sc.wait_ge(s_x.h, 16)
    for ch in range(NCH):
        v.tensor_copy(xnb[:, ch, :], xf32[:, ch, :])
        sc.activation(out=xsq[:, ch, :], in_=xf32[:, ch, :], func=AF.Square)
    ve_xbf = s_ve.v
    ac_xsq = s_ac.v
    t.wait_ge(s_ve.h, ve_xbf)
    t.wait_ge(s_ac.h, ac_xsq)
    t.wait_ge(s_ac.h, ac_tables)
    t.wait_ge(s_ve.h, ve_tables)
    for ti, (a, b2) in enumerate(STAT_TILES):
        w = b2 - a
        for ch in range(NCH):
            t.matmul(psA[0:1, 0, 0:w], onec[:, :], xnb[:, ch, a:b2], start=(ch == 0), stop=(ch == NCH - 1))
        for ch in range(NCH):
            t.matmul(psA[0:1, 1, 0:w], onec[:, :], xsq[:, ch, a:b2], start=(ch == 0), stop=(ch == NCH - 1))
        s_pe.inc(t.last)
        pe_stat = s_pe.v
        v.wait_ge(s_pe.h, pe_stat)
        v.tensor_scalar_mul(srow[0:1, 0, 0:w], psA[0:1, 0, 0:w], 1.0 / C)
        v.tensor_scalar_mul(srow[0:1, 1, 0:w], psA[0:1, 1, 0:w], 1.0 / C)
        sc.wait_ge(s_ve.h, s_ve.v)
        sc.activation(out=srow[0:1, 3, 0:w], in_=srow[0:1, 0, 0:w], func=AF.Square)
        v.wait_ge(s_ac.h, s_ac.v)
        v.tensor_sub(srow[0:1, 1, 0:w], srow[0:1, 1, 0:w], srow[0:1, 3, 0:w])
        v.tensor_scalar_add(srow[0:1, 1, 0:w], srow[0:1, 1, 0:w], EPS_LN)
        sc.wait_ge(s_ve.h, s_ve.v)
        sc.activation(out=srow[0:1, 1, 0:w], in_=srow[0:1, 1, 0:w], func=AF.Ln)
        v.wait_ge(s_ac.h, s_ac.v)
        v.tensor_scalar_mul(srow[0:1, 1, 0:w], srow[0:1, 1, 0:w], -0.5)
        sc.wait_ge(s_ve.h, s_ve.v)
        sc.activation(out=srow[0:1, 1, 0:w], in_=srow[0:1, 1, 0:w], func=AF.Exp)
        v.wait_ge(s_ac.h, s_ac.v)
        v.tensor_mul(srow[0:1, 2, 0:w], srow[0:1, 0, 0:w], srow[0:1, 1, 0:w])
        ve_row = s_ve.v
        t.wait_ge(s_ve.h, ve_row)
        t.matmul(psA[:, 2, 0:w], onerow32[:, :], srow[0:1, 1, 0:w], start=True, stop=True)
        s_pe.inc(t.matmul(psA[:, 3, 0:w], onerow32[:, :], srow[0:1, 2, 0:w], start=True, stop=True))
        pe_bc = s_pe.v
        v.wait_ge(s_pe.h, pe_bc)
        for ch in range(NCH):
            v.tensor_mul(xnb[:, ch, a:b2], xnb[:, ch, a:b2], psA[:, 2, 0:w])
            v.tensor_sub(xnb[:, ch, a:b2], xnb[:, ch, a:b2], psA[:, 3, 0:w])
        v.nop()
        t.wait_ge(s_ve.h, s_ve.v)
    ve_xn = s_ve.v
    esA.close()
    if upto == 'stageA':
        sy.wait_ge(s_ve.h, s_ve.v)
        es.close()
        return nc

    wbuf = sb("wbuf", [128, 2, NCH, C], BF16, esXN, side="right")
    mixt = sb("mixt", [128, 2, NCH, 512], BF16, esXN, side="right")
    esFFN = ExitStack()
    ffs = sb("ffs", [128, NCH, TC], BF16, esFFN, side="right")
    ffk = sb("ffk", [128, NCH, TC], BF16, esFFN, side="right")
    xrel = sb("xrel", [128, 2, 512], F32, esFFN, side="right")
    xpfs = sb("xpfs", [128, 2, 512], F32, esFFN, side="right")

    proj_done_pe = [None] * 6
    w_loaded = []

    def load_weight(i):
        sl = i % 2
        if i < 2:
            g.wait_ge(s_ve.h, ve_xn)
        else:
            g.wait_ge(s_pe.h, proj_done_pe[i - 2])
            g.wait_ge(s_w[sl].h, s_w[sl].v)
        s_w[sl].inc(g.dma_start(out=wbuf[:, sl, :, :],
                                in_=wts[wnames[i]].rearrange("(a p) o -> p a o", p=128)), 16)
        w_loaded.append((s_w[sl].h, s_w[sl].v))

    load_weight(0)
    load_weight(1)

    def emit_mix_tile2(slot, ti, vec_id):
        a = ti * 512
        for ch in range(NCH):
            sh = xnb[:, ch, a:a + 512]
            cu = xnb[:, ch, a + 1:a + 513]
            d = mixt[:, slot, ch, :]
            v.tensor_sub(d, sh, cu)
            last = v.scalar_tensor_tensor(d, d, vbuf[:, vec_id, ch:ch + 1], cu, ALU.mult, ALU.add)
        return last

    def emit_projA(wslot, vec_id, evac, marks, prev_war, mix_war_pe=0):
        t.wait_ge(*w_loaded[wslot])
        tile_pe_marks = []
        for ti in range(NT):
            v.wait_ge(s_pe.h, tile_pe_marks[ti - 2] if ti >= 2 else mix_war_pe)
            v.wait_ge(s_ve.h, ve_xn)
            emit_mix_tile2(ti % 2, ti, vec_id)
            t.wait_ge(s_ve.h, s_ve.v)
            for mi in range(8):
                idx = ti * 8 + mi
                bank = idx % 2
                pa = psA[:, bank, :]
                wv = marks[idx - 2] if idx >= 2 else prev_war[idx]
                t.wait_ge(wv[0], wv[1])
                for k in range(NCH):
                    t.matmul(pa, wbuf[:, wslot % 2, k, mi * 128:(mi + 1) * 128],
                             mixt[:, ti % 2, k, :], start=(k == 0), stop=(k == NCH - 1))
                s_pe.inc(t.last)
                evac(mi, ti, pa, s_pe.v)
            tile_pe_marks.append(s_pe.v)
        proj_done_pe[wslot] = s_pe.v
        return marks

    evac_marks = []

    def evac_sig(mi, ti, pa, pe_val):
        sc.wait_ge(s_pe.h, pe_val)
        sc.activation(out=ffs[:, mi, ti * 512:(ti + 1) * 512], in_=pa, func=AF.Sigmoid)
        evac_marks.append((s_ac.h, s_ac.v))

    v.wait_ge(s_ve.h, ve_omu)
    marks0 = emit_projA(0, VID["omur"], evac_sig, evac_marks, [(s_ve.h, ve_xn)] * 2)
    load_weight(2)

    p0_done = proj_done_pe[0]
    evac_marks = []

    def evac_relu(mi, ti, pa, pe_val):
        sc.wait_ge(s_pe.h, pe_val)
        sc.activation(out=ffk[:, mi, ti * 512:(ti + 1) * 512], in_=pa, func=AF.Relu)
        evac_marks.append((s_ac.h, s_ac.v))

    marks1 = emit_projA(1, VID["omuk"], evac_relu, evac_marks, marks0[-2:],
                        mix_war_pe=p0_done)
    load_weight(3)
    sc.wait_ge(s_ac.h, evac_marks[-1][1])
    for ch in range(NCH):
        sc.activation(out=ffk[:, ch, :], in_=ffk[:, ch, :], func=AF.Square)
    ac_ffk = s_ac.v

    evac_marks = []

    t.wait_ge(*w_loaded[2])
    t.wait_ge(s_ac.h, ac_ffk)
    idx = 0
    for ti in range(NT):
        for mi in range(8):
            bank = idx % 2
            pa = psA[:, bank, :]
            wv = evac_marks[idx - 2] if idx >= 2 else marks1[-2:][idx]
            t.wait_ge(wv[0], wv[1])
            for k in range(NCH):
                t.matmul(pa, wbuf[:, 2 % 2, k, mi * 128:(mi + 1) * 128],
                         ffk[:, k, ti * 512:(ti + 1) * 512], start=(k == 0), stop=(k == NCH - 1))
            s_pe.inc(t.last)
            pe_val = s_pe.v
            par = idx % 2
            if idx >= 2:
                sy.wait_ge(evac_marks[idx - 2][0], evac_marks[idx - 2][1] + 1)
            s_xr[par].inc(sy.dma_start(out=xrel[:, par, :],
                                       in_=ap(xin.tensor, mi * 128 * TQ + ti * 512 + 1, [[TQ, 128], [1, 512]])), 16)
            v.wait_ge(s_pe.h, pe_val)
            v.wait_ge(s_xr[par].h, s_xr[par].v)
            if idx >= 2:
                v.wait_ge(s_sp[par].h, 16 * (idx // 2))
            v.tensor_mul(xpfs[:, par, :], pa, ffs[:, mi, ti * 512:(ti + 1) * 512])
            v.tensor_add(xpfs[:, par, :], xpfs[:, par, :], xrel[:, par, :])
            evac_marks.append((s_ve.h, s_ve.v - 1))
            sy.wait_ge(s_ve.h, s_ve.v)
            s_sp[par].inc(sy.dma_start(out=ap(xpf_o.tensor, mi * 128 * TC + ti * 512, [[TC, 128], [1, 512]]),
                                       in_=xpfs[:, par, :]), 16)
            idx += 1
    proj_done_pe[2] = s_pe.v
    marks2 = evac_marks
    if upto == 'ffn':
        sy.wait_ge(s_sp[0].h, s_sp[0].v)
        sy.wait_ge(s_sp[1].h, s_sp[1].v)
        es.close()
        return nc
    load_weight(4)
    esFFN.close()

    rfe = sb("rfe", [128, NCH, TC], BF16)
    kfe = sb("kfe", [128, NCH, TC], BF16)
    vtk = sb("vtk", [128, NCHUNK, C], BF16)
    sc.wait_ge(s_sp[0].h, s_sp[0].v)
    sc.wait_ge(s_sp[1].h, s_sp[1].v)
    v.wait_ge(s_sp[0].h, s_sp[0].v)
    v.wait_ge(s_sp[1].h, s_sp[1].v)

    evac_marks = []

    def evac_r(mi, ti, pa, pe_val):
        sc.wait_ge(s_pe.h, pe_val)
        sc.activation(out=rfe[:, mi, ti * 512:(ti + 1) * 512], in_=pa, func=AF.Identity,
                               bias=vbuf[:, VID["br"], mi:mi + 1])
        evac_marks.append((s_ac.h, s_ac.v))
        sy.wait_ge(s_ac.h, s_ac.v)
        s_st.inc(sy.dma_start(out=ap(r_o.tensor, mi * 128 * TC + ti * 512, [[TC, 128], [1, 512]]),
                              in_=rfe[:, mi, ti * 512:(ti + 1) * 512]), 16)

    marks3 = emit_projA(3, VID["tmr"], evac_r, evac_marks, marks2[-2:],
                        mix_war_pe=proj_done_pe[1])
    load_weight(5)

    evac_marks = []

    def evac_k(mi, ti, pa, pe_val):
        sc.wait_ge(s_pe.h, pe_val)
        sc.activation(out=kfe[:, mi, ti * 512:(ti + 1) * 512], in_=pa, func=AF.Identity,
                               bias=vbuf[:, VID["bk"], mi:mi + 1])
        evac_marks.append((s_ac.h, s_ac.v))
        sy.wait_ge(s_ac.h, s_ac.v)
        s_kf.inc(sy.dma_start(out=ap(kfd.tensor, mi * 128 * TC + ti * 512, [[TC, 128], [1, 512]]),
                              in_=kfe[:, mi, ti * 512:(ti + 1) * 512]), 16)

    marks4 = emit_projA(4, VID["tmk"], evac_k, evac_marks, marks3[-2:],
                        mix_war_pe=proj_done_pe[3])
    kf_dram_done = s_kf.v

    vb_marks = []
    p5_tile_pe = []
    t.wait_ge(*w_loaded[5])
    for ti in range(NT):
        v.wait_ge(s_pe.h, p5_tile_pe[ti - 2] if ti >= 2 else proj_done_pe[4])
        emit_mix_tile2(ti % 2, ti, VID["tmv"])
        t.wait_ge(s_ve.h, s_ve.v)
        for sub in range(4):
            tj = ti * 4 + sub
            for ni in range(2):
                j2 = tj * 2 + ni
                pa = psA[:, j2 % 2, :]
                wv = vb_marks[j2 - 2] if j2 >= 2 else marks4[-2:][j2]
                t.wait_ge(wv[0], wv[1])
                for k in range(NCH):
                    t.matmul(pa, mixt[:, ti % 2, k, sub * 128:(sub + 1) * 128],
                             wbuf[:, 5 % 2, k, ni * 512:(ni + 1) * 512],
                             start=(k == 0), stop=False)
                t.matmul(pa, onerow[:, :], bvr[0:1, ni * 512:(ni + 1) * 512], start=False, stop=True)
                s_pe.inc(t.last)
                v.wait_ge(s_pe.h, s_pe.v)
                v.tensor_copy(vtk[:, tj, ni * 512:(ni + 1) * 512], pa)
                vb_marks.append((s_ve.h, s_ve.v))
        p5_tile_pe.append(s_pe.v)
    proj_done_pe[5] = s_pe.v
    vtk_done = s_ve.v

    esXN.close()

    ktk = sb("ktk", [128, NCHUNK, C], BF16)
    rwb = sb("rwb", [128, NCH, TC], BF16)
    asb = sb("asb", [128, 16, 128], BF16)
    st2 = sb("st2", [128, 2, 512], F32)
    st2b = sb("st2b", [128, 2, 512], BF16)
    y1s = sb("y1s", [128, 2, 8, 128], BF16)

    sy.wait_ge(s_kf.h, kf_dram_done)
    sy.wait_ge(s_pe.h, proj_done_pe[5])
    sy.wait_ge(s_ve.h, vtk_done)
    for tj in range(NCHUNK):
        s_tr.inc(sy.dma_start_transpose(ktk[:, tj, :],
                                        ap(kfd.tensor, tj * 128, [[TC, 1024], [1, 128]])), 16)
    ktk_done = s_tr.v
    v.wait_ge(s_tr.h, ktk_done)
    v.wait_ge(s_ac.h, ac_tables)
    for tj in range(NCHUNK):
        v.tensor_mul(ktk[:, tj, :], ktk[:, tj, :], kwk[:, :])
    kscaled_done = s_ve.v

    v.wait_ge(s_ac.h, s_ac.v)
    for cr in range(8):
        v.tensor_mul(ap(rwb, cr * TC, [[NCH * TC, 128], [128, 16], [1, 128]]),
                     ap(rfe, cr * TC, [[NCH * TC, 128], [128, 16], [1, 128]]),
                     ap(wbt, cr * 128, [[8 * 128, 128], [0, 16], [1, 128]]))
    v.memset(st2[:, 0, :], 0.0)
    v.memset(st2b[:, 0, :], 0.0)
    wkv_ready_ve = s_ve.v

    if upto == 'attnproj':
        sy.wait_ge(s_ve.h, s_ve.v)
        sy.wait_ge(s_sp[0].h, s_sp[0].v)
        sy.wait_ge(s_sp[1].h, s_sp[1].v)
        sy.wait_ge(s_st.h, s_st.v)
        es.close()
        return nc

    peb = s_pe.v
    veb = s_ve.v
    acb = s_ac.v

    def peA(j):
        return peb + 4 * j + 1

    def peS(j):
        return peb + 4 * j + 2

    def peY(j):
        return peb + 4 * j + 3

    def peYo(j):
        return peb + 4 * j + 4

    def veM(j):
        return veb + 15 * j + 4

    def veS(j):
        return veb + 15 * j + 13

    def veYo(j):
        return veb + 15 * j + 15

    def acY(j):
        return acb + 8 * (j + 1)

    def slot_of(hh):
        return (hh % 2) * 8 + hh // 2

    t.wait_ge(s_ve.h, wkv_ready_ve)
    t.wait_ge(s_ac.h, ac_tables)
    for j in range(NCHUNK):
        cur = j % 2
        if j >= 1:
            t.wait_ge(s_ve.h, veYo(j - 1))
        for s_ in range(16):
            hh = (s_ // 8) + 2 * (s_ % 8)
            cr, half = hh // 2, (hh % 2) * 64
            t.matmul(psA[:, s_ // 4, (s_ % 4) * 128:(s_ % 4) * 128 + 128],
                     kfe[half:half + 64, cr, j * 128:(j + 1) * 128],
                     rfe[half:half + 64, cr, j * 128:(j + 1) * 128],
                     start=True, stop=True, skip_group_check=True)
        s_pe.inc(t.last)
        if j >= 1:
            t.wait_ge(s_ve.h, veS(j - 1))
        for hh in range(16):
            pr, half = hh // 2, (hh % 2) * 64
            t.matmul(psS[half:half + 64, pr * 64:pr * 64 + 64],
                     ktk[:, j, hh * 64:hh * 64 + 64],
                     vtk[:, j, hh * 64:hh * 64 + 64],
                     start=True, stop=True, skip_group_check=True)
        s_pe.inc(t.last)
        t.wait_ge(s_ve.h, veM(j))
        if j >= 1:
            t.wait_ge(s_ac.h, acY(j - 1))
        for pr in range(8):
            bank, off = pr // 4, (pr % 4) * 128
            t.matmul(psY[0:64, bank, off:off + 128],
                     vtk[:, j, (2 * pr) * 64:(2 * pr) * 64 + 64],
                     asb[:, slot_of(2 * pr), :],
                     start=True, stop=False, skip_group_check=True)
            t.matmul(psY[0:64, bank, off:off + 128],
                     st2b[0:64, cur, pr * 64:pr * 64 + 64],
                     rwb[0:64, pr, j * 128:(j + 1) * 128],
                     start=False, stop=True, skip_group_check=True)
            t.matmul(psY[64:128, bank, off:off + 128],
                     vtk[:, j, (2 * pr + 1) * 64:(2 * pr + 1) * 64 + 64],
                     asb[:, slot_of(2 * pr + 1), :],
                     start=True, stop=True, skip_group_check=True)
        s_pe.inc(t.last)
        for pr in range(8):
            bank, off = 2 + pr // 4, (pr % 4) * 128
            t.matmul(psA[64:128, bank, off:off + 128],
                     st2b[64:128, cur, pr * 64:pr * 64 + 64],
                     rwb[64:128, pr, j * 128:(j + 1) * 128],
                     start=True, stop=True, skip_group_check=True)
        s_pe.inc(t.last)

    for j in range(NCHUNK):
        cur = j % 2
        nxt = 1 - cur
        v.wait_ge(s_pe.h, peA(j))
        for bank in range(4):
            v.tensor_mul(ap(asb, bank * 4 * 128, [[16 * 128, 128], [1, 512]]),
                         psA[:, bank, :],
                         ap(wtmask, bank * 4 * 128, [[16 * 128, 128], [1, 512]]))
        assert s_ve.v == veM(j)
        v.wait_ge(s_pe.h, peS(j))
        if j >= 1:
            v.wait_ge(s_pe.h, peY(j - 1))
        for pr in range(8):
            v.scalar_tensor_tensor(st2[:, nxt, pr * 64:pr * 64 + 64],
                                   st2[:, cur, pr * 64:pr * 64 + 64],
                                   wsp[:, pr:pr + 1],
                                   psS[:, pr * 64:pr * 64 + 64],
                                   ALU.mult, ALU.add)
        if j >= 1:
            v.wait_ge(s_pe.h, peYo(j - 1))
        v.tensor_copy(st2b[:, nxt, :], st2[:, nxt, :])
        assert s_ve.v == veS(j)
        v.wait_ge(s_ac.h, acY(j))
        v.wait_ge(s_pe.h, peYo(j))
        for bank in range(2):
            v.tensor_add(ap(y1s, (j % 2) * 8 * 128 + bank * 4 * 128 + 64 * (2 * 8 * 128),
                            [[2 * 8 * 128, 64], [1, 512]]),
                         ap(y1s, (j % 2) * 8 * 128 + bank * 4 * 128 + 64 * (2 * 8 * 128),
                            [[2 * 8 * 128, 64], [1, 512]]),
                         ap(psA, 64 * (4 * 512) + (2 + bank) * 512, [[4 * 512, 64], [1, 512]]))
        assert s_ve.v == veYo(j)

    for j in range(NCHUNK):
        sc.wait_ge(s_pe.h, peY(j))
        if j >= 2:
            sc.wait_ge(s_y[j % 2].h, 16 * (j // 2))
        for pr in range(8):
            bank, off = pr // 4, (pr % 4) * 128
            sc.activation(out=y1s[:, j % 2, pr, :], in_=psY[:, bank, off:off + 128], func=AF.Copy)
        assert s_ac.v == acY(j)
        sy.wait_ge(s_ve.h, veYo(j))
        s_y[j % 2].inc(sy.dma_start(out=y1_o[j], in_=y1s[:, j % 2, :, :]), 16)

    sy.wait_ge(s_ve.h, veS(NCHUNK - 1))
    s_st.inc(sy.dma_start(out=agg_o, in_=st2[:, NCHUNK % 2, :]), 16)
    sy.wait_ge(s_st.h, s_st.v)
    sy.wait_ge(s_sp[0].h, s_sp[0].v)
    sy.wait_ge(s_sp[1].h, s_sp[1].v)
    sy.wait_ge(s_y[0].h, s_y[0].v)
    sy.wait_ge(s_y[1].h, s_y[1].v)

    es.close()
    return nc




def build_launch2():
    nc = bass.Bass("TRN2", target_bir_lowering=False)
    es = ExitStack()

    def sb(name, shape, dt=F32, stack=None, side=None):
        return (stack or es).enter_context(nc.sbuf_tensor(name, shape, dt, side=side))

    xpf_i = nc.dram_tensor("xpfT", [C, TC], F32, kind="ExternalInput").ap()
    y1_i = nc.dram_tensor("y1d", [NCHUNK, 128, NCH, 128], BF16, kind="ExternalInput").ap()
    r_i = nc.dram_tensor("rT", [C, TC], BF16, kind="ExternalInput").ap()
    agg_i = nc.dram_tensor("aggp", [4, 128, 512], F32, kind="ExternalInput").ap()
    selm_i = nc.dram_tensor("selmat", [1, 16], F32, kind="ExternalInput").ap()
    wo_i = nc.dram_tensor("WoT", [C, C], F32, kind="ExternalInput").ap()
    wq_i = nc.dram_tensor("Wq", [C, C], F32, kind="ExternalInput").ap()
    wk1_i = nc.dram_tensor("Wk1", [C, C], F32, kind="ExternalInput").ap()
    wv1_i = nc.dram_tensor("Wv1", [C, C], F32, kind="ExternalInput").ap()
    gw_i = nc.dram_tensor("gwT", [C, 64], F32, kind="ExternalInput").ap()
    bo_i = nc.dram_tensor("bo", [128, 8], F32, kind="ExternalInput").ap()
    rmsw_i = nc.dram_tensor("rmsw", [128, 8], F32, kind="ExternalInput").ap()
    td16 = nc.dram_tensor("td16", [16, 1], F32, kind="ExternalInput").ap()
    cns = {}
    for nm in ["ones_col", "ones_row", "iota_q", "eye16", "sel_lo", "sel_hi",
               "half_lo", "half_hi"]:
        cns[nm] = nc.dram_tensor("c_" + nm, list(CONSTS[nm].shape), F32, kind="ExternalInput").ap()
    j128_i = nc.dram_tensor("c_j128", [128, 16], F32, kind="ExternalInput").ap()
    k2048_i = nc.dram_tensor("c_k2048", [128, 4], F32, kind="ExternalInput").ap()
    tri_i = nc.dram_tensor("c_tri", [128, 128], F32, kind="ExternalInput").ap()
    iota64_i = nc.dram_tensor("c_iota64", [128, 64], F32, kind="ExternalInput").ap()
    hsel_i = nc.dram_tensor("c_hsel", [128, 8, 16], F32, kind="ExternalInput").ap()
    hselT_i = nc.dram_tensor("c_hselT", [16, 8, 128], F32, kind="ExternalInput").ap()

    x1_o = nc.dram_tensor("x1T", [C, TC], F32, kind="ExternalOutput").ap()
    qn_o = nc.dram_tensor("qnT", [C, TC], BF16, kind="ExternalOutput").ap()
    xn2_o = nc.dram_tensor("xn2f", [C, TC], BF16, kind="ExternalOutput").ap()
    xr2f_o = nc.dram_tensor("xr2f", [C, TC], BF16, kind="ExternalOutput").ap()
    xr2t_o = nc.dram_tensor("xr2t", [TC, C], BF16, kind="ExternalOutput").ap()
    s_o = nc.dram_tensor("sloc", [1, C], F32, kind="ExternalOutput").ap()
    c_o = nc.dram_tensor("cloc", [64, C], F32, kind="ExternalOutput").ap()
    meta_o = nc.dram_tensor("meta", [128, 16, 8], F32, kind="ExternalOutput").ap()
    cnt_o = nc.dram_tensor("cnt", [1, 64], F32, kind="ExternalOutput").ap()
    xrfd = nc.dram_tensor("xrf_dram", [C, TC], BF16, kind="Internal").ap()

    onec = sb("onec", [128, 1], BF16)
    onerow = sb("onerow", [1, 128], BF16)
    onerow32 = sb("onerow32", [1, 128], F32)
    e16 = sb("e16", [16, 1])
    eye16 = sb("eye16", [16, 16])
    erow16 = sb("erow16", [1, 16])
    e16x8 = sb("e16x8", [16, 8])
    u16x8 = sb("u16x8", [16, 8])
    sel_lo = sb("sel_lo", [16, 8])
    sel_hi = sb("sel_hi", [16, 8])
    half_lo = sb("half_lo", [16, 128])
    half_hi = sb("half_hi", [16, 128])
    ecol = sb("ecol", [128, 8])
    necol = sb("necol", [128, 8])
    kq_iota = sb("kq_iota", [128, 128])
    j128m = sb("j128m", [128, 16])
    k2048m = sb("k2048m", [128, 4])
    wbt = sb("wbt", [128, 8, 128], BF16)
    wsjt = sb("wsjt", [128, 8, 16])
    ws2k = sb("ws2k", [128, 8, 4])
    selb = sb("selb", [128, 16])
    coefp = sb("coefp", [128, 4, 8])
    ist = sb("ist", [128, 512])
    istb = sb("istb", [128, 512], BF16)
    isttmp = sb("isttmp", [128, 512])
    bo_v = sb("bo_v", [128, 8])
    rmsw_v = sb("rmsw_v", [128, 8])
    tri = sb("tri", [128, 128], BF16)
    iota64 = sb("iota64", [128, 64])
    hsel = sb("hsel", [128, 8, 16], BF16)
    hselT = sb("hselT", [16, 8, 128])
    srow = sb("srow", [1, 4, 512])
    gwb = sb("gwb", [128, 8, 64], BF16)
    csb = sb("csb", [64, 16, 64])
    qrow = sb("qrow", [16, 512])
    xrt = sb("xrt", [128, 2, C], BF16)
    lgt = sb("lgt", [128, NCHUNK, 64])
    m8 = sb("m8", [128, 8])
    i8 = sb("i8", [128, 8], mybir.dt.uint32)
    i8f = sb("i8f", [128, 2])
    wrow = sb("wrow", [128, 4])
    m1 = sb("m1", [128, 64], BF16)
    m2 = sb("m2", [128, 64], BF16)
    mb = sb("mb", [128, 64], BF16)
    pfx = sb("pfx", [128, 64])
    carry = sb("carry", [1, 64])
    carryb = sb("carryb", [1, 64], BF16)
    meta = sb("meta_sb", [128, NCHUNK, 8])
    rk = sb("rk", [128, 2])
    ostage = sb("ostage", [128, 2, 512], BF16)
    sbrow = sb("sbrow", [1, 16])

    psA = es.enter_context(nc.psum_tensor("psA", [128, 4, 512], F32))
    psY = es.enter_context(nc.psum_tensor("psY", [128, 2, 512], F32))
    psS = es.enter_context(nc.psum_tensor("psS", [128, 512], F32))
    psG = es.enter_context(nc.psum_tensor("psG", [128, 512], F32))

    s_in = Ctr(es.enter_context(nc.semaphore("s_in")))
    s_x = Ctr(es.enter_context(nc.semaphore("s_x")))
    s_y1 = Ctr(es.enter_context(nc.semaphore("s_y1")))
    s_r = Ctr(es.enter_context(nc.semaphore("s_r")))
    s_w = [Ctr(es.enter_context(nc.semaphore("s_w0"))), Ctr(es.enter_context(nc.semaphore("s_w1")))]
    s_pe = Ctr(es.enter_context(nc.semaphore("s_pe")))
    s_ve = Ctr(es.enter_context(nc.semaphore("s_ve")))
    s_ac = Ctr(es.enter_context(nc.semaphore("s_ac")))
    s_st = Ctr(es.enter_context(nc.semaphore("s_st")))
    s_tr = Ctr(es.enter_context(nc.semaphore("s_tr")))
    s_xf = Ctr(es.enter_context(nc.semaphore("s_xf")))
    s_sc2 = Ctr(es.enter_context(nc.semaphore("s_sc2")))

    g = nc.gpsimd
    t = Rec(nc.tensor)
    v = Chain(nc.vector, s_ve)
    sc = Chain(nc.scalar, s_ac)
    sy = nc.sync

    for dst, srcp in [(onec, cns["ones_col"]), (onerow, cns["ones_row"]),
                      (onerow32, cns["ones_row"]), (e16, td16), (eye16, cns["eye16"]),
                      (sel_lo, cns["sel_lo"]), (sel_hi, cns["sel_hi"]),
                      (half_lo, cns["half_lo"]), (half_hi, cns["half_hi"]),
                      (kq_iota, cns["iota_q"]), (j128m, j128_i), (k2048m, k2048_i),
                      (bo_v, bo_i), (rmsw_v, rmsw_i), (tri, tri_i), (iota64, iota64_i),
                      (hsel, hsel_i), (hselT, hselT_i), (selb, None), (gwb, gw_i)]:
        if dst is selb:
            s_in.inc(g.dma_start(out=selb[0:1, :], in_=selm_i), 16)
        elif dst is gwb:
            s_in.inc(g.dma_start(out=gwb[:, :, :], in_=gw_i.rearrange("(a p) o -> p a o", p=128)), 16)
        elif dst is hsel:
            s_in.inc(g.dma_start(out=hsel[:, :, :], in_=hsel_i), 16)
        elif dst is hselT:
            s_in.inc(g.dma_start(out=hselT[:, :, :], in_=hselT_i), 16)
        else:
            s_in.inc(g.dma_start(out=dst[:, :], in_=srcp), 16)
    consts_loaded = s_in.v

    aggs = sb("aggs", [128, 4, 512])
    s_in.inc(g.dma_start(out=aggs[:, :, :], in_=agg_i.rearrange("c p n -> p c n")), 16)
    consts_loaded = s_in.v

    esX = ExitStack()
    xpf = sb("xpf", [128, NCH, TC], F32, esX, side="right")
    esY = ExitStack()
    ysb = sb("ysb", [128, NCHUNK, NCH, 128], BF16, esY, side="right")
    esR = ExitStack()
    rsb = sb("rsb", [128, NCH, TC], BF16, esR, side="right")
    rwb2 = rsb

    for j in range(NCHUNK):
        s_y1.inc(sy.dma_start(out=ysb[:, j, :, :], in_=y1_i[j]), 16)
    y1_loaded = s_y1.v
    s_r.inc(sy.dma_start(out=rsb[:, :, :], in_=r_i.rearrange("(a p) t -> p a t", p=128)), 16)
    s_x.inc(sy.dma_start(out=xpf[:, :, :], in_=xpf_i.rearrange("(a p) t -> p a t", p=128)), 16)

    wbuf = sb("wbuf", [128, 1, NCH, C], BF16)
    wnames2 = ["WoT", "Wv1", "Wk1", "Wq"]
    wsrcs = {"WoT": wo_i, "Wv1": wv1_i, "Wk1": wk1_i, "Wq": wq_i}
    w_loaded = []
    proj_done_pe2 = [0] * 4

    def load_weight2(i, war_pe=0):
        if i >= 1:
            g.wait_ge(s_pe.h, war_pe)
            g.wait_ge(s_w[0].h, s_w[0].v)
        s_w[0].inc(g.dma_start(out=wbuf[:, 0, :, :],
                               in_=wsrcs[wnames2[i]].rearrange("(a p) o -> p a o", p=128)), 16)
        w_loaded.append((s_w[0].h, s_w[0].v))

    load_weight2(0)

    sc.wait_ge(s_in.h, consts_loaded)
    sc.activation(out=e16[:, :], in_=e16[:, :], func=AF.Exp)
    v.wait_ge(s_ac.h, s_ac.v)
    v.tensor_scalar_min(e16[:, :], e16[:, :], E_CLAMP)
    ve_e = s_ve.v
    t.wait_ge(s_ve.h, ve_e)
    t.matmul(psG[0:1, 0:16], e16[:, :], eye16[:, :], start=True, stop=True)
    s_pe.inc(t.last)
    v.wait_ge(s_pe.h, s_pe.v)
    v.tensor_copy(erow16[:, :], psG[0:1, 0:16])
    v.tensor_copy(ap(e16x8, 0, [[8, 16], [1, 8]]), ap(e16, 0, [[1, 16], [0, 8]]))
    v.tensor_mul(e16x8[:, :], e16x8[:, :], sel_lo[:, :])
    v.tensor_copy(ap(u16x8, 0, [[8, 16], [1, 8]]), ap(e16, 0, [[1, 16], [0, 8]]))
    v.tensor_mul(u16x8[:, :], u16x8[:, :], sel_hi[:, :])
    ve_sel = s_ve.v
    t.wait_ge(s_ve.h, ve_sel)
    t.matmul(psG[:, 32:40], half_lo[:, :], e16x8[:, :], start=True, stop=False)
    t.matmul(psG[:, 32:40], half_hi[:, :], u16x8[:, :], start=False, stop=True)
    s_pe.inc(t.last)
    v.wait_ge(s_pe.h, s_pe.v)
    v.tensor_copy(ecol[:, :], psG[:, 32:40])
    v.tensor_scalar_mul(necol[:, :], ecol[:, :], -1.0)
    ve_ecol = s_ve.v
    wtf = sb("wtf", [128, 128])
    for cr in range(8):
        sc.wait_ge(s_ve.h, ve_ecol)
        sc.activation(out=wbt[:, cr, :], in_=kq_iota[:, :], func=AF.Exp,
                      scale=necol[:, cr:cr + 1])
        v.tensor_scalar(wsjt[:, cr, :], j128m[:, :16], necol[:, cr:cr + 1], None, ALU.mult)
        v.tensor_scalar(ws2k[:, cr, :], k2048m[:, :], necol[:, cr:cr + 1], None, ALU.mult)
    v_exps = s_ve.v
    sc.wait_ge(s_ve.h, v_exps)
    for cr in range(8):
        sc.activation(out=wsjt[:, cr, :], in_=wsjt[:, cr, :], func=AF.Exp)
        sc.activation(out=ws2k[:, cr, :], in_=ws2k[:, cr, :], func=AF.Exp)
    ac_tables = s_ac.v

    v.tensor_copy(sbrow[:, :], selb[0:1, :])
    v.wait_ge(s_in.h, consts_loaded)
    t.wait_ge(s_ve.h, s_ve.v)
    t.matmul(psG[:, 64:80], onerow32[:, :], sbrow[:, :], start=True, stop=True)
    s_pe.inc(t.last)
    v.wait_ge(s_pe.h, s_pe.v)
    v.tensor_copy(selb[:, :], psG[:, 64:80])
    v.wait_ge(s_ac.h, ac_tables)
    for cp in range(4):
        v.tensor_scalar(coefp[:, cp, :], ap(ws2k, 0, [[32, 128], [4, 8]]),
                        selb[:, cp * 4:cp * 4 + 1], None, ALU.mult)
        for k in range(1, 4):
            v.scalar_tensor_tensor(coefp[:, cp, :], ap(ws2k, k, [[32, 128], [4, 8]]),
                                   selb[:, cp * 4 + k:cp * 4 + k + 1],
                                   coefp[:, cp, :], ALU.mult, ALU.add)
    v.memset(ist[:, :], 0.0)
    for cp in range(4):
        v.tensor_mul(isttmp[:, :], aggs[:, cp, :],
                     ap(coefp, cp * 8, [[32, 128], [1, 8], [0, 64]]))
        v.tensor_add(ist[:, :], ist[:, :], isttmp[:, :])
    v.tensor_copy(istb[:, :], ist[:, :])
    ve_ist = s_ve.v

    v.wait_ge(s_r.h, 16)
    for cr in range(8):
        v.tensor_mul(ap(rwb2, cr * TC, [[NCH * TC, 128], [128, 16], [1, 128]]),
                     ap(rsb, cr * TC, [[NCH * TC, 128], [128, 16], [1, 128]]),
                     ap(wbt, cr * 128, [[8 * 128, 128], [0, 16], [1, 128]]))
        v.tensor_mul(ap(rwb2, cr * TC, [[NCH * TC, 128], [128, 16], [1, 128]]),
                     ap(rwb2, cr * TC, [[NCH * TC, 128], [128, 16], [1, 128]]),
                     ap(wsjt, cr * 16, [[8 * 16, 128], [1, 16], [0, 128]]))
    ve_rwb2 = s_ve.v

    t.wait_ge(s_ve.h, ve_rwb2)
    t.wait_ge(s_y1.h, y1_loaded)
    y2_marks = {}
    for hh in range(16):
        half = (hh % 2) * 64
        pr = hh // 2
        bank = 2 + (hh % 2)
        for ti in range(NT):
            key = (hh % 2, (hh // 2) * NT + ti)
            prev = y2_marks.get((hh % 2, (hh // 2) * NT + ti - 1))
            if prev is not None:
                t.wait_ge(s_ve.h, prev)
            t.matmul(psA[half:half + 64, bank, :],
                     istb[half:half + 64, pr * 64:pr * 64 + 64],
                     rwb2[half:half + 64, pr, ti * 512:(ti + 1) * 512],
                     start=True, stop=True, skip_group_check=True)
            s_pe.inc(t.last)
            v.wait_ge(s_pe.h, s_pe.v)
            off = half * (NCHUNK * NCH * 128) + (ti * 4) * (NCH * 128) + pr * 128
            vw = ap(ysb, off, [[NCHUNK * NCH * 128, 64], [NCH * 128, 4], [1, 128]])
            v.tensor_add(vw, vw, psA[half:half + 64, bank, :])
            y2_marks[key] = s_ve.v
    ve_y2 = s_ve.v
    esR.close()

    t.wait_ge(*w_loaded[0])
    t.wait_ge(s_ve.h, ve_y2)
    t.wait_ge(s_x.h, 16)
    op_marks = []
    add_marks = []
    for mi in range(8):
        for ti in range(NT):
            idx = mi * NT + ti
            pa = psA[:, idx % 2, :]
            if idx >= 2:
                t.wait_ge(*op_marks[idx - 2])
            for k in range(NCH):
                t.matmul(pa, wbuf[:, 0, k, mi * 128:(mi + 1) * 128],
                         ap(ysb, (ti * 4) * (NCH * 128) + k * 128,
                            [[NCHUNK * NCH * 128, 128], [NCH * 128, 4], [1, 128]]),
                         start=(k == 0), stop=(k == NCH - 1))
            s_pe.inc(t.last)
            sc.wait_ge(s_pe.h, s_pe.v)
            if idx >= 2:
                sc.wait_ge(s_ve.h, add_marks[idx - 2])
            sc.activation(out=ostage[:, idx % 2, :], in_=pa, func=AF.Identity,
                          bias=bo_v[:, mi:mi + 1])
            op_marks.append((s_ac.h, s_ac.v))
            v.wait_ge(s_ac.h, s_ac.v)
            v.tensor_add(xpf[:, mi, ti * 512:(ti + 1) * 512],
                         xpf[:, mi, ti * 512:(ti + 1) * 512], ostage[:, idx % 2, :])
            add_marks.append(s_ve.v)
    ve_x1 = s_ve.v
    proj_done_pe2[0] = s_pe.v
    load_weight2(1, war_pe=s_pe.v)
    esY.close()

    xn2 = sb("xn2", [128, NCH, TC], BF16)
    esQ = ExitStack()
    xsq2 = sb("xsq2", [128, NCH, TC], BF16, esQ, side="right")
    v.wait_ge(s_ve.h, ve_x1)
    for ch in range(NCH):
        v.tensor_copy(xn2[:, ch, :], xpf[:, ch, :])
        sc.wait_ge(s_ve.h, s_ve.v)
        sc.activation(out=xsq2[:, ch, :], in_=xpf[:, ch, :], func=AF.Square)
    ve_xbf2 = s_ve.v
    ac_xsq2 = s_ac.v
    t.wait_ge(s_ve.h, ve_xbf2)
    t.wait_ge(s_ac.h, ac_xsq2)
    for ti in range(NT):
        a, b2 = ti * 512, (ti + 1) * 512
        w = 512
        for ch in range(NCH):
            t.matmul(psA[0:1, 0, 0:w], onec[:, :], xn2[:, ch, a:b2], start=(ch == 0), stop=(ch == NCH - 1))
        for ch in range(NCH):
            t.matmul(psA[0:1, 1, 0:w], onec[:, :], xsq2[:, ch, a:b2], start=(ch == 0), stop=(ch == NCH - 1))
        s_pe.inc(t.last)
        v.wait_ge(s_pe.h, s_pe.v)
        v.tensor_scalar_mul(srow[0:1, 0, 0:w], psA[0:1, 0, 0:w], 1.0 / C)
        v.tensor_scalar_mul(srow[0:1, 1, 0:w], psA[0:1, 1, 0:w], 1.0 / C)
        sc.wait_ge(s_ve.h, s_ve.v)
        sc.activation(out=srow[0:1, 3, 0:w], in_=srow[0:1, 0, 0:w], func=AF.Square)
        v.wait_ge(s_ac.h, s_ac.v)
        v.tensor_sub(srow[0:1, 1, 0:w], srow[0:1, 1, 0:w], srow[0:1, 3, 0:w])
        v.tensor_scalar_add(srow[0:1, 1, 0:w], srow[0:1, 1, 0:w], EPS_LN)
        sc.wait_ge(s_ve.h, s_ve.v)
        sc.activation(out=srow[0:1, 1, 0:w], in_=srow[0:1, 1, 0:w], func=AF.Ln)
        v.wait_ge(s_ac.h, s_ac.v)
        v.tensor_scalar_mul(srow[0:1, 1, 0:w], srow[0:1, 1, 0:w], -0.5)
        sc.wait_ge(s_ve.h, s_ve.v)
        sc.activation(out=srow[0:1, 1, 0:w], in_=srow[0:1, 1, 0:w], func=AF.Exp)
        v.wait_ge(s_ac.h, s_ac.v)
        v.tensor_mul(srow[0:1, 2, 0:w], srow[0:1, 0, 0:w], srow[0:1, 1, 0:w])
        t.wait_ge(s_ve.h, s_ve.v)
        t.matmul(psA[:, 2, 0:w], onerow32[:, :], srow[0:1, 1, 0:w], start=True, stop=True)
        t.matmul(psA[:, 3, 0:w], onerow32[:, :], srow[0:1, 2, 0:w], start=True, stop=True)
        s_pe.inc(t.last)
        v.wait_ge(s_pe.h, s_pe.v)
        for ch in range(NCH):
            v.tensor_mul(xn2[:, ch, a:b2], xn2[:, ch, a:b2], psA[:, 2, 0:w])
            v.tensor_sub(xn2[:, ch, a:b2], xn2[:, ch, a:b2], psA[:, 3, 0:w])
        t.wait_ge(s_ve.h, s_ve.v)
    ve_xn2 = s_ve.v
    esQ.close()

    sy.wait_ge(s_ve.h, ve_xn2)
    s_st.inc(sy.dma_start(out=x1_o.rearrange("(a p) t -> p a t", p=128), in_=xpf[:, :, :]), 16)
    s_st.inc(sy.dma_start(out=xn2_o.rearrange("(a p) t -> p a t", p=128), in_=xn2[:, :, :]), 16)
    x1_stored = s_st.v
    esX.close()

    esKV = ExitStack()
    vtk2 = sb("vtk2", [128, NCHUNK, C], BF16, esKV, side="right")
    ek = sb("ek", [128, NCHUNK, C], BF16, esKV, side="right")
    v.wait_ge(s_st.h, x1_stored)
    sc.wait_ge(s_st.h, x1_stored)
    g.wait_ge(s_st.h, x1_stored)

    def emit_projB(wslot, dst, act_exp, war):
        t.wait_ge(*w_loaded[wslot])
        t.wait_ge(s_ve.h, ve_xn2)
        marks = []
        for tj in range(NCHUNK):
            for ni in range(2):
                j2 = tj * 2 + ni
                pa = psA[:, j2 % 2, :]
                wv = marks[j2 - 2] if j2 >= 2 else war[j2]
                if wv is not None:
                    t.wait_ge(wv[0], wv[1])
                for k in range(NCH):
                    t.matmul(pa, xn2[:, k, tj * 128:(tj + 1) * 128],
                             wbuf[:, 0, k, ni * 512:(ni + 1) * 512],
                             start=(k == 0), stop=(k == NCH - 1))
                s_pe.inc(t.last)
                if act_exp:
                    sc.wait_ge(s_pe.h, s_pe.v)
                    sc.activation(out=dst[:, tj, ni * 512:(ni + 1) * 512], in_=pa, func=AF.Exp)
                    marks.append((s_ac.h, s_ac.v))
                else:
                    v.wait_ge(s_pe.h, s_pe.v)
                    v.tensor_copy(dst[:, tj, ni * 512:(ni + 1) * 512], pa)
                    marks.append((s_ve.h, s_ve.v))
        return marks, s_pe.v

    mv, pe_v1 = emit_projB(1, vtk2, False, [op_marks[-2], op_marks[-1]])
    proj_done_pe2[1] = pe_v1
    load_weight2(2, war_pe=pe_v1)
    mk, pe_k1 = emit_projB(2, ek, True, [mv[-2], mv[-1]])
    proj_done_pe2[2] = pe_k1

    t.wait_ge(mk[-1][0], mk[-1][1])
    t.wait_ge(mk[-2][0], mk[-2][1])
    for half in range(2):
        for tj in range(NCHUNK):
            t.matmul(psY[0:1, 0, 0:512], onec[:, :], ek[:, tj, half * 512:(half + 1) * 512],
                     start=(tj == 0), stop=(tj == NCHUNK - 1))
        s_pe.inc(t.last)
        v.wait_ge(s_pe.h, s_pe.v)
        v.tensor_copy(srow[0:1, half, 0:512], psY[0:1, 0, 0:512])
        t.wait_ge(s_ve.h, s_ve.v)
    sy.wait_ge(s_ve.h, s_ve.v)
    s_sc2.inc(sy.dma_start(out=s_o, in_=srow[0:1, 0:2, :].rearrange("p a t -> p (a t)")), 16)
    for bank in range(2):
        for hh in range(bank * 8, bank * 8 + 8):
            off = (hh % 8) * 64
            for tj in range(NCHUNK):
                t.matmul(psY[0:64, bank, off:off + 64],
                         ek[:, tj, hh * 64:hh * 64 + 64],
                         vtk2[:, tj, hh * 64:hh * 64 + 64],
                         start=(tj == 0), stop=(tj == NCHUNK - 1), skip_group_check=True)
        s_pe.inc(t.last)
        v.wait_ge(s_pe.h, s_pe.v)
        v.tensor_copy(ap(csb, bank * 8 * 64, [[16 * 64, 64], [1, 512]]), psY[0:64, bank, :])
    ve_csb = s_ve.v
    sy.wait_ge(s_ve.h, s_ve.v)
    s_sc2.inc(sy.dma_start(out=c_o, in_=csb[:, :, :].rearrange("p a t -> p (a t)")), 16)
    pe_sc = s_pe.v
    load_weight2(3, war_pe=pe_sc)
    esKV.close()

    qe = sb("qe", [128, NCH, TC], BF16, side="right")
    v.wait_ge(s_pe.h, pe_sc)
    sc.wait_ge(s_pe.h, pe_sc)
    t.wait_ge(*w_loaded[3])
    q_marks = []
    for mi in range(8):
        for ti in range(NT):
            idx = mi * NT + ti
            pa = psA[:, idx % 2, :]
            wv = q_marks[idx - 2] if idx >= 2 else (mk[-2:] + [None, None])[idx]
            if wv is not None:
                t.wait_ge(wv[0], wv[1])
            for k in range(NCH):
                t.matmul(pa, wbuf[:, 0, k, mi * 128:(mi + 1) * 128],
                         xn2[:, k, ti * 512:(ti + 1) * 512],
                         start=(k == 0), stop=(k == NCH - 1))
            s_pe.inc(t.last)
            sc.wait_ge(s_pe.h, s_pe.v)
            sc.activation(out=qe[:, mi, ti * 512:(ti + 1) * 512], in_=pa, func=AF.Exp)
            q_marks.append((s_ac.h, s_ac.v))
    pe_q = s_pe.v
    sc.wait_ge(s_ac.h, q_marks[-1][1])
    t.wait_ge(s_ac.h, q_marks[-1][1])
    t.wait_ge(s_ve.h, ve_csb)
    for ti in range(NT):
        for cr in range(8):
            t.matmul(psY[0:16, 0, 0:512], hsel[:, cr, :], qe[:, cr, ti * 512:(ti + 1) * 512],
                     start=(cr == 0), stop=(cr == 7))
        s_pe.inc(t.last)
        v.wait_ge(s_pe.h, s_pe.v)
        v.reciprocal(qrow[:, :], psY[0:16, 0, 0:512])
        t.wait_ge(s_ve.h, s_ve.v)
        for cr in range(8):
            t.matmul(psY[:, 1, 0:512], hselT[:, cr, :], qrow[:, :], start=True, stop=True)
            s_pe.inc(t.last)
            v.wait_ge(s_pe.h, s_pe.v)
            v.tensor_mul(qe[:, cr, ti * 512:(ti + 1) * 512],
                         qe[:, cr, ti * 512:(ti + 1) * 512], psY[:, 1, 0:512])
            t.wait_ge(s_ve.h, s_ve.v)
    ve_qn = s_ve.v
    sy.wait_ge(s_ve.h, ve_qn)
    s_st.inc(sy.dma_start(out=qn_o.rearrange("(a p) t -> p a t", p=128), in_=qe[:, :, :]), 16)
    qn_stored = s_st.v

    xr2f = sb("xr2f_sb", [128, NCH, TC], BF16, side="right")
    v.wait_ge(s_sc2.h, s_sc2.v)
    sc.wait_ge(s_sc2.h, s_sc2.v)
    for ch in range(NCH):
        sc.activation(out=xr2f[:, ch, :], in_=xn2[:, ch, :], func=AF.Square)
    ac_sq3 = s_ac.v
    t.wait_ge(s_ac.h, ac_sq3)
    for ti in range(NT):
        a, b2 = ti * 512, (ti + 1) * 512
        for ch in range(NCH):
            t.matmul(psA[0:1, 2, 0:512], onec[:, :], xr2f[:, ch, a:b2], start=(ch == 0), stop=(ch == NCH - 1))
        s_pe.inc(t.last)
        v.wait_ge(s_pe.h, s_pe.v)
        v.tensor_scalar_mul(srow[0:1, 0, 0:512], psA[0:1, 2, 0:512], 1.0 / C)
        v.tensor_scalar_add(srow[0:1, 0, 0:512], srow[0:1, 0, 0:512], EPS_RMS)
        sc.wait_ge(s_ve.h, s_ve.v)
        sc.activation(out=srow[0:1, 0, 0:512], in_=srow[0:1, 0, 0:512], func=AF.Ln)
        v.wait_ge(s_ac.h, s_ac.v)
        v.tensor_scalar_mul(srow[0:1, 0, 0:512], srow[0:1, 0, 0:512], -0.5)
        sc.wait_ge(s_ve.h, s_ve.v)
        sc.activation(out=srow[0:1, 0, 0:512], in_=srow[0:1, 0, 0:512], func=AF.Exp)
        t.wait_ge(s_ac.h, s_ac.v)
        t.matmul(psA[:, 3, 0:512], onerow32[:, :], srow[0:1, 0, 0:512], start=True, stop=True)
        s_pe.inc(t.last)
        v.wait_ge(s_pe.h, s_pe.v)
        for ch in range(NCH):
            v.tensor_mul(xr2f[:, ch, a:b2], xn2[:, ch, a:b2], psA[:, 3, 0:512])
            v.tensor_scalar(xr2f[:, ch, a:b2], xr2f[:, ch, a:b2], rmsw_v[:, ch:ch + 1], None, ALU.mult)
        t.wait_ge(s_ve.h, s_ve.v)
    ve_xr2 = s_ve.v
    sy.wait_ge(s_ve.h, ve_xr2)
    s_st.inc(sy.dma_start(out=xr2f_o.rearrange("(a p) t -> p a t", p=128), in_=xr2f[:, :, :]), 16)
    s_xf.inc(sy.dma_start(out=xrfd.rearrange("(a p) t -> p a t", p=128), in_=xr2f[:, :, :]), 16)

    s_t2 = [Ctr(es.enter_context(nc.semaphore("s_t20"))), Ctr(es.enter_context(nc.semaphore("s_t21")))]
    sy.wait_ge(s_xf.h, s_xf.v)
    for tj in range(NCHUNK):
        par = tj % 2
        if tj >= 2:
            sy.wait_ge(s_t2[par].h, 16 * (tj // 2))
        s_tr.inc(sy.dma_start_transpose(xrt[:, par, :],
                                        ap(xrfd.tensor, tj * 128, [[TC, 1024], [1, 128]])), 16)
        sy.wait_ge(s_tr.h, s_tr.v)
        s_t2[par].inc(sy.dma_start(out=xr2t_o[tj * 128:(tj + 1) * 128, :], in_=xrt[:, par, :]), 16)

    t.wait_ge(s_ve.h, ve_xr2)
    lg_marks = []
    for tj in range(NCHUNK):
        pa2 = psG[:, 0:64]
        if tj >= 1:
            t.wait_ge(s_ve.h, lg_marks[tj - 1])
        for k in range(NCH):
            t.matmul(pa2, xr2f[:, k, tj * 128:(tj + 1) * 128], gwb[:, k, :],
                     start=(k == 0), stop=(k == NCH - 1))
        s_pe.inc(t.last)
        v.wait_ge(s_pe.h, s_pe.v)
        v.tensor_copy(lgt[:, tj, :], pa2)
        lg_marks.append(s_ve.v)
    ve_lgt = s_ve.v
    v.memset(carry[:, :], 0.0)
    v.memset(meta[:, :, :], 0.0)
    for tj in range(NCHUNK):
        v.max(m8[:, :], lgt[:, tj, :])
        v.max_index(i8[:, :], m8[:, :], lgt[:, tj, :])
        v.tensor_copy(i8f[:, 0:2], i8[:, 0:2])
        v.tensor_sub(wrow[:, 0:1], m8[:, 0:1], m8[:, 1:2])
        sc.wait_ge(s_ve.h, s_ve.v)
        sc.activation(out=wrow[:, 1:2], in_=wrow[:, 0:1], func=AF.Sigmoid)
        v.wait_ge(s_ac.h, s_ac.v)
        v.tensor_scalar(wrow[:, 2:3], wrow[:, 1:2], -1.0, 1.0, ALU.mult, ALU.add)
        v.tensor_scalar(m1[:, :], iota64[:, :], i8f[:, 0:1], None, ALU.is_equal)
        v.tensor_scalar(m2[:, :], iota64[:, :], i8f[:, 1:2], None, ALU.is_equal)
        v.tensor_add(mb[:, :], m1[:, :], m2[:, :])
        v.tensor_copy(carryb[:, :], carry[:, :])
        ve_oh = s_ve.v
        t.wait_ge(s_ve.h, ve_oh)
        t.matmul(psG[:, 64:128], tri[:, :], mb[:, :], start=True, stop=False)
        t.matmul(psG[:, 64:128], onerow[:, :], carryb[0:1, :], start=False, stop=True)
        s_pe.inc(t.last)
        pe_pfx = s_pe.v
        t.matmul(psS[0:1, 0:64], onec[:, :], mb[:, :], start=True, stop=True)
        s_pe.inc(t.last)
        v.wait_ge(s_pe.h, s_pe.v)
        v.tensor_copy(pfx[:, :], psG[:, 64:128])
        v.tensor_add(carry[:, :], carry[:, :], psS[0:1, 0:64])
        v.tensor_mul(m1[:, :], m1[:, :], pfx[:, :])
        v.reduce_sum(rk[:, 0:1], m1[:, :], axis=mybir.AxisListType.X)
        v.tensor_add(pfx[:, :], pfx[:, :], m1[:, :])
        v.tensor_mul(m2[:, :], m2[:, :], pfx[:, :])
        v.reduce_sum(rk[:, 1:2], m2[:, :], axis=mybir.AxisListType.X)
        v.tensor_copy(meta[:, tj, 0:2], i8f[:, 0:2])
        v.tensor_copy(meta[:, tj, 2:3], wrow[:, 1:2])
        v.tensor_copy(meta[:, tj, 3:4], wrow[:, 2:3])
        v.tensor_copy(meta[:, tj, 4:6], rk[:, 0:2])
        t.wait_ge(s_ve.h, s_ve.v)
    ve_meta = s_ve.v
    sy.wait_ge(s_ve.h, ve_meta)
    s_st.inc(sy.dma_start(out=meta_o, in_=meta[:, :, :]), 16)
    s_st.inc(sy.dma_start(out=cnt_o, in_=carry[:, :]), 16)

    sy.wait_ge(s_st.h, s_st.v)
    sy.wait_ge(s_sc2.h, s_sc2.v)
    sy.wait_ge(s_t2[0].h, s_t2[0].v)
    sy.wait_ge(s_t2[1].h, s_t2[1].v)
    es.close()
    return nc


def _vec_pc(vec):
    return np.ascontiguousarray(np.asarray(vec, np.float32).reshape(8, 128).T)


def launch1_inputs(inp):
    x = np.asarray(inp["L0_attn_x"], np.float32)
    w_l1 = {
        "fWrT": np.ascontiguousarray(np.asarray(inp["L0_ffn_W_r"], np.float32).T),
        "fWkT": np.ascontiguousarray(np.asarray(inp["L0_ffn_W_k"], np.float32).T),
        "fWvT": np.ascontiguousarray(np.asarray(inp["L0_ffn_W_v"], np.float32).T),
        "WrT": np.ascontiguousarray(np.asarray(inp["L0_attn_Wr"], np.float32).T),
        "WkT": np.ascontiguousarray(np.asarray(inp["L0_attn_Wk"], np.float32).T),
        "WvT": np.ascontiguousarray(np.asarray(inp["L0_attn_Wv"], np.float32).T),
    }
    vecs = {
        "br": _vec_pc(inp["L0_attn_br"]),
        "bk": _vec_pc(inp["L0_attn_bk"]),
        "tmk": _vec_pc(inp["L0_attn_time_maa_k"]),
        "tmv": _vec_pc(inp["L0_attn_time_maa_v"]),
        "tmr": _vec_pc(inp["L0_attn_time_maa_r"]),
        "mur": _vec_pc(inp["L0_ffn_mu_r"]),
        "muk": _vec_pc(inp["L0_ffn_mu_k"]),
    }
    consts = {("c_" + k): v for k, v in CONSTS.items()}
    in_maps = []
    for c in range(8):
        b, s = divmod(c, 4)
        t0 = s * TC
        xs = np.zeros((C, TQ), np.float32)
        xs[:, 1:] = x[b, t0:t0 + TC, :].T
        if t0 > 0:
            xs[:, 0] = x[b, t0 - 1, :]
        m = {"xT": np.ascontiguousarray(xs),
             "bv_row": np.asarray(inp["L0_attn_bv"], np.float32)[None, :],
             "td16": np.asarray(inp["L0_attn_time_decay"], np.float32)[:, None],
             "tf16": np.asarray(inp["L0_attn_time_faaaa"], np.float32)[:, None]}
        m.update(w_l1)
        m.update(vecs)
        m.update(consts)
        in_maps.append(m)
    return in_maps


def _l2_consts():
    c = {}
    c["c_j128"] = np.broadcast_to((128.0 * np.arange(16, dtype=np.float32))[None, :], (128, 16)).copy()
    c["c_k2048"] = np.broadcast_to((2048.0 * np.arange(4, dtype=np.float32))[None, :], (128, 4)).copy()
    tri = (np.arange(128)[:, None] < np.arange(128)[None, :]).astype(np.float32)
    c["c_tri"] = tri
    c["c_iota64"] = np.broadcast_to(np.arange(64, dtype=np.float32)[None, :], (128, 64)).copy()
    hsel = np.zeros((128, 8, 16), np.float32)
    for p in range(128):
        for cr in range(8):
            hsel[p, cr, 2 * cr + (p >= 64)] = 1.0
    c["c_hsel"] = hsel
    c["c_hselT"] = np.ascontiguousarray(hsel.transpose(2, 1, 0))
    return c


L2CONSTS = _l2_consts()


def launch2_inputs(inp, l1res):
    consts = {("c_" + k): CONSTS[k] for k in ["ones_col", "ones_row", "iota_q", "eye16",
                                              "sel_lo", "sel_hi", "half_lo", "half_hi"]}
    consts.update(L2CONSTS)
    w2 = {
        "WoT": np.ascontiguousarray(np.asarray(inp["L0_attn_Wo"], np.float32).T),
        "Wq": np.ascontiguousarray(np.asarray(inp["L1_attn_Wq"], np.float32)),
        "Wk1": np.ascontiguousarray(np.asarray(inp["L1_attn_Wk"], np.float32)),
        "Wv1": np.ascontiguousarray(np.asarray(inp["L1_attn_Wv"], np.float32)),
        "gwT": np.ascontiguousarray(np.asarray(inp["L1_ffn_gate_w"], np.float32).T),
    }
    bo = _vec_pc(inp["L0_attn_bo"])
    rmsw = _vec_pc(inp["L1_ffn_rms_w"])
    in_maps = []
    for c in range(8):
        b, s = divmod(c, 4)
        aggp = np.stack([l1res[b * 4 + cc]["agg"] for cc in range(4)])
        selmat = np.zeros((4, 4), np.float32)
        for cp in range(s):
            selmat[cp, s - 1 - cp] = 1.0
        m = {"xpfT": l1res[c]["xpf"], "y1d": l1res[c]["y1T"], "rT": l1res[c]["rT"],
             "aggp": np.ascontiguousarray(aggp), "selmat": selmat.reshape(1, 16),
             "bo": bo, "rmsw": rmsw,
             "td16": np.asarray(inp["L0_attn_time_decay"], np.float32)[:, None]}
        m.update(w2)
        m.update(consts)
        in_maps.append(m)
    return in_maps


def launch2b_inputs(l2res):
    cnts = np.stack([np.asarray(l2res[cc]["cnt"], np.float32)[0] for cc in range(8)])
    base_tok = np.add.outer(np.arange(128, dtype=np.float32),
                            128.0 * np.arange(16, dtype=np.float32))
    in_maps = []
    for c in range(8):
        m = {"meta": np.asarray(l2res[c]["meta"], np.float32),
             "cnts": cnts,
             "pmask": (np.arange(8) < c).astype(np.float32)[:, None],
             "tokid": np.ascontiguousarray(base_tok + c * TC),
             "c_iota64": L2CONSTS["c_iota64"],
             "c_ones_row": CONSTS["ones_row"]}
        in_maps.append(m)
    return in_maps


def launch3_inputs(inp, l2res, l2bres):
    xall = np.concatenate([np.asarray(l2res[c]["xr2t"], BF) for c in range(8)], axis=0)
    w13 = np.asarray(inp["L1_ffn_w13"], np.float32)
    w2 = np.asarray(inp["L1_ffn_w2"], np.float32)
    in_maps = []
    for c in range(8):
        arr = np.stack([np.asarray(l2bres[cc]["invt"], np.float32)[c * 8192:(c + 1) * 8192]
                        for cc in range(8)])
        invp = np.ascontiguousarray(arr.reshape(8, 64, 128, 2).transpose(2, 0, 3, 1))
        m = {"invp": invp, "xall": xall, "c_eye": CONSTS["diagmask"],
             "w13s": np.ascontiguousarray(w13[c * 8:(c + 1) * 8]),
             "w2s": np.ascontiguousarray(w2[c * 8:(c + 1) * 8])}
        in_maps.append(m)
    return in_maps


def launch4_inputs(inp, l2res, l2bres, l3res):
    ytab = np.concatenate([np.asarray(l3res[c]["ytab"], BF) for c in range(8)]
                          + [np.zeros((1, C), BF)], axis=0)
    shared_w = {
        "Wo1": np.asarray(inp["L1_attn_Wo"], np.float32),
        "bo1pc": _vec_pc(inp["L1_attn_bo"]),
        "swg": np.asarray(inp["L1_ffn_sw_gate"], np.float32),
        "swu": np.asarray(inp["L1_ffn_sw_up"], np.float32),
        "swd": np.asarray(inp["L1_ffn_sw_down"], np.float32),
    }
    in_maps = []
    for c in range(8):
        b = c // 4
        sp4 = np.stack([np.asarray(l2res[b * 4 + cc]["sloc"], np.float32)[0]
                        for cc in range(4)])[None]
        cp4 = np.stack([np.asarray(l2res[b * 4 + cc]["cloc"], np.float32)
                        for cc in range(4)], axis=1)
        m = {"x1T": np.asarray(l2res[c]["x1T"], np.float32),
             "qnT": np.asarray(l2res[c]["qnT"], BF),
             "xn2f": np.asarray(l2res[c]["xn2f"], BF),
             "xr2f": np.asarray(l2res[c]["xr2f"], BF),
             "sp4": np.ascontiguousarray(sp4), "cp4": np.ascontiguousarray(cp4),
             "meta": np.asarray(l2res[c]["meta"], np.float32),
             "dtok": np.asarray(l2bres[c]["dtok"], np.float32),
             "ytab": ytab}
        m.update(shared_w)
        in_maps.append(m)
    return in_maps


def kernel(**inp):
    res1 = run_bass_kernel_spmd(build_launch1(), launch1_inputs(inp),
                                core_ids=list(range(8))).results
    res2 = run_bass_kernel_spmd(build_launch2(), launch2_inputs(inp, res1),
                                core_ids=list(range(8))).results
    res2b = run_bass_kernel_spmd(build_launch2b(), launch2b_inputs(res2),
                                 core_ids=list(range(8))).results
    res3 = run_bass_kernel_spmd(build_launch3(), launch3_inputs(inp, res2, res2b),
                                core_ids=list(range(8))).results
    res4 = run_bass_kernel_spmd(build_launch4(), launch4_inputs(inp, res2, res2b, res3),
                                core_ids=list(range(8))).results
    out = np.empty((B, T, C), np.float32)
    for c in range(8):
        b, s = divmod(c, 4)
        out[b, s * TC:(s + 1) * TC, :] = np.asarray(res4[c]["outT"], np.float32).T
    return out

I32 = mybir.dt.int32


def build_launch2b():
    nc = bass.Bass("TRN2", target_bir_lowering=False)
    es = ExitStack()

    def sb(name, shape, dt=F32):
        return es.enter_context(nc.sbuf_tensor(name, shape, dt))

    meta_i = nc.dram_tensor("meta", [128, 16, 8], F32, kind="ExternalInput").ap()
    cnts_i = nc.dram_tensor("cnts", [8, 64], F32, kind="ExternalInput").ap()
    pmask_i = nc.dram_tensor("pmask", [8, 1], F32, kind="ExternalInput").ap()
    tokid_i = nc.dram_tensor("tokid", [128, 16], F32, kind="ExternalInput").ap()
    iota64_i = nc.dram_tensor("c_iota64", [128, 64], F32, kind="ExternalInput").ap()
    onesr_i = nc.dram_tensor("c_ones_row", [1, 128], F32, kind="ExternalInput").ap()

    invt_o = nc.dram_tensor("invt", [65536, 2], F32, kind="ExternalOutput").ap()
    dtok_o = nc.dram_tensor("dtok", [128, 16, 2], F32, kind="ExternalOutput").ap()

    meta = sb("meta_sb", [128, 16, 8])
    cnts = sb("cnts_sb", [8, 64])
    pmask = sb("pmask_sb", [8, 1])
    tokid = sb("tokid_sb", [128, 16])
    iota64 = sb("iota64", [128, 64])
    onerow32 = sb("onerow32", [1, 128])
    prow = sb("prow", [1, 64])
    pb = sb("pb", [128, 64])
    m1 = sb("m1", [128, 64])
    selv = sb("selv", [128, 16, 2])
    gr = sb("gr", [128, 16, 2])
    flg = sb("flg", [128, 16, 2])
    nfl = sb("nfl", [128, 16, 2])
    dcol = sb("dcol", [128, 16, 2])
    di = sb("di", [128, 16, 2], I32)
    pay = sb("pay", [128, 16, 2, 2])
    zt = sb("zt", [128, 1024])

    psG = es.enter_context(nc.psum_tensor("psG", [128, 512], F32))
    s_in = Ctr(es.enter_context(nc.semaphore("s_in")))
    s_pe = Ctr(es.enter_context(nc.semaphore("s_pe")))
    s_ve = Ctr(es.enter_context(nc.semaphore("s_ve")))
    s_z = Ctr(es.enter_context(nc.semaphore("s_z")))
    s_st = Ctr(es.enter_context(nc.semaphore("s_st")))
    s_scp = [Ctr(es.enter_context(nc.semaphore("s_sc0"))),
             Ctr(es.enter_context(nc.semaphore("s_sc1")))]

    g = nc.gpsimd
    t = Rec(nc.tensor)
    v = Chain(nc.vector, s_ve)
    sy = nc.sync

    s_in.inc(g.dma_start(out=meta[:, :, :], in_=meta_i), 16)
    s_in.inc(g.dma_start(out=cnts[:, :], in_=cnts_i), 16)
    s_in.inc(g.dma_start(out=pmask[:, :], in_=pmask_i), 16)
    s_in.inc(g.dma_start(out=tokid[:, :], in_=tokid_i), 16)
    s_in.inc(g.dma_start(out=iota64[:, :], in_=iota64_i), 16)
    s_in.inc(g.dma_start(out=onerow32[:, :], in_=onesr_i), 16)
    loaded = s_in.v

    v.memset(zt[:, :], 0.0)
    sy.wait_ge(s_ve.h, s_ve.v)
    s_z.inc(sy.dma_start(out=ap(invt_o.tensor, 0, [[1024, 128], [1, 1024]]), in_=zt[:, :]), 16)

    t.wait_ge(s_in.h, loaded)
    t.matmul(psG[0:1, 0:64], pmask[:, :], cnts[:, :], start=True, stop=True)
    s_pe.inc(t.last)
    v.wait_ge(s_pe.h, s_pe.v)
    v.tensor_copy(prow[:, :], psG[0:1, 0:64])
    t.wait_ge(s_ve.h, s_ve.v)
    t.matmul(psG[:, 64:128], onerow32[:, :], prow[:, :], start=True, stop=True)
    s_pe.inc(t.last)
    v.wait_ge(s_pe.h, s_pe.v)
    v.tensor_copy(pb[:, :], psG[:, 64:128])
    g.wait_ge(s_z.h, 16)

    for tj in range(NCHUNK):
        for sl in range(2):
            v.scalar_tensor_tensor(m1[:, :], iota64[:, :], meta[:, tj, sl:sl + 1],
                                   pb[:, :], ALU.is_equal, ALU.mult)
            v.reduce_sum(selv[:, tj, sl:sl + 1], m1[:, :], axis=mybir.AxisListType.X)
    v.tensor_add(gr[:, :, :], selv[:, :, :], meta[:, :, 4:6])
    v.tensor_scalar(dcol[:, :, :], meta[:, :, 0:2], 1024.0, None, ALU.mult)
    v.tensor_add(dcol[:, :, :], dcol[:, :, :], gr[:, :, :])
    v.tensor_scalar(flg[:, :, :], gr[:, :, :], 1024.0, None, ALU.is_ge)
    v.tensor_scalar(nfl[:, :, :], flg[:, :, :], -1.0, 1.0, ALU.mult, ALU.add)
    v.tensor_mul(dcol[:, :, :], dcol[:, :, :], nfl[:, :, :])
    v.scalar_tensor_tensor(dcol[:, :, :], flg[:, :, :], 65536.0, dcol[:, :, :],
                           ALU.mult, ALU.add)
    v.tensor_copy(di[:, :, :], dcol[:, :, :])
    for sl in range(2):
        v.tensor_copy(pay[:, :, sl, 0], tokid[:, :])
        v.tensor_copy(pay[:, :, sl, 1], meta[:, :, 2 + sl])
    ve_all = s_ve.v
    sy.wait_ge(s_ve.h, ve_all)
    s_st.inc(sy.dma_start(out=dtok_o, in_=dcol[:, :, :]), 16)
    g.wait_ge(s_ve.h, ve_all)
    for tj in range(NCHUNK):
        for sl in range(2):
            s_scp[tj % 2].inc(g.indirect_dma_start(
                invt_o[:, :], bass.IndirectOffsetOnAxis(di[:, tj, sl:sl + 1], 0),
                pay[:, tj, sl, :], None,
                bounds_check=65535, oob_is_err=False), 16)
    sy.wait_ge(s_scp[0].h, s_scp[0].v)
    sy.wait_ge(s_scp[1].h, s_scp[1].v)
    sy.wait_ge(s_z.h, 16)
    sy.wait_ge(s_st.h, s_st.v)
    es.close()
    return nc


def build_launch3():
    nc = bass.Bass("TRN2", target_bir_lowering=False)
    es = ExitStack()

    def sb(name, shape, dt=F32, side=None):
        return es.enter_context(nc.sbuf_tensor(name, shape, dt, side=side))

    invp_i = nc.dram_tensor("invp", [128, 8, 2, 64], F32, kind="ExternalInput").ap()
    xall_i = nc.dram_tensor("xall", [B * T, C], BF16, kind="ExternalInput").ap()
    w13_i = nc.dram_tensor("w13s", [8, C, C], F32, kind="ExternalInput").ap()
    w2_i = nc.dram_tensor("w2s", [8, 512, C], F32, kind="ExternalInput").ap()
    eye_i = nc.dram_tensor("c_eye", [128, 128], F32, kind="ExternalInput").ap()

    ytab_o = nc.dram_tensor("ytab", [8192, C], BF16, kind="ExternalOutput").ap()

    invsb = sb("invsb", [128, 8, 2, 64])
    invs = sb("invs", [128, 2, 64])
    toki = sb("toki", [128, 64], I32)
    wcol = sb("wcol", [128, 64])
    eyeb = sb("eyeb", [128, 128], BF16)
    w13b = sb("w13b", [128, 2, NCH, C], BF16)
    w2b = sb("w2b", [128, 2, 4, C], BF16)
    xbt = sb("xbt", [128, 2, NCH, 1024], BF16)
    gact = sb("gact", [128, 4, 1024], BF16)
    usb = sb("usb", [128, 4, 1024], BF16)
    gsg = sb("gsg", [128, NCH, C], BF16)
    ybse = sb("ybse", [128, 16, 512], BF16)
    ytk2 = sb("ytk2", [128, NCH, C], BF16)

    psA = es.enter_context(nc.psum_tensor("psA", [128, 4, 512], F32))
    psT = es.enter_context(nc.psum_tensor("psT", [128, 4, 4, 128], F32))

    s_in = Ctr(es.enter_context(nc.semaphore("s_in")))
    s_pe = Ctr(es.enter_context(nc.semaphore("s_pe")))
    s_ve = Ctr(es.enter_context(nc.semaphore("s_ve")))
    s_ac = Ctr(es.enter_context(nc.semaphore("s_ac")))
    s_gq = Ctr(es.enter_context(nc.semaphore("s_gq")))
    s_w13 = [Ctr(es.enter_context(nc.semaphore("s_w13a"))), Ctr(es.enter_context(nc.semaphore("s_w13b")))]
    s_w2 = [Ctr(es.enter_context(nc.semaphore("s_w2a"))), Ctr(es.enter_context(nc.semaphore("s_w2b")))]
    s_st = Ctr(es.enter_context(nc.semaphore("s_st")))

    g = nc.gpsimd
    t = Rec(nc.tensor)
    v = Chain(nc.vector, s_ve)
    sc = Chain(nc.scalar, s_ac)
    sy = nc.sync

    s_in.inc(g.dma_start(out=invsb[:, :, :, :], in_=invp_i), 16)
    s_in.inc(g.dma_start(out=eyeb[:, :], in_=eye_i), 16)
    v.wait_ge(s_in.h, s_in.v)
    v.tensor_copy(invs[:, :, :], invsb[:, 0, :, :])
    for c2 in range(1, 8):
        v.tensor_add(invs[:, :, :], invs[:, :, :], invsb[:, c2, :, :])
    v.tensor_copy(toki[:, :], invs[:, 0, :])
    v.tensor_copy(wcol[:, :], invs[:, 1, :])
    ve_idx = s_ve.v
    g.wait_ge(s_ve.h, ve_idx)

    def load_w13(le, pe_mark):
        sl = le % 2
        if pe_mark:
            g.wait_ge(s_pe.h, pe_mark)
        s_w13[sl].inc(g.dma_start(out=w13b[:, sl, :, :],
                                  in_=w13_i[le].rearrange("(a p) o -> p a o", p=128)), 16)
        return (s_w13[sl].h, s_w13[sl].v)

    def load_w2(le, pe_mark):
        sl = le % 2
        if pe_mark:
            g.wait_ge(s_pe.h, pe_mark)
        s_w2[sl].inc(g.dma_start(out=w2b[:, sl, :, :],
                                 in_=w2_i[le].rearrange("(a p) o -> p a o", p=128)), 16)
        return (s_w2[sl].h, s_w2[sl].v)

    w13m = {0: load_w13(0, 0)}
    w2m = {0: load_w2(0, 0)}
    gu_mark = [0] * 8
    yb_mark = [0] * 8
    t1_mark = [0] * 8
    t1ve = [0] * 8
    hact_ve = [0] * 8
    guac = []
    ybve = []
    t1ev = {}
    t2ev = {}
    gq_mark = [0] * 8
    st_mark = 0
    ac_store_gate = 0

    for le in range(8):
        sl = le % 2
        if le >= 1:
            g.wait_ge(s_pe.h, t1_mark[le - 1])
        for i in range(8):
            s_gq.inc(g.indirect_dma_start(
                gsg[:, i, :], None,
                xall_i[:, :], bass.IndirectOffsetOnAxis(toki[:, le * 8 + i:le * 8 + i + 1], 0)), 16)
        gq_mark[le] = s_gq.v
        t.wait_ge(s_gq.h, gq_mark[le])
        for i in range(8):
            bp = (i % 2) * 2
            if (le, i - 2) in t1ev:
                t.wait_ge(s_ve.h, t1ev[(le, i - 2)])
            elif le >= 1 and i < 2:
                t.wait_ge(*t2ev[(le - 1, 14 + i)])
            for k in range(NCH):
                t.matmul(psT[:, bp + k // 4, k % 4, :], gsg[:, i, k * 128:(k + 1) * 128],
                         eyeb[:, :], start=True, stop=True)
            s_pe.inc(t.last)
            v.wait_ge(s_pe.h, s_pe.v)
            if i == 0 and le >= 2:
                v.wait_ge(s_pe.h, gu_mark[le - 2])
            v.tensor_copy(xbt[:, sl, :, i * 128:(i + 1) * 128], psT[:, bp:bp + 2, :, :])
            t1ev[(le, i)] = s_ve.v
        t1_mark[le] = s_pe.v
        t1ve[le] = s_ve.v
        t.wait_ge(s_ve.h, t1ve[le])
        t.wait_ge(*w13m[le])
        for n in range(16):
            mi, ti = n // 2, n % 2
            if len(guac) >= 2:
                t.wait_ge(s_ac.h, guac[-2])
            pa = psA[:, n % 2, :]
            for k in range(NCH):
                t.matmul(pa, w13b[:, sl, k, mi * 128:(mi + 1) * 128],
                         xbt[:, sl, k, ti * 512:(ti + 1) * 512],
                         start=(k == 0), stop=(k == NCH - 1))
            s_pe.inc(t.last)
            sc.wait_ge(s_pe.h, s_pe.v)
            if mi < 4:
                sc.activation(out=gact[:, mi, ti * 512:(ti + 1) * 512], in_=pa, func=AF.Silu)
            else:
                sc.activation(out=usb[:, mi - 4, ti * 512:(ti + 1) * 512], in_=pa, func=AF.Copy)
            guac.append(s_ac.v)
            if n >= 9 and n % 2 == 1:
                i4 = (n - 9) // 2
                v.wait_ge(s_ac.h, s_ac.v)
                v.tensor_mul(gact[:, i4, :], gact[:, i4, :], usb[:, i4, :])
        gu_mark[le] = s_pe.v
        if le < 7:
            w13m[le + 1] = load_w13(le + 1, gu_mark[le - 1] if le >= 1 else 0)
        hact_ve[le] = s_ve.v
        t.wait_ge(s_ve.h, hact_ve[le])
        t.wait_ge(*w2m[le])
        for n in range(16):
            mi, ti = n // 2, n % 2
            par = n % 2
            if len(ybve) >= 2:
                t.wait_ge(s_ve.h, ybve[-2])
            pa = psA[:, 2 + par, :]
            for k in range(4):
                t.matmul(pa, w2b[:, sl, k, mi * 128:(mi + 1) * 128],
                         gact[:, k, ti * 512:(ti + 1) * 512],
                         start=(k == 0), stop=(k == 3))
            s_pe.inc(t.last)
            v.wait_ge(s_pe.h, s_pe.v)
            v.tensor_copy(ybse[:, n, :], pa)
            ybve.append(s_ve.v)
        t.wait_ge(s_ve.h, ybve[-1])
        for n in range(16):
            mi, ti = n // 2, n % 2
            bp = (n % 2) * 2
            if n >= 2:
                t.wait_ge(*t2ev[(le, n - 2)])
            else:
                t.wait_ge(s_ve.h, t1ev[(le, 6 + n)])
            for j in range(4):
                t.matmul(psT[:, bp + j // 2, j % 2, :], ybse[:, n, j * 128:(j + 1) * 128],
                         eyeb[:, :], start=True, stop=True)
            s_pe.inc(t.last)
            eng = sc if n % 2 == 0 else v
            eng.wait_ge(s_pe.h, s_pe.v)
            if le >= 1 and n < 2:
                eng.wait_ge(s_st.h, ac_store_gate)
            if n % 2 == 0:
                sc.activation(out=ytk2[:, ti * 4:(ti + 1) * 4, mi * 128:(mi + 1) * 128],
                              in_=psT[:, bp:bp + 2, 0:2, :], func=AF.Copy)
                t2ev[(le, n)] = (s_ac.h, s_ac.v)
            else:
                v.tensor_copy(ytk2[:, ti * 4:(ti + 1) * 4, mi * 128:(mi + 1) * 128],
                              psT[:, bp:bp + 2, 0:2, :])
                t2ev[(le, n)] = (s_ve.h, s_ve.v)
        yb_mark[le] = s_pe.v
        if le < 7:
            w2m[le + 1] = load_w2(le + 1, yb_mark[le - 1] if le >= 1 else 0)
        v.wait_ge(s_ac.h, s_ac.v)
        if le >= 1:
            v.wait_ge(s_st.h, st_mark)
        for st in range(NCH):
            v.tensor_scalar(ytk2[:, st, :], ytk2[:, st, :],
                            wcol[:, le * 8 + st:le * 8 + st + 1], None, ALU.mult)
        ve_scale = s_ve.v
        sy.wait_ge(s_ve.h, ve_scale)
        s_st.inc(sy.dma_start(
            out=ap(ytab_o.tensor, le * 1024 * C, [[C, 128], [128 * C, NCH], [1, C]]),
            in_=ytk2[:, :, :]), 16)
        st_mark = s_st.v
        ac_store_gate = s_st.v
    sy.wait_ge(s_st.h, s_st.v)
    es.close()
    return nc


def build_launch4():
    nc = bass.Bass("TRN2", target_bir_lowering=False)
    es = ExitStack()

    def sb(name, shape, dt=F32, stack=None, side=None):
        return (stack or es).enter_context(nc.sbuf_tensor(name, shape, dt, side=side))

    x1_i = nc.dram_tensor("x1T", [C, TC], F32, kind="ExternalInput").ap()
    qn_i = nc.dram_tensor("qnT", [C, TC], BF16, kind="ExternalInput").ap()
    xn2_i = nc.dram_tensor("xn2f", [C, TC], BF16, kind="ExternalInput").ap()
    xr2_i = nc.dram_tensor("xr2f", [C, TC], BF16, kind="ExternalInput").ap()
    sp4_i = nc.dram_tensor("sp4", [1, 4, C], F32, kind="ExternalInput").ap()
    cp4_i = nc.dram_tensor("cp4", [64, 4, C], F32, kind="ExternalInput").ap()
    meta_i = nc.dram_tensor("meta", [128, 16, 8], F32, kind="ExternalInput").ap()
    dtok_i = nc.dram_tensor("dtok", [128, 16, 2], F32, kind="ExternalInput").ap()
    ytab_i = nc.dram_tensor("ytab", [65537, C], BF16, kind="ExternalInput").ap()
    wo1_i = nc.dram_tensor("Wo1", [C, C], F32, kind="ExternalInput").ap()
    bo1_i = nc.dram_tensor("bo1pc", [128, 8], F32, kind="ExternalInput").ap()
    swg_i = nc.dram_tensor("swg", [C, 512], F32, kind="ExternalInput").ap()
    swu_i = nc.dram_tensor("swu", [C, 512], F32, kind="ExternalInput").ap()
    swd_i = nc.dram_tensor("swd", [512, C], F32, kind="ExternalInput").ap()

    out_o = nc.dram_tensor("outT", [C, TC], F32, kind="ExternalOutput").ap()
    ctd = nc.dram_tensor("ctd", [64, 1024], F32, kind="Internal").ap()
    srsd = nc.dram_tensor("srsd", [1, 1024], F32, kind="Internal").ap()
    maccd = nc.dram_tensor("maccd", [TC, C], BF16, kind="Internal").ap()

    xacc = sb("xacc", [128, NCH, TC], F32)
    cbb = sb("cbb", [128, 8, 64], BF16)
    srs = sb("srs", [128, 8])
    bo1v = sb("bo1v", [128, 8])
    metw = sb("metw", [128, 16, 8])
    dtk = sb("dtk", [128, 16, 2])
    dti = sb("dti", [128, 16, 2], I32)

    psB = es.enter_context(nc.psum_tensor("psB", [128, 4, 512], F32))

    s_in = Ctr(es.enter_context(nc.semaphore("s_in")))
    s_x = Ctr(es.enter_context(nc.semaphore("s_x")))
    s_pe = Ctr(es.enter_context(nc.semaphore("s_pe")))
    s_ve = Ctr(es.enter_context(nc.semaphore("s_ve")))
    s_ac = Ctr(es.enter_context(nc.semaphore("s_ac")))
    s_rt = Ctr(es.enter_context(nc.semaphore("s_rt")))
    s_sr = Ctr(es.enter_context(nc.semaphore("s_sr")))
    s_cb = Ctr(es.enter_context(nc.semaphore("s_cb")))
    s_w = Ctr(es.enter_context(nc.semaphore("s_w")))
    s_ga = Ctr(es.enter_context(nc.semaphore("s_ga")))
    s_tr = Ctr(es.enter_context(nc.semaphore("s_tr")))
    s_mp = [Ctr(es.enter_context(nc.semaphore(f"s_mp{i}"))) for i in range(4)]
    s_gap = [Ctr(es.enter_context(nc.semaphore(f"s_gap{i}"))) for i in range(4)]
    s_trp = [Ctr(es.enter_context(nc.semaphore("s_trp0"))), Ctr(es.enter_context(nc.semaphore("s_trp1")))]
    s_st = Ctr(es.enter_context(nc.semaphore("s_st")))

    g = nc.gpsimd
    t = Rec(nc.tensor)
    v = Chain(nc.vector, s_ve)
    sc = Chain(nc.scalar, s_ac)
    sy = nc.sync

    b01 = []
    b23 = []

    s_in.inc(sy.dma_start(out=metw[:, :, :], in_=meta_i), 16)
    s_in.inc(sy.dma_start(out=dtk[:, :, :], in_=dtok_i), 16)
    s_in.inc(sy.dma_start(out=bo1v[:, :], in_=bo1_i), 16)
    s_x.inc(sy.dma_start(out=xacc[:, :, :], in_=x1_i.rearrange("(a p) t -> p a t", p=128)), 16)

    gth = sb("gth", [128, 4, 2, C], BF16, side="right")
    mst = sb("mst", [128, 4, C], BF16, side="right")
    msb = sb("msb", [128, 2, TC], BF16, side="right")

    st1 = ExitStack()
    xn2sb = sb("xn2sb", [128, NCH, TC], BF16, stack=st1, side="right")
    s4r = sb("s4r", [1, 4, C], F32, stack=st1, side="right")
    c4 = sb("c4", [64, 4, C], F32, stack=st1, side="right")
    csum = sb("csum", [64, C], F32, stack=st1, side="right")
    srow = sb("srow", [1, 2, C], F32, stack=st1, side="right")
    s_x.inc(sy.dma_start(out=xn2sb[:, :, :], in_=xn2_i.rearrange("(a p) t -> p a t", p=128)), 16)
    s_in.inc(sy.dma_start(out=s4r[:, :, :], in_=sp4_i), 16)
    s_in.inc(sy.dma_start(out=c4[:, :, :], in_=cp4_i), 16)
    in_done = s_in.v
    x_done = s_x.v

    v.wait_ge(s_in.h, in_done)
    v.tensor_copy(dti[:, :, :], dtk[:, :, :])
    ve_dti = s_ve.v
    v.tensor_add(csum[:, :], c4[:, 0, :], c4[:, 1, :])
    v.tensor_add(csum[:, :], csum[:, :], c4[:, 2, :])
    v.tensor_add(csum[:, :], csum[:, :], c4[:, 3, :])
    v.tensor_add(srow[0:1, 0, :], s4r[0:1, 0, :], s4r[0:1, 1, :])
    v.tensor_add(srow[0:1, 0, :], srow[0:1, 0, :], s4r[0:1, 2, :])
    v.tensor_add(srow[0:1, 0, :], srow[0:1, 0, :], s4r[0:1, 3, :])
    v.reciprocal(srow[0:1, 1, :], srow[0:1, 0, :])
    ve_sc = s_ve.v
    sy.wait_ge(s_ve.h, ve_sc)
    s_rt.inc(sy.dma_start(out=ctd, in_=csum[:, :]), 16)
    s_rt.inc(sy.dma_start(out=srsd, in_=srow[0:1, 1, :]), 16)
    rt_done = s_rt.v
    sy.wait_ge(s_rt.h, rt_done)
    s_sr.inc(sy.dma_start(out=srs[:, :], in_=srsd.rearrange("o (a p) -> p (o a)", p=128),
                          allow_slow_non_contiguous=True), 16)
    g.wait_ge(s_rt.h, rt_done)
    s_cb.inc(g.dma_start(out=cbb[0:64, :, :], in_=ap(ctd.tensor, 0, [[1024, 64], [128, 8], [1, 64]])), 16)
    s_cb.inc(g.dma_start(out=cbb[64:128, :, :], in_=ap(ctd.tensor, 64, [[1024, 64], [128, 8], [1, 64]])), 16)
    cb_done = s_cb.v

    gth_ve = [0, 0, 0, 0]

    def moe_gather(tj):
        slot = tj % 4
        if gth_ve[slot]:
            g.wait_ge(s_ve.h, gth_ve[slot])
        for sl in range(2):
            s_gap[slot].inc(g.indirect_dma_start(
                gth[:, slot, sl, :], None,
                ytab_i[:, :], bass.IndirectOffsetOnAxis(dti[:, tj, sl:sl + 1], 0)), 16)

    g.wait_ge(s_ve.h, ve_dti)
    for tj in range(4):
        moe_gather(tj)

    v.wait_ge(s_x.h, x_done)
    for a in range(NCH):
        v.tensor_add(xacc[:, a, :], xacc[:, a, :], xn2sb[:, a, :])
    ve_ph1 = s_ve.v
    st1.close()

    st2 = ExitStack()
    qnsb = sb("qnsb", [128, NCH, TC], BF16, stack=st2, side="right")
    asb = sb("asb", [128, NCH, TC], BF16, stack=st2, side="right")
    wo1b = sb("wo1b", [128, NCH, C], BF16, stack=st2, side="right")
    sy.wait_ge(s_ve.h, ve_ph1)
    s_x.inc(sy.dma_start(out=qnsb[:, :, :], in_=qn_i.rearrange("(a p) t -> p a t", p=128)), 16)
    qn_done = s_x.v
    s_w.inc(g.dma_start(out=wo1b[:, :, :], in_=wo1_i.rearrange("(a p) o -> p a o", p=128)), 16)
    wo_done = s_w.v

    v.wait_ge(s_sr.h, s_sr.v)
    v.tensor_scalar(srs[:, :], srs[:, :], 0.125, None, ALU.mult)
    v.wait_ge(s_x.h, qn_done)
    for a in range(NCH):
        v.tensor_scalar(qnsb[:, a, :], qnsb[:, a, :], srs[:, a:a + 1], None, ALU.mult)
    ve_qn = s_ve.v

    t.wait_ge(s_cb.h, cb_done)
    t.wait_ge(s_ve.h, ve_qn)
    n = 0
    for a in range(NCH):
        for ti in range(NT):
            bk = n % 2
            if len(b01) >= 2:
                t.wait_ge(*b01[-2])
            pa = psB[:, bk, :]
            t.matmul(pa[0:64, :], cbb[0:64, a, :], qnsb[0:64, a, ti * 512:(ti + 1) * 512],
                     start=True, stop=True)
            t.matmul(pa[64:128, :], cbb[64:128, a, :], qnsb[64:128, a, ti * 512:(ti + 1) * 512],
                     start=True, stop=True)
            s_pe.inc(t.last)
            v.wait_ge(s_pe.h, s_pe.v)
            v.tensor_copy(asb[:, a, ti * 512:(ti + 1) * 512], pa)
            b01.append((s_ve.h, s_ve.v))
            n += 1
    t.wait_ge(*b01[-1])
    t.wait_ge(s_w.h, wo_done)
    n = 0
    for mi in range(NCH):
        for ti in range(NT):
            bk = 2 + n % 2
            if len(b23) >= 2:
                t.wait_ge(*b23[-2])
            pa = psB[:, bk, :]
            for k in range(NCH):
                t.matmul(pa, wo1b[:, k, mi * 128:(mi + 1) * 128],
                         asb[:, k, ti * 512:(ti + 1) * 512],
                         start=(k == 0), stop=(k == NCH - 1))
            s_pe.inc(t.last)
            v.wait_ge(s_pe.h, s_pe.v)
            v.scalar_tensor_tensor(xacc[:, mi, ti * 512:(ti + 1) * 512], pa, bo1v[:, mi:mi + 1],
                                   xacc[:, mi, ti * 512:(ti + 1) * 512], ALU.add, ALU.add)
            b23.append((s_ve.h, s_ve.v))
            n += 1
    ve_attn = s_ve.v
    st2.close()

    st3 = ExitStack()
    xr2sb = sb("xr2sb", [128, NCH, TC], BF16, stack=st3, side="right")
    gs = sb("gs", [128, 4, TC], BF16, stack=st3, side="right")
    us = sb("us", [128, 4, TC], BF16, stack=st3, side="right")
    swgb = sb("swgb", [128, NCH, 512], BF16, stack=st3, side="right")
    swub = sb("swub", [128, NCH, 512], BF16, stack=st3, side="right")
    swdb = sb("swdb", [128, 4, C], BF16, stack=st3, side="right")
    sy.wait_ge(s_ve.h, ve_attn)
    s_x.inc(sy.dma_start(out=xr2sb[:, :, :], in_=xr2_i.rearrange("(a p) t -> p a t", p=128)), 16)
    xr_done = s_x.v
    g.wait_ge(s_ve.h, ve_attn)
    s_w.inc(g.dma_start(out=swgb[:, :, :], in_=swg_i.rearrange("(a p) o -> p a o", p=128)), 16)
    s_w.inc(g.dma_start(out=swub[:, :, :], in_=swu_i.rearrange("(a p) o -> p a o", p=128)), 16)
    s_w.inc(g.dma_start(out=swdb[:, :, :], in_=swd_i.rearrange("(k p) o -> p k o", p=128)), 16)
    sw_done = s_w.v

    for tj in range(NCHUNK):
        slot = tj % 4
        v.wait_ge(s_gap[slot].h, s_gap[slot].v)
        if s_mp[slot].v:
            v.wait_ge(s_mp[slot].h, s_mp[slot].v)
        v.tensor_add(mst[:, slot, :], gth[:, slot, 0, :], gth[:, slot, 1, :])
        gth_ve[slot] = s_ve.v
        if tj + 4 < NCHUNK:
            moe_gather(tj + 4)
        sy.wait_ge(s_ve.h, s_ve.v)
        s_mp[slot].inc(sy.dma_start(out=maccd[tj * 128:(tj + 1) * 128, :], in_=mst[:, slot, :]), 16)


    t.wait_ge(s_x.h, xr_done)
    t.wait_ge(s_w.h, sw_done)
    for phase, dst in ((0, gs), (1, us)):
        wsb = swgb if phase == 0 else swub
        n = 0
        for mi in range(4):
            for ti in range(NT):
                bk = n % 2
                if len(b01) >= 2:
                    t.wait_ge(*b01[-2])
                pa = psB[:, bk, :]
                for k in range(NCH):
                    t.matmul(pa, wsb[:, k, mi * 128:(mi + 1) * 128],
                             xr2sb[:, k, ti * 512:(ti + 1) * 512],
                             start=(k == 0), stop=(k == NCH - 1))
                s_pe.inc(t.last)
                sc.wait_ge(s_pe.h, s_pe.v)
                sc.activation(out=dst[:, mi, ti * 512:(ti + 1) * 512], in_=pa,
                              func=AF.Silu if phase == 0 else AF.Copy)
                b01.append((s_ac.h, s_ac.v))
                n += 1
    v.wait_ge(s_ac.h, s_ac.v)
    for i4 in range(4):
        v.tensor_mul(gs[:, i4, :], gs[:, i4, :], us[:, i4, :])
    ve_hs = s_ve.v
    t.wait_ge(s_ve.h, ve_hs)
    n = 0
    for mi in range(NCH):
        for ti in range(NT):
            bk = 2 + n % 2
            if len(b23) >= 2:
                t.wait_ge(*b23[-2])
            pa = psB[:, bk, :]
            for k in range(4):
                t.matmul(pa, swdb[:, k, mi * 128:(mi + 1) * 128],
                         gs[:, k, ti * 512:(ti + 1) * 512],
                         start=(k == 0), stop=(k == 3))
            s_pe.inc(t.last)
            v.wait_ge(s_pe.h, s_pe.v)
            v.tensor_add(xacc[:, mi, ti * 512:(ti + 1) * 512], pa,
                         xacc[:, mi, ti * 512:(ti + 1) * 512])
            b23.append((s_ve.h, s_ve.v))
            n += 1
    ve_sh = s_ve.v
    st3.close()

    for i in range(4):
        sy.wait_ge(s_mp[i].h, s_mp[i].v)
    msb_ve = [0, 0]
    for k in range(NCH):
        par = k % 2
        if msb_ve[par]:
            sy.wait_ge(s_ve.h, msb_ve[par])
        s_trp[par].inc(sy.dma_start_transpose(msb[:, par, :],
                                              ap(maccd.tensor, k * 128, [[C, TC], [1, 128]])), 16)
        v.wait_ge(s_trp[par].h, s_trp[par].v)
        v.tensor_add(xacc[:, k, :], msb[:, par, :], xacc[:, k, :])
        msb_ve[par] = s_ve.v

    sy.wait_ge(s_ve.h, s_ve.v)
    s_st.inc(sy.dma_start(out=out_o.rearrange("(a p) t -> p a t", p=128), in_=xacc[:, :, :]), 16)
    sy.wait_ge(s_st.h, s_st.v)
    es.close()
    return nc
